# version 06
# baseline (speedup 1.0000x reference)
import numpy as np
import concourse.bass as bass
import concourse.mybir as mybir

F32 = mybir.dt.float32
BF16 = mybir.dt.bfloat16
I32 = mybir.dt.int32
AF = mybir.ActivationFunctionType
ALU = mybir.AluOpType
AX = mybir.AxisListType

NT = 32
SUB7 = {"pt", "gw", "rt", "evac", "wp", "gevac", "rtc"}
SUB9 = {"u", "o", "d", "supd", "y", "ln", "st", "skel", "ue"}
DT = 1024
TOK = NT * 128


class Sem:
    def __init__(self, h):
        self.h = h
        self.n = 0

    def inc(self, instr, k=1):
        instr.then_inc(self.h, k)
        self.n += k
        return self.n


CONST_VALUES = [1e-5, 1e-30, 1.0 / 7.5, 0.0, 1.0, -1.0, 0.5, 1e-6, 1e-8]


class Common:
    pass


def seq_dma(eng, sem, out, in_):
    eng.wait_ge(sem.h, sem.n)
    instr = eng.dma_start(out, in_)
    instr.then_inc(sem.h, 16)
    sem.n += 16
    return instr



class Arena:
    def __init__(self, nc, base, limit):
        self.nc = nc
        self.base = base
        self.off = base
        self.limit = limit

    def reset(self):
        self.off = self.base

    def __call__(self, name, shape, dtype):
        isz = {F32: 4, BF16: 2, I32: 4}[dtype]
        off = (self.off + 31) & ~31
        t = self.nc.alloc_sbuf_tensor_at(name, shape, dtype, offset=off)
        self.off = off + shape[-1] * isz
        assert self.off <= self.limit, (name, self.off, self.limit)
        return t


def setup_common(nc):
    import ml_dtypes
    c = Common()
    c.id128_bf = nc.alloc_sbuf_tensor("c_id128b", [128, 128], BF16)
    c.id128_f = nc.alloc_sbuf_tensor("c_id128f", [128, 128], F32)
    c.ones_f = nc.alloc_sbuf_tensor("c_onesf", [1, 128], F32)
    c.ones_cb = nc.alloc_sbuf_tensor("c_onescb", [128, 1], BF16)
    c.consts = nc.alloc_sbuf_tensor("c_consts", [128, len(CONST_VALUES)], F32)
    idn = np.eye(128, dtype=np.float32)
    c.d_idb = nc.inline_tensor(idn.astype(ml_dtypes.bfloat16), "d_idb")
    c.d_idf = nc.inline_tensor(idn, "d_idf")
    c.d_onesf = nc.inline_tensor(np.ones((1, 128), np.float32), "d_onesf")
    c.d_onescb = nc.inline_tensor(np.ones((128, 1), np.float32).astype(ml_dtypes.bfloat16), "d_onescb")
    c.d_consts = nc.inline_tensor(np.array([CONST_VALUES], np.float32), "d_consts")
    for i, v in enumerate(CONST_VALUES):
        nc.const_aps.register(F32, v, c.consts[:, i:i + 1])
    c.PS = [nc.alloc_psum_tensor(f"cps{j}", [128, 512], F32) for j in range(8)]
    return c


def dram_ap(handle, shape):
    r, co = shape
    return bass.AP(handle, 0, [[co, r], [1, 1], [1, co]])


def dram_row_bcast(handle, cols, parts=128):
    return bass.AP(handle, 0, [[0, parts], [1, 1], [1, cols]])


def emit_common_load(c, sy, ld):
    seq_dma(sy, ld, c.id128_bf[:, :], dram_ap(c.d_idb, (128, 128)))
    seq_dma(sy, ld, c.id128_f[:, :], dram_ap(c.d_idf, (128, 128)))
    seq_dma(sy, ld, c.ones_f[:, :], dram_ap(c.d_onesf, (1, 128)))
    seq_dma(sy, ld, c.ones_cb[:, :], dram_ap(c.d_onescb, (128, 1)))
    seq_dma(sy, ld, c.consts[:, :], dram_row_bcast(c.d_consts, len(CONST_VALUES)))


def emit_ln_tile(nc, block, c, v_tile, scr, sv, sa, pre_v,
                 gamma_b=None, beta_b=None, out_tile=None, eps=1e-5):
    out_tile = v_tile if out_tile is None else out_tile
    sqs, ssq, s1 = scr["sqs"], scr["ssq"], scr["s1"]
    mean, var, rstd, nb = scr["mean"], scr["var"], scr["rstd"], scr["nb"]

    @block.scalar
    def _(sc):
        sc.wait_ge(sv.h, pre_v)
        sa.inc(sc.activation(sqs, v_tile, AF.Square, accum_out=ssq))
    ssq_done = sa.n

    @block.vector
    def _(v):
        v.wait_ge(sv.h, pre_v)
        sv.inc(v.reduce_sum(s1, v_tile, axis=AX.X))
        v.wait_ge(sv.h, sv.n)
        sv.inc(v.tensor_scalar_mul(mean, s1, 1.0 / DT))
        v.wait_ge(sv.h, sv.n)
        sv.inc(v.tensor_tensor(var, mean, mean, ALU.mult))
        v.wait_ge(sa.h, ssq_done)
        v.wait_ge(sv.h, sv.n)
        sv.inc(v.scalar_tensor_tensor(var, ssq, 1.0 / DT, var, ALU.mult, ALU.subtract))
    var_done = sv.n

    @block.scalar
    def _(sc):
        sc.wait_ge(sv.h, var_done)
        sa.inc(sc.activation(var, var, AF.Sqrt, bias=eps))
    std_done = sa.n

    @block.vector
    def _(v):
        v.wait_ge(sa.h, std_done)
        sv.inc(v.reciprocal(rstd, var))
        v.wait_ge(sv.h, sv.n)
        sv.inc(v.tensor_tensor(nb, mean, rstd, ALU.mult))
        v.wait_ge(sv.h, sv.n)
        sv.inc(v.tensor_scalar_mul(nb, nb, -1.0))
        v.wait_ge(sv.h, sv.n)
        sv.inc(v.tensor_scalar(out_tile, v_tile, rstd, nb, ALU.mult, ALU.add))
        if gamma_b is not None:
            v.wait_ge(sv.h, sv.n)
            sv.inc(v.tensor_tensor(out_tile, out_tile, gamma_b, ALU.mult))
            v.wait_ge(sv.h, sv.n)
            sv.inc(v.tensor_tensor(out_tile, out_tile, beta_b, ALU.add))
    return sv.n


def alloc_ln_scratch(A, parts=128):
    return {
        "sqs": A("ln_sqs", [parts, DT], F32)[:, :],
        "ssq": A("ln_ssq", [parts, 1], F32)[:, :],
        "s1": A("ln_s1", [parts, 1], F32)[:, :],
        "mean": A("ln_mean", [parts, 1], F32)[:, :],
        "var": A("ln_var", [parts, 1], F32)[:, :],
        "rstd": A("ln_rstd", [parts, 1], F32)[:, :],
        "nb": A("ln_nb", [parts, 1], F32)[:, :],
    }



def emit_phaseD(nc, block, c, io, sems, pref="pd_", arena=None):
    so2 = [sems["s0"], sems["s1"]]
    """io: x_in [4096,1024] f32 dram AP, w_in [1024,1024], b_in [1,1024],
    y_out [4096,1024]. sems: dict ld, v, a, t, st (fresh counters ok)."""
    A = (lambda n, sh, dt: arena(pref + n, sh, dt)) if arena is not None \
        else (lambda n, sh, dt: nc.alloc_sbuf_tensor(pref + n, sh, dt))

    xt = [A(f"xt{j}", [128, DT], F32) for j in range(2)]
    wtf = A("wtf", [128, DT], F32)
    wd = A("wd", [128, DT], BF16)
    wdT = A("wdT", [128, 8 * DT], BF16)
    wdT3 = wdT.rearrange("p (k o) -> p k o", k=8)
    bias_b = A("biasb", [128, DT], F32)
    bias_r = A("biasr", [1, DT], F32)
    qs = []
    for j in range(2):
        qs.append({nm: A(f"{nm}{j}", [128, 32] if nm in ("am", "scb", "iscb")
                         else [128, DT], I32 if nm == "ti" else F32)
                   for nm in ("am", "scb", "iscb", "uu", "au", "ge2", "ge4",
                              "invs", "ss", "tt", "ti", "tf", "sbl")})
    xd = [A(f"xd{j}", [128, DT], BF16) for j in range(2)]
    xdf = [A(f"xdf{j}", [128, DT], BF16) for j in range(2)]
    ot = [A(f"ot{j}", [128, DT], F32) for j in range(2)]
    lns = alloc_ln_scratch(A)

    ps_tr = [c.PS[0].bitcast(BF16), c.PS[1][:, :].bitcast(BF16)]
    ps_y = [c.PS[2 + j][:, :] for j in range(4)]

    ld, sv, sa, st_, so = sems["ld"], sems["v"], sems["a"], sems["t"], sems["st"]

    @block.sync
    def _(sy):
        emit_common_load(c, sy, ld)
        seq_dma(sy, ld, bias_r[:, :], io["b_in"])
    ld_common = ld.n

    @block.tensor
    def _(t):
        t.wait_ge(ld.h, ld_common)
        t.matmul(ps_y[0][:, :], c.ones_f[0:1, 0:128], bias_r[0:1, 0:512],
                 start=True, stop=True)
        st_.inc(t.matmul(ps_y[1][:, :], c.ones_f[0:1, 0:128], bias_r[0:1, 512:1024],
                         start=True, stop=True))
    bias_mm = st_.n

    @block.vector
    def _(v):
        v.wait_ge(st_.h, bias_mm)
        sv.inc(v.tensor_copy(bias_b[:, 0:512], ps_y[0][:, :]))
        sv.inc(v.tensor_copy(bias_b[:, 512:1024], ps_y[1][:, :]))
    bias_done = sv.n

    def quant_chain(q, src, dst, pre, dst_guard=None):
        am, scb, iscb = q["am"], q["scb"], q["iscb"]
        uu, au, ge2, ge4 = q["uu"], q["au"], q["ge2"], q["ge4"]
        invs, ss, tt_, ti, tf, sbl = (q["invs"], q["ss"], q["tt"], q["ti"],
                                      q["tf"], q["sbl"])

        @block.vector
        def _(v):
            for (sh, n_) in pre:
                v.wait_ge(sh.h, n_)
            sv.inc(v.reduce_max(am[:, :], src.rearrange("p (g x) -> p g x", g=32),
                                axis=AX.X, apply_absolute_value=True))
            v.wait_ge(sv.h, sv.n)
            sv.inc(v.tensor_scalar(scb[:, :], am[:, :], 1.0 / 7.5, 1e-30,
                                   ALU.mult, ALU.max))
            v.wait_ge(sv.h, sv.n)
            sv.inc(v.reciprocal(iscb[:, :], scb[:, :]))
            v.wait_ge(sv.h, sv.n)
            sv.inc(v.tensor_tensor(
                uu.rearrange("p (g x) -> p g x", g=32), src.rearrange("p (g x) -> p g x", g=32),
                iscb.rearrange("p (g o) -> p g o", o=1).to_broadcast([128, 32, 32]),
                ALU.mult))
        src_free = sv.n
        uu_done = sv.n

        @block.scalar
        def _(sc):
            sc.wait_ge(sv.h, uu_done)
            sa.inc(sc.activation(au[:, :], uu[:, :], AF.Abs))
        au_done = sa.n

        @block.vector
        def _(v):
            v.wait_ge(sa.h, au_done)
            sv.inc(v.tensor_single_scalar(ge2[:, :], au[:, :], 2.0, ALU.is_ge))
            sv.inc(v.tensor_single_scalar(ge4[:, :], au[:, :], 4.0, ALU.is_ge))
            v.wait_ge(sv.h, sv.n)
            sv.inc(v.tensor_scalar(invs[:, :], ge2[:, :], -4.0, 8.0, ALU.mult, ALU.add))
            sv.inc(v.tensor_scalar(ss[:, :], ge2[:, :], 0.125, 0.125, ALU.mult, ALU.add))
            v.wait_ge(sv.h, sv.n)
            sv.inc(v.scalar_tensor_tensor(invs[:, :], ge4[:, :], -2.0, invs[:, :],
                                          ALU.mult, ALU.add))
            sv.inc(v.scalar_tensor_tensor(ss[:, :], ge4[:, :], 0.25, ss[:, :],
                                          ALU.mult, ALU.add))
            v.wait_ge(sv.h, sv.n)
            sv.inc(v.tensor_tensor(tt_[:, :], uu[:, :], invs[:, :], ALU.mult))
        t_done = sv.n

        @block.scalar
        def _(sc):
            sc.wait_ge(sv.h, t_done)
            sa.inc(sc.copy(ti[:, :], tt_[:, :]))
            sc.wait_ge(sa.h, sa.n)
            sa.inc(sc.copy(tf[:, :], ti[:, :]))
        tf_done = sa.n

        @block.vector
        def _(v):
            sv.inc(v.tensor_tensor(
                sbl.rearrange("p (g x) -> p g x", g=32), ss.rearrange("p (g x) -> p g x", g=32),
                scb.rearrange("p (g o) -> p g o", o=1).to_broadcast([128, 32, 32]),
                ALU.mult))
            v.wait_ge(sa.h, tf_done)
            v.wait_ge(sv.h, sv.n)
            sv.inc(v.tensor_tensor(au[:, :], tt_[:, :], tf[:, :], ALU.subtract))
            v.wait_ge(sv.h, sv.n)
            sv.inc(v.scalar_tensor_tensor(tf[:, :], au[:, :], 0.5, tf[:, :],
                                          ALU.is_gt, ALU.add))
            sv.inc(v.tensor_single_scalar(ge2[:, :], au[:, :], -0.5, ALU.is_lt))
            v.wait_ge(sv.h, sv.n)
            sv.inc(v.tensor_tensor(tf[:, :], tf[:, :], ge2[:, :], ALU.subtract))
            v.wait_ge(sv.h, sv.n)
            if dst_guard is not None:
                v.wait_ge(dst_guard[0].h, dst_guard[1])
            sv.inc(v.tensor_tensor(dst, tf[:, :], sbl[:, :], ALU.mult))
        return src_free, sv.n

    w_free = {}
    wdT_evac = {}
    last_tr = [st_.n]
    for r in range(8):
        @block.sync
        def _(sy, r=r):
            if r >= 1:
                sy.wait_ge(sv.h, w_free[r - 1])
            seq_dma(sy, ld, wtf[:, :], io["w_in"][r * 128:(r + 1) * 128, :])
        w_free[r], wq_end = quant_chain(qs[r % 2], wtf[:, :], wd[:, :], [(ld, ld.n)],
                                        dst_guard=(st_, last_tr[0]))

        @block.tensor
        def _(t, r=r, wq_end=wq_end):
            t.wait_ge(sv.h, wq_end)
            if r >= 2:
                t.wait_ge(sv.h, wdT_evac[r - 2])
            for kk in range(8):
                mm = t.transpose(ps_tr[r % 2][:, kk * 128:(kk + 1) * 128],
                                 wd[:, kk * 128:(kk + 1) * 128], c.id128_bf[:, :])
            st_.inc(mm)
        tr_n = st_.n
        last_tr[0] = tr_n

        @block.vector
        def _(v, r=r, tr_n=tr_n):
            v.wait_ge(st_.h, tr_n)
            sv.inc(v.tensor_copy(wdT3[:, :, r * 128:(r + 1) * 128],
                                 ps_tr[r % 2].rearrange("p (k o) -> p k o", k=8)))
        wdT_evac[r] = sv.n
    wdT_done = sv.n

    x_free = {}
    xdf_evac = {}
    y_free = {}
    store_done = {}
    xq_done = {}
    tr_done = {}

    def emit_s1(i):
        @block.sync
        def _(sy, i=i):
            if i >= 2:
                sy.wait_ge(sv.h, x_free[i - 2])
                sy.wait_ge(so2[i % 2].h, store_done[i - 2])
            seq_dma(sy, ld, xt[i % 2][:, :], io["x_in"][i * 128:(i + 1) * 128, :])
        x_ld = ld.n
        dg = (st_, tr_done[i - 2]) if i >= 2 else (st_, last_tr[0])
        _, xq = quant_chain(qs[i % 2], xt[i % 2][:, :], xd[i % 2][:, :],
                            [(ld, x_ld)], dst_guard=dg)
        xq_done[i] = xq

    def emit_s2(i):
        @block.tensor
        def _(t, i=i):
            t.wait_ge(sv.h, xq_done[i])
            if i < 2:
                t.wait_ge(sv.h, wdT_done)
            else:
                t.wait_ge(sv.h, xdf_evac[i - 2])
            for kk in range(8):
                mm = t.transpose(ps_tr[i % 2][:, kk * 128:(kk + 1) * 128],
                                 xd[i % 2][:, kk * 128:(kk + 1) * 128], c.id128_bf[:, :])
            st_.inc(mm)
        tr_done[i] = st_.n

        @block.vector
        def _(v, i=i):
            v.wait_ge(st_.h, tr_done[i])
            sv.inc(v.tensor_copy(xdf[i % 2][:, :], ps_tr[i % 2][:, :]))
        xdf_evac[i] = sv.n

        @block.tensor
        def _(t, i=i):
            t.wait_ge(sv.h, xdf_evac[i])
            if i == 0:
                t.wait_ge(sv.h, bias_done)
            if i >= 2:
                t.wait_ge(sv.h, y_free[i - 2])
            for half in range(2):
                for kk in range(8):
                    mm = t.matmul(ps_y[2 * (i % 2) + half][:, :],
                                  xdf[i % 2][:, kk * 128:(kk + 1) * 128],
                                  wdT3[:, kk, half * 512:(half + 1) * 512],
                                  start=(kk == 0), stop=(kk == 7))
            st_.inc(mm)
        mm_done = st_.n

        @block.vector
        def _(v, i=i, mm_done=mm_done):
            v.wait_ge(st_.h, mm_done)
            if i >= 2:
                v.wait_ge(so2[i % 2].h, store_done[i - 2])
            sv.inc(v.tensor_tensor(ot[i % 2][:, 0:512], ps_y[2 * (i % 2)][:, :],
                                   bias_b[:, 0:512], ALU.add))
            sv.inc(v.tensor_tensor(ot[i % 2][:, 512:1024], ps_y[2 * (i % 2) + 1][:, :],
                                   bias_b[:, 512:1024], ALU.add))
            v.wait_ge(sv.h, sv.n)
            sv.inc(v.tensor_tensor(ot[i % 2][:, :], ot[i % 2][:, :], xt[i % 2][:, :],
                                   ALU.add))
        y_free[i] = sv.n
        x_free[i] = sv.n
        ot_done = sv.n

        ln_done = emit_ln_tile(nc, block, c, ot[i % 2][:, :], lns, sv, sa, ot_done)

        @block.sync
        def _(sy, i=i, ln_done=ln_done):
            sy.wait_ge(sv.h, ln_done)
            seq_dma(sy, so2[i % 2], io["y_out"][i * 128:(i + 1) * 128, :], ot[i % 2][:, :])
        store_done[i] = so2[i % 2].n

    emit_s1(0)
    for i in range(1, NT):
        emit_s1(i)
        emit_s2(i - 1)
    emit_s2(NT - 1)

    @block.sync
    def _(sy):
        sy.wait_ge(so2[0].h, so2[0].n)
        sy.wait_ge(so2[1].h, so2[1].n)
    return


def make_sems(nc, es, pref, names=("ld", "v", "a", "t", "st", "wld", "wc", "s0", "s1", "s2")):
    return {nm: Sem(es.enter_context(nc.semaphore(pref + nm))) for nm in names}



def emit_phaseB(nc, block, c, io, sems, pref="pb_", n_experts=64, nbuf=2, arena=None):
    A = (lambda n, sh, dt: arena(pref + n, sh, dt)) if arena is not None \
        else (lambda n, sh, dt: nc.alloc_sbuf_tensor(pref + n, sh, dt))

    HT = 16
    acc = [A(f"acc{j}", [128, DT], F32) for j in range(HT)]
    x_fm = A("xfm", [128, 8 * 2048], BF16)
    x_fm3 = x_fm.rearrange("p (k t) -> p k t", k=8)
    xb = A("xb", [128, DT], BF16)
    cw = [A(f"cw{j}", [128, 64], F32) for j in range(HT)]
    centT = A("centT", [128, 8 * 64], F32)
    centT3 = centT.rearrange("p (k e) -> p k e", k=8)
    xf32 = A("xf32", [128, DT], F32)
    cent_s = xf32
    ebb = A("ebb", [128, 64], F32)
    eb_r = A("eb_r", [1, 64], F32)
    aff = A("aff", [128, 64], F32)
    biased = A("biased", [128, 64], F32)
    max8 = A("max8", [128, 8], F32)
    rep = A("rep", [128, 64], F32)
    mask = A("mask", [128, 64], F32)
    sig = A("sig", [128, 64], F32)
    ssel = A("ssel", [128, 64], F32)
    nrm = A("nrm", [128, 1], F32)
    wub = [A(f"wub{j}", [128, 8 * 512], BF16) for j in range(nbuf)]
    wdb = [A(f"wdb{j}", [128, 4 * DT], BF16) for j in range(nbuf)]
    wstg_u = A("wstgu", [128, 8 * 512], F32)
    wstg_d = A("wstgd", [128, 4 * DT], F32)
    h_s = A("hs", [128, 4 * 512], BF16)
    lns = alloc_ln_scratch(A)

    ps_h = [c.PS[j][:, :] for j in range(4)]
    ps_y = [c.PS[4 + j][:, :] for j in range(4)]
    ps_trv = ps_h[0].bitcast(BF16)
    ps_aff = ps_h[1]

    ld, sv, sa, st_, so = sems["ld"], sems["v"], sems["a"], sems["t"], sems["st"]
    wld = sems["wld"]
    wcast = sems["wc"]

    NE = n_experts

    @block.sync
    def _(sy):
        emit_common_load(c, sy, ld)
        seq_dma(sy, ld, eb_r[:, :], io["eb"])
        seq_dma(sy, ld, cent_s[0:64, :], io["cent"])
    setup_ld = ld.n

    @block.tensor
    def _(t):
        t.wait_ge(ld.h, setup_ld)
        for kk in range(8):
            mm = t.transpose(ps_h[2][:, kk * 64:(kk + 1) * 64],
                             cent_s[0:64, kk * 128:(kk + 1) * 128], c.id128_f[0:64, 0:64])
        st_.inc(mm)
        st_.inc(t.matmul(ps_aff[:, 0:64], c.ones_f[0:1, 0:128], eb_r[0:1, 0:64],
                         start=True, stop=True))
    setup_mm = st_.n

    @block.vector
    def _(v):
        v.wait_ge(st_.h, setup_mm)
        sv.inc(v.tensor_copy(centT[:, :], ps_h[2][:, :]))
        v.wait_ge(sv.h, sv.n)
        sv.inc(v.tensor_copy(ebb[:, :], ps_aff[:, 0:64]))
    setup_done = sv.n

    last_up = [st_.n]
    store_done = {}
    dn_tile = {}
    te_done_c = {}
    xf32_evac = {0: 0}

    for half in range(2):
        base_tile = half * HT
        tile_ld = {}
        xfm_done = {}
        cw_done = {}
        for i in range(HT):
            gi = base_tile + i
            @block.sync
            def _(sy, i=i, gi=gi, half=half, so_all=None):
                if half == 1:
                    sy.wait_ge(so.h, half0_stores)
                seq_dma(sy, ld, acc[i][:, :], io["x_in"][gi * 128:(gi + 1) * 128, :])
            tile_ld[i] = ld.n

            @block.scalar
            def _(sc, i=i):
                sc.wait_ge(ld.h, tile_ld[i])
                if i == 0 and half == 0:
                    sc.wait_ge(sv.h, setup_done)
                sa.inc(sc.copy(xb[:, :], acc[i][:, :]))
            xb_done = sa.n

            @block.tensor
            def _(t, i=i, xb_done=xb_done, half=half):
                sc_wait = xb_done
                t.wait_ge(sa.h, sc_wait)
                if i >= 1:
                    t.wait_ge(sv.h, xfm_done[i - 1])
                if half == 1 and i == 0:
                    t.wait_ge(sv.h, half0_acc_done)
                for kk in range(8):
                    mm = t.transpose(ps_trv[:, kk * 128:(kk + 1) * 128],
                                     xb[:, kk * 128:(kk + 1) * 128], c.id128_bf[:, :])
                st_.inc(mm)
            tr_done = st_.n

            @block.vector
            def _(v, i=i, tr_done=tr_done):
                v.wait_ge(st_.h, tr_done)
                sv.inc(v.tensor_copy(x_fm3[:, :, i * 128:(i + 1) * 128],
                                     ps_trv.rearrange("p (k t) -> p k t", k=8)))
            xfm_done[i] = sv.n

            @block.tensor
            def _(t, i=i):
                t.wait_ge(sv.h, xfm_done[i])
                if i >= 1 or half == 1:
                    t.wait_ge(sv.h, xf32_evac[0])
                if i == 0 and half == 0:
                    t.wait_ge(sv.h, setup_done)
                for kk in range(8):
                    pb = ps_h[2] if kk < 4 else ps_h[3]
                    mm = t.transpose(pb[:, (kk % 4) * 128:(kk % 4 + 1) * 128],
                                     acc[i][:, kk * 128:(kk + 1) * 128], c.id128_f[:, :])
                st_.inc(mm)
            xtr_mm = st_.n

            @block.vector
            def _(v, i=i, xtr_mm=xtr_mm):
                v.wait_ge(st_.h, xtr_mm)
                sv.inc(v.tensor_copy(xf32[:, 0:512], ps_h[2][:, :]))
                sv.inc(v.tensor_copy(xf32[:, 512:1024], ps_h[3][:, :]))
                xf32_evac[0] = sv.n
            xf32_done = sv.n

            @block.tensor
            def _(t, i=i, xf32_done=xf32_done):
                t.wait_ge(sv.h, xf32_done)
                if i >= 1:
                    t.wait_ge(sv.h, aff_evac[i - 1])
                for kk in range(8):
                    mm = t.matmul(ps_aff[:, 0:64], xf32[:, kk * 128:(kk + 1) * 128],
                                  centT3[:, kk, :], start=(kk == 0), stop=(kk == 7))
                st_.inc(mm)
            aff_mm = st_.n

            @block.vector
            def _(v, i=i, aff_mm=aff_mm):
                v.wait_ge(st_.h, aff_mm)
                sv.inc(v.tensor_copy(aff[:, :], ps_aff[:, 0:64]))
            if i == 0 and half == 0:
                aff_evac = {}
            aff_evac[i] = sv.n

            @block.scalar
            def _(sc, i=i):
                sc.wait_ge(sv.h, aff_evac[i])
                sa.inc(sc.activation(sig[:, :], aff[:, :], AF.Sigmoid))
            sig_done = sa.n

            @block.vector
            def _(v, i=i, sig_done=sig_done):
                v.wait_ge(sv.h, aff_evac[i])
                sv.inc(v.tensor_tensor(biased[:, :], aff[:, :], ebb[:, :], ALU.add))
                v.wait_ge(sv.h, sv.n)
                sv.inc(v.max(out=max8[:, :], in_=biased[:, :]))
                v.wait_ge(sv.h, sv.n)
                sv.inc(v.match_replace(out=rep[:, :], in_to_replace=max8[:, :],
                                       in_values=biased[:, :], imm_value=-1e30))
                v.wait_ge(sv.h, sv.n)
                sv.inc(v.tensor_tensor(mask[:, :], biased[:, :], rep[:, :], ALU.not_equal))
                v.wait_ge(sa.h, sig_done)
                v.wait_ge(sv.h, sv.n)
                sv.inc(v.tensor_tensor(ssel[:, :], sig[:, :], mask[:, :], ALU.mult))
                v.wait_ge(sv.h, sv.n)
                sv.inc(v.reduce_sum(nrm[:, :], ssel[:, :], axis=AX.X))
                v.wait_ge(sv.h, sv.n)
                sv.inc(v.tensor_scalar_add(nrm[:, :], nrm[:, :], 1e-8))
                v.wait_ge(sv.h, sv.n)
                sv.inc(v.reciprocal(nrm[:, :], nrm[:, :]))
                v.wait_ge(sv.h, sv.n)
                sv.inc(v.tensor_scalar(cw[i][:, :], ssel[:, :], nrm[:, :], None,
                                       ALU.mult))
            cw_done[i] = sv.n
        routing_done = sv.n

        eld_done = {}
        cast_done = {}
        te_done = {}
        acc_done = {}
        silu_done = {}

        for ei in range(NE + 1):
            e = ei - 1
            buf = ei % nbuf
            @block.sync
            def _(sy, e=e, ei=ei):
                if ei >= 1:
                    sy.wait_ge(wcast.h, cast_done[ei - 1])
                if e < 0:
                    src_u = io["wsu"]
                    src_d = io["wsd"]
                else:
                    src_u = io["wu_e"](e)
                    src_d = io["wd_e"](e)
                seq_dma(sy, wld, wstg_u.rearrange("p (db f) -> p db f", db=8), src_u)
                seq_dma(sy, wld, wstg_d.rearrange("p (fb d) -> p fb d", fb=4), src_d)
            eld_done[ei] = wld.n

            @block.gpsimd
            def _(g, ei=ei, buf=buf):
                g.wait_ge(wld.h, eld_done[ei])
                if ei >= nbuf:
                    g.wait_ge(st_.h, te_done[ei - nbuf])
                g.tensor_copy(wub[buf][:, :], wstg_u[:, :])
                wcast.inc(g.tensor_copy(wdb[buf][:, :], wstg_d[:, :]))
            cast_done[ei] = wcast.n

            for cc in range(4):
                up_fb = {}

                @block.tensor
                def _(t, ei=ei, cc=cc, buf=buf, half=half, up_fb=up_fb):
                    if cc == 0:
                        t.wait_ge(wcast.h, cast_done[ei])
                        if ei == 0:
                            t.wait_ge(sv.h, routing_done)
                    key = (ei, cc - 1) if cc > 0 else (ei - 1, 3)
                    if key in silu_done:
                        t.wait_ge(sa.h, silu_done[key])
                    for fb in range(4):
                        for db in range(8):
                            mm = t.matmul(ps_h[fb][:, :],
                                          wub[buf].rearrange("p (db f) -> p db f", db=8)[:, db, fb * 128:(fb + 1) * 128],
                                          x_fm3[:, db, cc * 512:(cc + 1) * 512],
                                          start=(db == 0), stop=(db == 7))
                        st_.inc(mm)
                        up_fb[fb] = st_.n
                    last_up[0] = st_.n
                up_done = st_.n

                @block.scalar
                def _(sc, ei=ei, cc=cc, up_fb=up_fb):
                    key = (ei, cc - 1) if cc > 0 else (ei - 1, 3)
                    if key in te_done_c:
                        sc.wait_ge(st_.h, te_done_c[key])
                    for fb in range(4):
                        sc.wait_ge(st_.h, up_fb[fb])
                        sa.inc(sc.activation(h_s.rearrange("p (fb f) -> p fb f", fb=4)[:, fb, :],
                                             ps_h[fb][:, :], AF.Silu))
                silu_done[(ei, cc)] = sa.n

                for ti in range(4):
                    gt = cc * 4 + ti
                    par = gt % 2

                    @block.tensor
                    def _(t, ei=ei, cc=cc, buf=buf, ti=ti, gt=gt, par=par):
                        if ti == 0:
                            t.wait_ge(sa.h, silu_done[(ei, cc)])
                        if (ei, gt - 2) in acc_done:
                            t.wait_ge(sv.h, acc_done[(ei, gt - 2)])
                        elif ei > 0 and (ei - 1, gt + 14) in acc_done:
                            t.wait_ge(sv.h, acc_done[(ei - 1, gt + 14)])
                        for haf in range(2):
                            for fb in range(4):
                                mm = t.matmul(ps_y[2 * par + haf][:, :],
                                              h_s.rearrange("p (fb f) -> p fb f", fb=4)[:, fb, ti * 128:(ti + 1) * 128],
                                              wdb[buf].rearrange("p (fb d) -> p fb d", fb=4)[:, fb, haf * 512:(haf + 1) * 512],
                                              start=(fb == 0), stop=(fb == 3))
                        st_.inc(mm)
                        dn_tile[(ei, gt)] = st_.n

                    @block.vector
                    def _(v, ei=ei, gt=gt, par=par):
                        v.wait_ge(st_.h, dn_tile[(ei, gt)])
                        cwcol = c.consts[:, 4:5] if ei == 0 else cw[gt][:, ei - 1:ei]
                        sv.inc(v.scalar_tensor_tensor(acc[gt][:, 0:512], ps_y[2 * par][:, :],
                                               cwcol, acc[gt][:, 0:512], ALU.mult, ALU.add))
                        sv.inc(v.scalar_tensor_tensor(acc[gt][:, 512:1024], ps_y[2 * par + 1][:, :],
                                                      cwcol, acc[gt][:, 512:1024], ALU.mult, ALU.add))
                        acc_done[(ei, gt)] = sv.n
                te_done_c[(ei, cc)] = st_.n
                if cc == 3:
                    te_done[ei] = st_.n
        experts_done = sv.n
        if half == 0:
            half0_acc_done = experts_done

        for i in range(HT):
            gi = base_tile + i
            lnd = emit_ln_tile(nc, block, c, acc[i][:, :], lns, sv, sa, experts_done)
            @block.sync
            def _(sy, i=i, gi=gi, lnd=lnd):
                sy.wait_ge(sv.h, lnd)
                seq_dma(sy, so, io["y_out"][gi * 128:(gi + 1) * 128, :], acc[i][:, :])
            store_done[i] = so.n
        if half == 0:
            half0_stores = so.n

    @block.sync
    def _(sy):
        sy.wait_ge(so.h, so.n)



def emit_phaseA(nc, block, c, io, sems, pref="pa_", n_tiles=NT, arena=None):
    so2 = [sems["s0"], sems["s1"]]
    A = (lambda n, sh, dt: arena(pref + n, sh, dt)) if arena is not None \
        else (lambda n, sh, dt: nc.alloc_sbuf_tensor(pref + n, sh, dt))

    H = 16
    xt = [A(f"xt{j}", [128, DT], F32) for j in range(2)]
    xb = A("xb", [128, DT], BF16)
    xfm = A("xfm", [128, 8 * 128], BF16)
    xfm3 = xfm.rearrange("p (k t) -> p k t", k=8)
    wstg = A("wstg", [128, DT], F32)
    wstgb = A("wstgb", [128, DT], BF16)
    wqT = A("wqT", [128, 8 * 3072], BF16)
    wqT3 = wqT.rearrange("p (k o) -> p k o", k=8)
    woT = A("woT", [128, 8 * DT], BF16)
    woT3 = woT.rearrange("p (k o) -> p k o", k=8)
    q_tm = A("q_tm", [128, DT], F32)
    qn = A("qn", [128, DT], BF16)
    q_fm = A("q_fm", [64, 16 * 128], BF16)
    k_tm = A("k_tm", [128, DT], BF16)
    v_tm = A("v_tm", [128, DT], BF16)
    qmax = A("qmax", [128, 16], F32)
    qsum = A("qsum", [128, 16], F32)
    qscl = A("qscl", [128, 16], F32)
    ctx_f = A("ctx_f", [64, 1040], F32)
    ctx_b = A("ctx_b", [64, 1040], BF16)
    dinv = A("dinv", [128, 16], F32)
    attn_tm = A("attn_tm", [128, DT], BF16)
    attn_fm = A("attn_fm", [128, 8 * 128], BF16)
    attn_fm3 = attn_fm.rearrange("p (k t) -> p k t", k=8)
    ot = [A(f"ot{j}", [128, DT], F32) for j in range(2)]
    lns = alloc_ln_scratch(A)

    G = [c.PS[j][:, :] for j in range(6)]
    C0 = c.PS[6][:, :]
    C1 = c.PS[7][:, :]
    Gb = [g.bitcast(BF16) for g in G]

    ld, sv, sa, st_, so = sems["ld"], sems["v"], sems["a"], sems["t"], sems["st"]
    wld = sems["wld"]
    wcast = sems["wc"]

    @block.sync
    def _(sy):
        emit_common_load(c, sy, ld)
    ld_common = ld.n
    wq_rows = 24
    wo_rows = 8
    ev_done = {}
    for r in range(wq_rows + wo_rows):
        is_wo = r >= wq_rows
        rr = r - wq_rows if is_wo else r
        @block.sync
        def _(sy, r=r, rr=rr, is_wo=is_wo):
            if r >= 1:
                sy.wait_ge(wcast.h, r)
            src = (io["wout"] if is_wo else io["wqkv"])[rr * 128:(rr + 1) * 128, :]
            seq_dma(sy, wld, wstg[:, :], src)
        @block.gpsimd
        def _(g, r=r):
            g.wait_ge(wld.h, wld.n)
            if r >= 1:
                g.wait_ge(st_.h, tr_done[r - 1])
            wcast.inc(g.tensor_copy(wstgb[:, :], wstg[:, :]))
        @block.tensor
        def _(t, r=r, rr=rr, is_wo=is_wo):
            t.wait_ge(wcast.h, wcast.n)
            if r == 0:
                t.wait_ge(ld.h, ld_common)
            if r >= 2:
                t.wait_ge(sv.h, ev_done[r - 2])
            for kk in range(8):
                mm = t.transpose(Gb[r % 2][:, kk * 128:(kk + 1) * 128],
                                 wstgb[:, kk * 128:(kk + 1) * 128], c.id128_bf[:, :])
            st_.inc(mm)
        if r == 0:
            tr_done = {}
        tr_done[r] = st_.n
        @block.vector
        def _(v, r=r, rr=rr, is_wo=is_wo):
            v.wait_ge(st_.h, tr_done[r])
            dst3 = woT3 if is_wo else wqT3
            sv.inc(v.tensor_copy(dst3[:, :, rr * 128:(rr + 1) * 128],
                                 Gb[r % 2].rearrange("p (k t) -> p k t", k=8)))
        ev_done[r] = sv.n
    wT_done = sv.n

    @block.vector
    def _(v):
        sv.inc(v.memset(ctx_f[:, :], 0.0))
        v.wait_ge(sv.h, sv.n)
        sv.inc(v.tensor_copy(ctx_b[:, :], ctx_f[:, :]))
    ctx_init = sv.n

    x_free = {}
    store_done = {}
    evq = {}
    ctx_evac = {0: 0}
    for i in range(n_tiles):
        @block.sync
        def _(sy, i=i):
            if i >= 2:
                sy.wait_ge(sv.h, x_free[i - 2])
                sy.wait_ge(so2[i % 2].h, store_done[i - 2])
            seq_dma(sy, ld, xt[i % 2][:, :], io["x_in"][i * 128:(i + 1) * 128, :])
        x_ld = ld.n

        @block.scalar
        def _(sc, i=i, x_ld=x_ld):
            sc.wait_ge(ld.h, x_ld)
            if i >= 1:
                sc.wait_ge(st_.h, xT_mm[0])
            sa.inc(sc.copy(xb[:, :], xt[i % 2][:, :]))
        xb_done = sa.n

        @block.tensor
        def _(t, i=i, xb_done=xb_done):
            t.wait_ge(sa.h, xb_done)
            if i == 0:
                t.wait_ge(sv.h, wT_done)
            else:
                t.wait_ge(sv.h, xfm_evac[0])
            for kk in range(8):
                mm = t.transpose(Gb[0][:, kk * 128:(kk + 1) * 128],
                                 xb[:, kk * 128:(kk + 1) * 128], c.id128_bf[:, :])
            st_.inc(mm)
        if i == 0:
            xT_mm = {}
            xfm_evac = {0: 0}
        xT_mm[0] = st_.n
        xT_done = st_.n

        @block.vector
        def _(v, i=i, xT_done=xT_done):
            v.wait_ge(st_.h, xT_done)
            sv.inc(v.tensor_copy(xfm3[:, :, :], Gb[0].rearrange("p (k t) -> p k t", k=8)))
        xfm_evac[0] = sv.n
        xfm_done = sv.n

        pass_mm = {}
        for p_ in range(2):
            @block.tensor
            def _(t, i=i, p_=p_):
                t.wait_ge(sv.h, xfm_done)
                if p_ == 0:
                    t.wait_ge(sv.h, attn_evac[0] if i >= 1 else 0)
                else:
                    t.wait_ge(sv.h, evac_v[0] if i >= 1 else 0)
                for ob in range(3):
                    for kk in range(8):
                        mm = t.matmul(G[3 * p_ + ob][:, :], xfm3[:, kk, :],
                                      wqT3[:, kk, (3 * p_ + ob) * 512:(3 * p_ + ob + 1) * 512],
                                      start=(kk == 0), stop=(kk == 7))
                st_.inc(mm)
            pass_mm[p_] = st_.n

        if i == 0:
            evac_v = {0: 0}
            attn_evac = {0: 0}

        @block.vector
        def _(v, i=i, p0=pass_mm[0]):
            v.wait_ge(st_.h, p0)
            sv.inc(v.reduce_max(qmax[:, 0:8], G[0].rearrange("p (g x) -> p g x", g=8), axis=AX.X))
            sv.inc(v.reduce_max(qmax[:, 8:16], G[1].rearrange("p (g x) -> p g x", g=8), axis=AX.X))
            v.wait_ge(sv.h, sv.n)
            sv.inc(v.tensor_tensor(q_tm[:, 0:512].rearrange("p (g x) -> p g x", g=8),
                            G[0].rearrange("p (g x) -> p g x", g=8),
                            qmax[:, 0:8].rearrange("p (g o) -> p g o", o=1).to_broadcast([128, 8, 64]),
                            ALU.subtract))
            sv.inc(v.tensor_tensor(q_tm[:, 512:1024].rearrange("p (g x) -> p g x", g=8),
                                   G[1].rearrange("p (g x) -> p g x", g=8),
                                   qmax[:, 8:16].rearrange("p (g o) -> p g o", o=1).to_broadcast([128, 8, 64]),
                                   ALU.subtract))
        qsub_done = sv.n

        @block.scalar
        def _(sc, i=i, qsub_done=qsub_done):
            sc.wait_ge(sv.h, qsub_done)
            sa.inc(sc.activation(q_tm[:, :], q_tm[:, :], AF.Exp))
            sa.inc(sc.activation(k_tm[:, 0:512], G[2][:, :], AF.Exp))
        qexp_done = sa.n

        @block.vector
        def _(v, i=i, qexp_done=qexp_done):
            v.wait_ge(sa.h, qexp_done)
            sv.inc(v.reduce_sum(qsum[:, :], q_tm.rearrange("p (g x) -> p g x", g=16), axis=AX.X))
            v.wait_ge(sv.h, sv.n)
            sv.inc(v.reciprocal(qscl[:, :], qsum[:, :]))
            v.wait_ge(sv.h, sv.n)
            sv.inc(v.tensor_scalar_mul(qscl[:, :], qscl[:, :], 0.125))
            v.wait_ge(sv.h, sv.n)
            sv.inc(v.tensor_tensor(qn.rearrange("p (g x) -> p g x", g=16),
                                   q_tm.rearrange("p (g x) -> p g x", g=16),
                                   qscl.rearrange("p (g o) -> p g o", o=1).to_broadcast([128, 16, 64]),
                                   ALU.mult))
        qn_done = sv.n

        @block.scalar
        def _(sc, i=i, p1=pass_mm[1]):
            sc.wait_ge(st_.h, p1)
            sa.inc(sc.activation(k_tm[:, 512:1024], G[3][:, :], AF.Exp))
            sa.inc(sc.activation(v_tm[:, 0:512], G[4][:, :], AF.Copy))
            sa.inc(sc.activation(v_tm[:, 512:1024], G[5][:, :], AF.Copy))
        kv_done = sa.n

        @block.tensor
        def _(t, i=i, qn_done=qn_done):
            t.wait_ge(sv.h, qn_done)
            for h in range(16):
                mm = t.transpose(Gb[0 if h < 8 else 1][0:64, (h % 8) * 128:(h % 8 + 1) * 128],
                                 qn[:, h * 64:(h + 1) * 64], c.id128_bf[:, :])
            st_.inc(mm)
        qT_mm = st_.n

        @block.vector
        def _(v, i=i, qT_mm=qT_mm):
            v.wait_ge(st_.h, qT_mm)
            sv.inc(v.tensor_copy(q_fm[0:64, 0:1024], Gb[0][0:64, 0:1024]))
            sv.inc(v.tensor_copy(q_fm[0:64, 1024:2048], Gb[1][0:64, 0:1024]))
        qfm_done = sv.n

        for nb in range(2):
            gb = 2 * i + nb
            bp = 64 * nb
            tp_attn = (0, bp)
            tp_ctx = (bp, 0)
            @block.tensor
            def _(t, i=i, nb=nb, gb=gb, bp=bp, qfm_done=qfm_done, kv_done=kv_done):
                t.wait_ge(sv.h, qfm_done)
                t.wait_ge(sa.h, kv_done)
                t.wait_ge(sa.h, ctx_evac[0])
                if gb >= 1:
                    t.wait_ge(sv.h, attn_evac[0])
                for h in range(16):
                    qs = q_fm[0:64, h * 128 + bp: h * 128 + bp + 64]
                    tp = None if bp == 0 else (0, 64)
                    mm = t.matmul(G[1 if h < 8 else 2][bp:bp + 64, (h % 8) * 64:(h % 8) * 64 + 64],
                                  qs, ctx_b[0:64, h * 64:h * 64 + 64],
                                  start=True, stop=True, tile_position=tp)
                    mm = t.matmul(G[0][bp:bp + 64, h:h + 1],
                                  qs, ctx_b[0:64, 1024 + h:1025 + h],
                                  start=True, stop=True, tile_position=tp)
                st_.inc(mm)
                for h in range(16):
                    Cps = C0 if h < 8 else C1
                    tp = None if bp == 0 else (64, 0)
                    kslice = k_tm[bp:bp + 64, h * 64:(h + 1) * 64]
                    mm = t.matmul(Cps[0:64, (h % 8) * 64:(h % 8) * 64 + 64],
                                  kslice, v_tm[bp:bp + 64, h * 64:(h + 1) * 64],
                                  start=True, stop=True, tile_position=tp)
                    mm = t.matmul(G[0][0:64, 16 + h:17 + h],
                                  kslice, c.ones_cb[bp:bp + 64, 0:1],
                                  start=True, stop=True, tile_position=(bp, 0))
                st_.inc(mm)
            attn_mm = st_.n

            @block.vector
            def _(v, i=i, nb=nb, gb=gb, bp=bp, attn_mm=attn_mm):
                v.wait_ge(st_.h, attn_mm)
                sv.inc(v.tensor_scalar_max(dinv[bp:bp + 64, :], G[0][bp:bp + 64, 0:16], 1e-6))
                v.wait_ge(sv.h, sv.n)
                sv.inc(v.reciprocal(dinv[bp:bp + 64, :], dinv[bp:bp + 64, :]))
                v.wait_ge(sv.h, sv.n)
                sv.inc(v.tensor_tensor(
                    attn_tm[bp:bp + 64, 0:512].rearrange("p (g x) -> p g x", g=8),
                    G[1][bp:bp + 64, :].rearrange("p (g x) -> p g x", g=8),
                    dinv[bp:bp + 64, 0:8].rearrange("p (g o) -> p g o", o=1).to_broadcast([64, 8, 64]),
                    ALU.mult))
                sv.inc(v.tensor_tensor(
                    attn_tm[bp:bp + 64, 512:1024].rearrange("p (g x) -> p g x", g=8),
                    G[2][bp:bp + 64, :].rearrange("p (g x) -> p g x", g=8),
                    dinv[bp:bp + 64, 8:16].rearrange("p (g o) -> p g o", o=1).to_broadcast([64, 8, 64]),
                    ALU.mult))
                v.wait_ge(sv.h, sv.n)
                sv.inc(v.tensor_tensor(ctx_f[0:64, 0:512], ctx_f[0:64, 0:512], C0[0:64, :], ALU.add))
                sv.inc(v.tensor_tensor(ctx_f[0:64, 512:1024], ctx_f[0:64, 512:1024], C1[0:64, :], ALU.add))
                sv.inc(v.tensor_tensor(ctx_f[0:64, 1024:1040], ctx_f[0:64, 1024:1040],
                                       G[0][0:64, 16:32], ALU.add))
            attn_evac[0] = sv.n
            ctx_add = sv.n

            @block.scalar
            def _(sc, gb=gb, ctx_add=ctx_add):
                sc.wait_ge(sv.h, ctx_add)
                sa.inc(sc.copy(ctx_b[:, :], ctx_f[:, :]))
            ctx_evac[0] = sa.n

        @block.tensor
        def _(t, i=i):
            t.wait_ge(sv.h, attn_evac[0])
            for kk in range(8):
                mm = t.transpose(Gb[3][:, kk * 128:(kk + 1) * 128],
                                 attn_tm[:, kk * 128:(kk + 1) * 128], c.id128_bf[:, :])
            st_.inc(mm)
        aT_mm = st_.n

        @block.vector
        def _(v, i=i, aT_mm=aT_mm):
            v.wait_ge(st_.h, aT_mm)
            sv.inc(v.tensor_copy(attn_fm3[:, :, :], Gb[3].rearrange("p (k t) -> p k t", k=8)))
        afm_done = sv.n

        @block.tensor
        def _(t, i=i, afm_done=afm_done):
            t.wait_ge(sv.h, afm_done)
            for haf in range(2):
                for kk in range(8):
                    mm = t.matmul(G[4 + haf][:, :], attn_fm3[:, kk, :],
                                  woT3[:, kk, haf * 512:(haf + 1) * 512],
                                  start=(kk == 0), stop=(kk == 7))
            st_.inc(mm)
        wo_mm = st_.n

        @block.vector
        def _(v, i=i, wo_mm=wo_mm):
            v.wait_ge(st_.h, wo_mm)
            if i >= 2:
                v.wait_ge(so2[i % 2].h, store_done[i - 2])
            sv.inc(v.tensor_tensor(ot[i % 2][:, 0:512], G[4][:, :], xt[i % 2][:, 0:512], ALU.add))
            sv.inc(v.tensor_tensor(ot[i % 2][:, 512:1024], G[5][:, :], xt[i % 2][:, 512:1024], ALU.add))
        evac_v[0] = sv.n
        x_free[i] = sv.n
        ot_done = sv.n

        ln_done = emit_ln_tile(nc, block, c, ot[i % 2][:, :], lns, sv, sa, ot_done)

        @block.sync
        def _(sy, i=i, ln_done=ln_done):
            sy.wait_ge(sv.h, ln_done)
            seq_dma(sy, so2[i % 2], io["y_out"][i * 128:(i + 1) * 128, :], ot[i % 2][:, :])
        store_done[i] = so2[i % 2].n

    @block.sync
    def _(sy):
        sy.wait_ge(so2[0].h, so2[0].n)
        sy.wait_ge(so2[1].h, so2[1].n)



def emit_phaseC_setup(nc, block, c, io, sems, sb, pref="pc_", arena=None):
    A = (lambda n, sh, dt: arena(pref + n, sh, dt)) if arena is not None \
        else (lambda n, sh, dt: nc.alloc_sbuf_tensor(pref + n, sh, dt))
    ld, sv, sa, st_ = sems["ld"], sems["v"], sems["a"], sems["t"]
    wld, wcast = sems["wld"], sems["wc"]

    eta_s = A("eta", [1, 1], F32)
    laml_s = A("laml", [1, 1], F32)
    lam_s = A("lam", [1, 4], F32)
    prow = A("prow", [1, 65], F32)
    lamrow = A("lamrow", [1, 64], F32)
    onesr = A("onesr", [1, 64], F32)
    bc = A("bc", [128, 8], F32)
    lampow = A("lampow", [128, 1], F32)
    lamrev = A("lamrev", [128, 1], F32)
    nbetapow = A("nbetapow", [128, 1], F32)
    E128 = A("E128", [128, 64], F32)
    masks = A("masks", [128, 4 * 64], F32)
    idbeta = A("idbeta", [128, 128], BF16)
    gam_b = A("gamb", [128, DT], F32)
    bet_b = A("betb", [128, DT], F32)
    gam_r = A("gamr", [1, DT], F32)
    bet_r = A("betr", [1, DT], F32)
    wT = A("wT", [128, 8 * DT], BF16)
    wT3 = wT.rearrange("p (k o) -> p k o", k=8)
    wstg = A("wstg", [128, DT], F32)
    wstgb = A("wstgb", [128, DT], BF16)
    sb.update(locals())

    import ml_dtypes
    rev = np.zeros((64, 64), np.float32)
    for i in range(64):
        rev[i, 63 - i] = 1.0
    d_rev = nc.inline_tensor(rev.astype(ml_dtypes.bfloat16), pref + "d_rev")
    revm = A("revm", [64, 64], BF16)
    sb["revm"] = revm

    PS = [c.PS[j][:, :] for j in range(8)]
    sb["PS"] = PS

    @block.sync
    def _(sy):
        emit_common_load(c, sy, ld)
        seq_dma(sy, ld, eta_s[:, :], io["eta"])
        seq_dma(sy, ld, laml_s[:, :], io["laml"])
        seq_dma(sy, ld, gam_r[:, :], io["gam"])
        seq_dma(sy, ld, bet_r[:, :], io["bet"])
        seq_dma(sy, ld, revm[:, :], dram_ap(d_rev, (64, 64)))
    setup_ld = ld.n

    @block.scalar
    def _(sc):
        sc.wait_ge(ld.h, setup_ld)
        sa.inc(sc.activation(lam_s[0:1, 0:1], laml_s[0:1, 0:1], AF.Sigmoid))
        sc.wait_ge(sa.h, sa.n)
        sa.inc(sc.activation(lam_s[0:1, 3:4], lam_s[0:1, 0:1], AF.Ln))
    lam_done = sa.n

    @block.vector
    def _(v):
        v.wait_ge(sa.h, lam_done)
        sv.inc(v.tensor_scalar(lam_s[0:1, 1:2], lam_s[0:1, 0:1], -1.0, 1.0, ALU.mult, ALU.add))
        v.wait_ge(sv.h, sv.n)
        sv.inc(v.tensor_tensor(lam_s[0:1, 1:2], lam_s[0:1, 1:2], eta_s[0:1, 0:1], ALU.mult))
        v.wait_ge(sv.h, sv.n)
        sv.inc(v.tensor_scalar_mul(lam_s[0:1, 1:2], lam_s[0:1, 1:2], 1.0 / 64))
        sv.inc(v.memset(onesr[0:1, :], 1.0))
        v.wait_ge(sv.h, sv.n)
        sv.inc(v.tensor_scalar(lamrow[0:1, :], onesr[0:1, :], lam_s[0:1, 0:1], None, ALU.mult))
        v.wait_ge(sv.h, sv.n)
        sv.inc(v.tensor_tensor_scan(prow[0:1, 1:65], lamrow[0:1, :], onesr[0:1, :], 1.0,
                                    ALU.mult, ALU.mult))
        sv.inc(v.memset(prow[0:1, 0:1], 1.0))
        v.wait_ge(sv.h, sv.n)
        sv.inc(v.tensor_copy(lam_s[0:1, 2:3], prow[0:1, 64:65]))
    scal_done = sv.n

    @block.tensor
    def _(t):
        t.wait_ge(sv.h, scal_done)
        t.matmul(PS[0][0:64, 0:1], prow[0:1, 1:65], c.ones_f[0:1, 0:1], start=True, stop=True)
        t.matmul(PS[0][64:128, 0:1], prow[0:1, 1:65], c.ones_f[0:1, 0:1], start=True, stop=True,
                 tile_position=(0, 64))
        t.matmul(PS[0][:, 8:9], c.ones_f[0:1, 0:128], lam_s[0:1, 1:2], start=True, stop=True)
        t.matmul(PS[0][:, 9:10], c.ones_f[0:1, 0:128], lam_s[0:1, 2:3], start=True, stop=True)
        t.matmul(PS[0][:, 10:11], c.ones_f[0:1, 0:128], lam_s[0:1, 3:4], start=True, stop=True)
        st_.inc(t.matmul(PS[0][:, 11:12], c.ones_f[0:1, 0:128], lam_s[0:1, 0:1], start=True, stop=True))
    bc_mm = st_.n

    @block.vector
    def _(v):
        v.wait_ge(st_.h, bc_mm)
        sv.inc(v.tensor_copy(lampow[:, :], PS[0][:, 0:1]))
        sv.inc(v.tensor_copy(bc[:, 0:4], PS[0][:, 8:12]))
    bc_done = sv.n

    lampow_b = A("lampowb", [128, 1], BF16)
    sb["lampow_b"] = lampow_b

    @block.scalar
    def _(sc):
        sc.wait_ge(sv.h, bc_done)
        sa.inc(sc.copy(lampow_b[:, :], lampow[:, :]))
    lpb_done = sa.n

    @block.tensor
    def _(t):
        t.wait_ge(sa.h, lpb_done)
        t.matmul(PS[1][0:64, 0:1], revm[0:64, 0:64], lampow_b[0:64, 0:1], start=True, stop=True)
        st_.inc(t.matmul(PS[1][64:128, 0:1], revm[0:64, 0:64], lampow_b[0:64, 0:1],
                         start=True, stop=True, tile_position=(0, 64)))
    rev_mm = st_.n

    @block.vector
    def _(v):
        v.wait_ge(st_.h, rev_mm)
        sv.inc(v.reciprocal(bc[:, 4:5], bc[:, 3:4]))
        v.wait_ge(sv.h, sv.n)
        sv.inc(v.tensor_tensor(lamrev[:, :], PS[1][:, 0:1], bc[:, 4:5], ALU.mult))
    lamrev_done = sv.n

    @block.gpsimd
    def _(g):
        g.wait_ge(sv.h, lamrev_done)
        g.iota(E128[0:64, :], [[-1, 64]], base=0, channel_multiplier=1,
               allow_small_or_imprecise_dtypes=True)
        g.iota(E128[64:128, :], [[-1, 64]], base=0, channel_multiplier=1,
               allow_small_or_imprecise_dtypes=True)
        wcast.inc(g.memset(masks[:, :], 0.0))
    iota_done = wcast.n

    @block.vector
    def _(v):
        v.wait_ge(wcast.h, iota_done)
        sv.inc(v.tensor_scalar_max(masks[:, 128:192], E128[:, :], 0.0))
    emax_done = sv.n

    @block.scalar
    def _(sc):
        sc.wait_ge(sv.h, emax_done)
        sa.inc(sc.activation(masks[:, 128:192], masks[:, 128:192], AF.Exp, scale=bc[:, 2:3]))
    lexp_done = sa.n

    @block.vector
    def _(v):
        v.wait_ge(sa.h, lexp_done)
        sv.inc(v.tensor_single_scalar(masks[:, 0:64], E128[:, :], 0.0, ALU.is_gt))
        v.wait_ge(sv.h, sv.n)
        sv.inc(v.tensor_tensor(masks[:, 0:64], masks[:, 0:64], masks[:, 128:192], ALU.mult))
        v.wait_ge(sv.h, sv.n)
        sv.inc(v.tensor_scalar(masks[:, 0:64], masks[:, 0:64], bc[:, 0:1], -1.0,
                               ALU.mult, ALU.mult))
        sv.inc(v.tensor_single_scalar(masks[:, 192:256], E128[:, :], 0.0, ALU.is_ge))
        v.wait_ge(sv.h, sv.n)
        sv.inc(v.tensor_tensor(masks[:, 128:192], masks[:, 128:192], masks[:, 192:256], ALU.mult))
    l_done = sv.n

    @block.vector
    def _(v):
        v.wait_ge(sv.h, l_done)
        sv.inc(v.tensor_scalar_mul(E128[:, :], E128[:, :], -1.0))
        v.wait_ge(sv.h, sv.n)
        sv.inc(v.tensor_scalar_max(masks[:, 64:128], E128[:, :], 0.0))
    emax2_done = sv.n

    @block.scalar
    def _(sc):
        sc.wait_ge(sv.h, emax2_done)
        sa.inc(sc.activation(masks[:, 64:128], masks[:, 64:128], AF.Exp, scale=bc[:, 2:3]))
    lexp2_done = sa.n

    @block.vector
    def _(v):
        v.wait_ge(sa.h, lexp2_done)
        sv.inc(v.tensor_single_scalar(masks[:, 192:256], E128[:, :], 0.0, ALU.is_gt))
        v.wait_ge(sv.h, sv.n)
        sv.inc(v.tensor_tensor(masks[:, 192:256], masks[:, 192:256], masks[:, 64:128], ALU.mult))
        v.wait_ge(sv.h, sv.n)
        sv.inc(v.tensor_scalar(masks[:, 192:256], masks[:, 192:256], bc[:, 0:1], -1.0,
                               ALU.mult, ALU.mult))
        sv.inc(v.tensor_single_scalar(E128[:, :], E128[:, :], 0.0, ALU.is_ge))
        v.wait_ge(sv.h, sv.n)
        sv.inc(v.tensor_tensor(masks[:, 64:128], masks[:, 64:128], E128[:, :], ALU.mult))
    masks_done = sv.n

    @block.vector
    def _(v):
        v.wait_ge(sv.h, masks_done)
        sv.inc(v.tensor_scalar(idbeta[:, :], c.id128_f[:, :], bc[:, 0:1], None, ALU.mult))
    idb_done = sv.n

    @block.tensor
    def _(t):
        t.wait_ge(ld.h, setup_ld)
        t.wait_ge(sv.h, idb_done)
        t.matmul(PS[2][:, :], c.ones_f[0:1, 0:128], gam_r[0:1, 0:512], start=True, stop=True)
        t.matmul(PS[3][:, :], c.ones_f[0:1, 0:128], gam_r[0:1, 512:1024], start=True, stop=True)
        t.matmul(PS[4][:, :], c.ones_f[0:1, 0:128], bet_r[0:1, 0:512], start=True, stop=True)
        st_.inc(t.matmul(PS[5][:, :], c.ones_f[0:1, 0:128], bet_r[0:1, 512:1024], start=True, stop=True))
    gb_mm = st_.n

    @block.vector
    def _(v):
        v.wait_ge(st_.h, gb_mm)
        sv.inc(v.tensor_copy(gam_b[:, 0:512], PS[2][:, :]))
        sv.inc(v.tensor_copy(gam_b[:, 512:1024], PS[3][:, :]))
        sv.inc(v.tensor_copy(bet_b[:, 0:512], PS[4][:, :]))
        sv.inc(v.tensor_copy(bet_b[:, 512:1024], PS[5][:, :]))
    gb_done = sv.n

    tr_done = {}
    ev_done = {}
    for r in range(8):
        @block.sync
        def _(sy, r=r):
            if r >= 1:
                sy.wait_ge(wcast.h, iota_done + r)
            seq_dma(sy, wld, wstg[:, :], io["w_in"][r * 128:(r + 1) * 128, :])
        @block.gpsimd
        def _(g, r=r):
            g.wait_ge(wld.h, wld.n)
            if r >= 1:
                g.wait_ge(st_.h, tr_done[r - 1])
            wcast.inc(g.tensor_copy(wstgb[:, :], wstg[:, :]))
        @block.tensor
        def _(t, r=r):
            t.wait_ge(wcast.h, wcast.n)
            if r == 0:
                t.wait_ge(sv.h, gb_done)
            if r >= 2:
                t.wait_ge(sv.h, ev_done[r - 2])
            for kk in range(8):
                mm = t.transpose(PS[r % 2].bitcast(BF16)[:, kk * 128:(kk + 1) * 128],
                                 wstgb[:, kk * 128:(kk + 1) * 128], c.id128_bf[:, :])
            st_.inc(mm)
        tr_done[r] = st_.n
        @block.vector
        def _(v, r=r):
            v.wait_ge(st_.h, tr_done[r])
            sv.inc(v.tensor_copy(wT3[:, :, r * 128:(r + 1) * 128],
                                 PS[r % 2].bitcast(BF16).rearrange("p (k t) -> p k t", k=8)))
        ev_done[r] = sv.n
    sb["setup_done"] = sv.n
    return sb


def emit_phaseC(nc, block, c, io, sems, pref="pc_", n_tiles=NT, stage_limit=99, arena=None):
    so3 = [sems["s0"], sems["s1"], sems["s2"]]
    A = (lambda n, sh, dt: arena(pref + n, sh, dt)) if arena is not None \
        else (lambda n, sh, dt: nc.alloc_sbuf_tensor(pref + n, sh, dt))
    sb = {}
    emit_phaseC_setup(nc, block, c, io, sems, sb, pref, arena=arena)
    ld, sv, sa, st_, so = sems["ld"], sems["v"], sems["a"], sems["t"], sems["st"]
    PS = sb["PS"]
    masks, idbeta = sb["masks"], sb["idbeta"]
    lampow, lamrev, bc = sb["lampow"], sb["lamrev"], sb["bc"]
    gam_b, bet_b, wT3 = sb["gam_b"], sb["bet_b"], sb["wT3"]

    RD = 3
    xt_r = [A(f"xtr{j}", [128, DT], F32) for j in range(RD)]
    st_r = [A(f"str{j}", [128, DT], BF16) for j in range(RD)]
    kb_r = [A(f"kbr{j}", [128, DT], BF16) for j in range(RD)]
    pt_r = [A(f"ptr{j}", [128, DT], BF16) for j in range(RD)]
    gw_r = [A(f"gwr{j}", [128, DT], BF16) for j in range(RD)]
    rt_r = [A(f"rtr{j}", [128, DT], BF16) for j in range(RD)]
    wn_r = [A(f"wnr{j}", [128, DT], BF16) for j in range(RD)]

    xa_tm = A("xa_tm", [128, DT], BF16)
    xa_fm = A("xa_fm", [128, 8 * 128], BF16)
    xa_fm3 = xa_fm.rearrange("p (k t) -> p k t", k=8)
    LK = A("LK", [128, DT], BF16)
    NLK = A("NLK", [128, DT], BF16)
    Nsb = A("Nsb", [128, 512], BF16)
    NTsb = A("NTsb", [128, 512], BF16)
    Gsb = A("Gsb", [128, 512], BF16)
    GTsb = A("GTsb", [128, 512], BF16)
    Npw = [A(f"Npw{j}", [128, 512], BF16) for j in range(2)]
    NTpw = [A(f"NTpw{j}", [128, 512], BF16) for j in range(2)]
    X_sb = A("X_sb", [128, 8 * 128], BF16)
    S_f = A("S_f", [128, 1024], F32)
    S_sb = A("S_sb", [128, 1024], BF16)
    U_sb = A("U_sb", [128, 1024], BF16)
    lkt_tmp = A("lkt_tmp", [128, 512], BF16)
    yt = A("yt", [128, DT], F32)
    ot = [A(f"ot{j}", [128, DT], F32) for j in range(2)]
    lns = alloc_ln_scratch(A)

    mb3 = lambda col: masks[:, col * 64:(col + 1) * 64].rearrange(
        "p (o x) -> p o x", o=1).to_broadcast([128, 8, 64])
    scol = lambda h: (0 if h < 8 else 512) + ((h // 2) % 4) * 64

    @block.vector
    def _(v):
        v.wait_ge(sv.h, sb["setup_done"])
        for j in range(8):
            v.memset(PS[j], 0.0)
        sv.inc(v.memset(S_f[:, :], 0.0))
        v.wait_ge(sv.h, sv.n)
        sv.inc(v.tensor_copy(S_sb[:, :], S_f[:, :]))
    init_done = sv.n

    ring_ok = {}
    ring_sv = {}
    store_done = {}
    x_free = {}
    scan_done = {-1: 0}

    def emit_scan(it):
        r = it % RD

        for nb in range(2):
            cp = 64 * nb
            csl = slice(nb * 512, (nb + 1) * 512)
            @block.tensor
            def _(t, it=it, nb=nb, cp=cp):
                t.wait_ge(sa.h, ring_ok[it])
                t.wait_ge(sv.h, ring_sv[it])
                t.wait_ge(sv.h, init_done)
                t.wait_ge(sa.h, s_refresh[0])
                if it >= 1 or nb == 1:
                    t.wait_ge(sv.h, ydone[0])
                    t.wait_ge(sv.h, u_evac[0])
                    t.wait_ge(sv.h, sdelta_done[0])
                if "u" not in SUB9:
                    st_.inc(t.matmul(PS[6][0:64, 0:64], c.id128_bf[0:64, 0:64],
                                     S_sb[0:64, 0:64], start=True, stop=True))
                else:
                    for h in range(16):
                        p0 = (h % 2) * 64
                        Pu = PS[6] if h % 2 == 0 else PS[7]
                        mm = t.matmul(Pu[cp:cp + 64, (h // 2) * 64:(h // 2) * 64 + 64],
                                      pt_r[it % RD][p0:p0 + 64, nb * 512 + (h // 2) * 64: nb * 512 + (h // 2) * 64 + 64],
                                      S_sb[p0:p0 + 64, scol(h):scol(h) + 64],
                                      start=True, stop=True, tile_position=(p0, cp))
                    st_.inc(mm)
            u_mm = st_.n

            @block.vector
            def _(v, it=it, nb=nb, cp=cp, u_mm=u_mm):
                v.wait_ge(st_.h, u_mm)
                if "ue" in SUB9:
                    u3 = U_sb[cp:cp + 64, :].rearrange("p (s two x) -> p s two x", s=8, two=2)
                    wr3 = wn_r[it % RD][cp:cp + 64, :].rearrange("p (s two x) -> p s two x", s=8, two=2)
                    sv.inc(v.scalar_tensor_tensor(u3[:, :, 0, :],
                                           PS[6][cp:cp + 64, :].rearrange("p (s x) -> p s x", s=8),
                                           -1.0, wr3[:, :, 0, :], ALU.mult, ALU.add))
                    sv.inc(v.scalar_tensor_tensor(u3[:, :, 1, :],
                                                  PS[7][cp:cp + 64, :].rearrange("p (s x) -> p s x", s=8),
                                                  -1.0, wr3[:, :, 1, :], ALU.mult, ALU.add))
                else:
                    sv.inc(v.memset(U_sb[cp:cp + 64, :], 0.0))
            u_evac[0] = sv.n

            @block.tensor
            def _(t, it=it, nb=nb, cp=cp):
                if "o" not in SUB9:
                    st_.inc(t.matmul(PS[2][0:64, 0:64], c.id128_bf[0:64, 0:64],
                                     S_sb[0:64, 0:64], start=True, stop=True))
                    return
                for h in range(16):
                    p0 = (h % 2) * 64
                    Po = PS[2] if h % 2 == 0 else PS[5]
                    mm = t.matmul(Po[cp:cp + 64, (h // 2) * 64:(h // 2) * 64 + 64],
                                  rt_r[it % RD][p0:p0 + 64, nb * 512 + (h // 2) * 64: nb * 512 + (h // 2) * 64 + 64],
                                  S_sb[p0:p0 + 64, scol(h):scol(h) + 64],
                                  start=True, stop=True, tile_position=(p0, cp))
                st_.inc(mm)
            o_mm = st_.n

            @block.tensor
            def _(t, it=it, nb=nb, cp=cp):
                t.wait_ge(sv.h, u_evac[0])
                if "d" not in SUB9:
                    st_.inc(t.matmul(PS[3][0:64, 0:64], c.id128_bf[0:64, 0:64],
                                     S_sb[0:64, 0:64], start=True, stop=True))
                    return
                for h in range(16):
                    p0 = (h % 2) * 64
                    Pd = PS[3] if h < 8 else PS[4]
                    mm = t.matmul(Pd[p0:p0 + 64, ((h // 2) % 4) * 64:((h // 2) % 4) * 64 + 64],
                                  kb_r[it % RD][cp:cp + 64, h * 64:h * 64 + 64],
                                  U_sb[cp:cp + 64, h * 64:h * 64 + 64],
                                  start=True, stop=True, tile_position=(cp, p0))
                st_.inc(mm)
            d_mm = st_.n

            @block.vector
            def _(v, d_mm=d_mm):
                v.wait_ge(st_.h, d_mm)
                if "supd" in SUB9:
                    sv.inc(v.scalar_tensor_tensor(S_f[:, 0:512], S_f[:, 0:512], bc[:, 1:2], PS[3][:, :],
                                           ALU.mult, ALU.add))
                    sv.inc(v.scalar_tensor_tensor(S_f[:, 512:1024], S_f[:, 512:1024], bc[:, 1:2], PS[4][:, :],
                                                  ALU.mult, ALU.add))
                else:
                    sv.inc(v.memset(S_f[0:1, 0:1], 0.0))
            sdelta_done[0] = sv.n

            @block.scalar
            def _(sc):
                sc.wait_ge(sv.h, sdelta_done[0])
                sa.inc(sc.copy(S_sb[:, :], S_f[:, :]))
            s_refresh[0] = sa.n

            @block.vector
            def _(v, it=it, nb=nb, cp=cp, o_mm=o_mm):
                v.wait_ge(st_.h, o_mm)
                if "y" in SUB9:
                    y3 = yt[cp:cp + 64, :].rearrange("p (s two x) -> p s two x", s=8, two=2)
                    s3 = st_r[it % RD][cp:cp + 64, :].rearrange("p (s two x) -> p s two x", s=8, two=2)
                    sv.inc(v.tensor_tensor(y3[:, :, 0, :], PS[2][cp:cp + 64, :].rearrange("p (s x) -> p s x", s=8),
                                    s3[:, :, 0, :], ALU.add))
                    sv.inc(v.tensor_tensor(y3[:, :, 1, :], PS[5][cp:cp + 64, :].rearrange("p (s x) -> p s x", s=8),
                                           s3[:, :, 1, :], ALU.add))
                    v.wait_ge(sv.h, sv.n)
                    sv.inc(v.tensor_tensor(yt[cp:cp + 64, :], yt[cp:cp + 64, :],
                                           gw_r[it % RD][cp:cp + 64, :], ALU.add))
                else:
                    sv.inc(v.memset(yt[cp:cp + 64, :], 0.5))
            ydone[0] = sv.n

        yt_done = sv.n
        if "ln" in SUB9:
            ln1 = emit_ln_tile(nc, block, c, yt[:, :], lns, sv, sa, yt_done,
                               gamma_b=gam_b[:, :], beta_b=bet_b[:, :])

            @block.vector
            def _(v, it=it, ln1=ln1):
                v.wait_ge(sv.h, ln1)
                if it >= 2:
                    v.wait_ge(so3[(it - 2) % 3].h, store_done[it - 2])
                sv.inc(v.tensor_tensor(ot[it % 2][:, :], yt[:, :], xt_r[it % RD][:, :], ALU.add))
            x_free[it] = sv.n
            ot_done = sv.n

            ln2 = emit_ln_tile(nc, block, c, ot[it % 2][:, :], lns, sv, sa, ot_done)
        else:
            @block.vector
            def _(v, it=it, yt_done=yt_done):
                v.wait_ge(sv.h, yt_done)
                if it >= 2:
                    v.wait_ge(so3[(it - 2) % 3].h, store_done[it - 2])
                sv.inc(v.tensor_copy(ot[it % 2][:, :], yt[:, :]))
            x_free[it] = sv.n
            ln2 = sv.n

        @block.sync
        def _(sy, it=it, ln2=ln2):
            sy.wait_ge(sv.h, ln2)
            seq_dma(sy, so3[it % 3], io["y_out"][it * 128:(it + 1) * 128, :], ot[it % 2][:, :])
        store_done[it] = so3[it % 3].n
        scan_done[it] = sv.n

    u_evac = {0: 0}
    sdelta_done = {0: 0}
    ydone = {0: 0}
    maskprod_done = {0: 0}
    x0_evac = {0: 0}
    solve_done = {0: 0}
    gw_ev = {0: 0}
    wr_ev = {0: 0}
    rt_ev = {0: 0}
    s_refresh = {0: init_done and 0}
    s_refresh[0] = sems["a"].n

    for i in range(n_tiles):
        if i >= 1 and stage_limit >= 9:
            emit_scan(i - 1)

        r = i % RD
        @block.sync
        def _(sy, i=i):
            if i >= RD:
                sy.wait_ge(sv.h, x_free.get(i - RD, 0))
                sy.wait_ge(so3[(i - RD) % 3].h, store_done.get(i - RD, 0))
            seq_dma(sy, ld, xt_r[i % RD][:, :], io["x_in"][i * 128:(i + 1) * 128, :])
        x_ld = ld.n

        @block.scalar
        def _(sc, i=i, x_ld=x_ld):
            sc.wait_ge(ld.h, x_ld)
            if i >= 1:
                sc.wait_ge(sv.h, lk_done[0])
            sa.inc(sc.activation(xa_tm[:, :], xt_r[i % RD][:, :], AF.Tanh))
        tanh_done = sa.n

        @block.tensor
        def _(t, i=i, tanh_done=tanh_done):
            t.wait_ge(sa.h, tanh_done)
            if i == 0:
                t.wait_ge(sv.h, init_done)
            else:
                t.wait_ge(sv.h, xafm_evac[0])
            for kk in range(8):
                mm = t.transpose(PS[0].bitcast(BF16)[:, kk * 128:(kk + 1) * 128],
                                 xa_tm[:, kk * 128:(kk + 1) * 128], c.id128_bf[:, :])
            st_.inc(mm)
        xaT_mm = st_.n

        @block.vector
        def _(v, i=i, xaT_mm=xaT_mm):
            v.wait_ge(st_.h, xaT_mm)
            sv.inc(v.tensor_copy(xa_fm3[:, :, :],
                                 PS[0].bitcast(BF16).rearrange("p (k t) -> p k t", k=8)))
        if i == 0:
            xafm_evac = {0: 0}
            lk_done = {0: 0}
            st_evac = {0: 0}
        xafm_evac[0] = sv.n
        xafm_done = sv.n

        if stage_limit < 2:
            continue
        @block.tensor
        def _(t, i=i, xafm_done=xafm_done):
            t.wait_ge(sv.h, xafm_done)
            if i >= 1:
                t.wait_ge(sv.h, ydone[0])
                t.wait_ge(sv.h, st_evac[0])
            for haf in range(2):
                for kk in range(8):
                    mm = t.matmul(PS[1 + haf][:, :], xa_fm3[:, kk, :],
                                  wT3[:, kk, haf * 512:(haf + 1) * 512],
                                  start=(kk == 0), stop=(kk == 7))
            st_.inc(mm)
        stat_mm = st_.n

        @block.vector
        def _(v, i=i, stat_mm=stat_mm):
            v.wait_ge(st_.h, stat_mm)
            sv.inc(v.tensor_copy(st_r[i % RD][:, 0:512], PS[1][:, :]))
            sv.inc(v.tensor_copy(st_r[i % RD][:, 512:1024], PS[2][:, :]))
        st_evac[0] = sv.n
        static_done = sv.n

        if stage_limit < 3:
            continue
        @block.vector
        def _(v, i=i, static_done=static_done, tanh_done=tanh_done):
            v.wait_ge(sa.h, tanh_done)
            sv.inc(v.tensor_scalar(LK[:, :], xa_tm[:, :], lampow[:, 0:1], None, ALU.mult))
            v.wait_ge(sv.h, sv.n)
            sv.inc(v.tensor_scalar(NLK[:, :], LK[:, :], -1.0, None, ALU.mult))
            sv.inc(v.tensor_scalar(kb_r[i % RD][:, :], xa_tm[:, :], lamrev[:, 0:1], None, ALU.mult))
        lk_done[0] = sv.n

        for nb in range(2):
            if stage_limit < 4 or (nb == 1 and stage_limit < 8):
                continue
            bp = 64 * nb
            csl_base = nb * 512
            @block.tensor
            def _(t, i=i, nb=nb, bp=bp, sa_bar=None):
                t.wait_ge(sv.h, lk_done[0])
                t.wait_ge(sv.h, sdelta_done[0])
                t.wait_ge(sv.h, rt_ev[0])
                t.wait_ge(sa.h, sa.n)
                if i >= 1 or nb == 1:
                    t.wait_ge(sv.h, maskprod_done[0])
                for h in range(16):
                    p0 = (h % 2) * 64
                    mm = t.matmul(PS[3][p0:p0 + 64, (h // 2) * 64:(h // 2) * 64 + 64],
                                  xa_fm3[(h % 2) * 64:(h % 2) * 64 + 64, h // 2, bp:bp + 64],
                                  xa_fm3[(h % 2) * 64:(h % 2) * 64 + 64, h // 2, bp:bp + 64],
                                  start=True, stop=True, tile_position=(p0, p0))
                st_.inc(mm)
            a_mm = st_.n

            @block.vector
            def _(v, i=i, nb=nb, a_mm=a_mm):
                v.wait_ge(st_.h, a_mm)
                PA3 = PS[3].rearrange("p (s x) -> p s x", s=8)
                sv.inc(v.tensor_tensor(Nsb.rearrange("p (s x) -> p s x", s=8), PA3, mb3(0), ALU.mult))
                sv.inc(v.tensor_tensor(NTsb.rearrange("p (s x) -> p s x", s=8), PA3, mb3(3), ALU.mult))
                sv.inc(v.tensor_tensor(Gsb.rearrange("p (s x) -> p s x", s=8), PA3, mb3(2), ALU.mult))
                sv.inc(v.tensor_tensor(GTsb.rearrange("p (s x) -> p s x", s=8), PA3, mb3(1), ALU.mult))
            maskprod_done[0] = sv.n

            if stage_limit < 5:
                continue
            @block.tensor
            def _(t, i=i, nb=nb, bp=bp, static_done=static_done):
                t.wait_ge(sv.h, static_done)
                t.wait_ge(sv.h, lk_done[0])
                if i >= 1 or nb == 1:
                    t.wait_ge(sv.h, x0_evac[0])
                    t.wait_ge(sv.h, wr_ev[0])
                    t.wait_ge(sv.h, ydone[0])
                for h in range(16):
                    p0 = (h % 2) * 64
                    PX = PS[4] if h < 8 else PS[5]
                    sl = (h % 8) // 2 * 128 + (h % 2) * 0
                    sl = ((h // 2) % 4) * 128
                    t.matmul(PX[p0:p0 + 64, sl:sl + 64],
                             idbeta[bp:bp + 64, bp:bp + 64],
                             st_r[i % RD][bp:bp + 64, h * 64:h * 64 + 64],
                             start=True, stop=True, tile_position=(bp, p0))
                    mm = t.matmul(PX[p0:p0 + 64, sl + 64:sl + 128],
                                  idbeta[bp:bp + 64, bp:bp + 64],
                                  LK[bp:bp + 64, h * 64:h * 64 + 64],
                                  start=True, stop=True, tile_position=(bp, p0))
                st_.inc(mm)
            x0_mm = st_.n

            @block.vector
            def _(v, x0_mm=x0_mm):
                v.wait_ge(st_.h, x0_mm)
                sv.inc(v.tensor_copy(X_sb[:, 0:512], PS[4][:, :]))
                sv.inc(v.tensor_copy(X_sb[:, 512:1024], PS[5][:, :]))
            x0_evac[0] = sv.n

            if stage_limit < 6:
                continue
            for lvl in range(6):
                cur = lvl % 2
                nxt = 1 - cur
                @block.tensor
                def _(t, i=i, nb=nb, lvl=lvl, cur=cur):
                    t.wait_ge(sv.h, x0_evac[0])
                    if lvl == 0:
                        pass
                    else:
                        t.wait_ge(sa.h, sq_evac[0])
                    for h in range(16):
                        p0 = (h % 2) * 64
                        PX = PS[4] if h < 8 else PS[5]
                        sl = ((h // 2) % 4) * 128
                        NTs = NTsb if lvl == 0 else NTpw[cur]
                        t.matmul(PX[p0:p0 + 64, sl:sl + 128],
                                 c.id128_bf[p0:p0 + 64, p0:p0 + 64],
                                 X_sb[p0:p0 + 64, sl:sl + 128],
                                 start=True, stop=False, tile_position=(p0, p0))
                        mm = t.matmul(PX[p0:p0 + 64, sl:sl + 128],
                                      NTs[p0:p0 + 64, (h // 2) * 64:(h // 2) * 64 + 64],
                                      X_sb[p0:p0 + 64, sl:sl + 128],
                                      start=False, stop=True, tile_position=(p0, p0))
                    st_.inc(mm)
                ap_mm = st_.n

                if lvl < 5:
                    @block.tensor
                    def _(t, i=i, nb=nb, lvl=lvl, cur=cur, nxt=nxt, sa_now=None):
                        if lvl >= 1:
                            t.wait_ge(sa.h, sq_evac[0])
                        else:
                            t.wait_ge(sv.h, maskprod_done[0])
                            t.wait_ge(sa.h, sa.n)
                        for h in range(16):
                            p0 = (h % 2) * 64
                            Ns = Nsb if lvl == 0 else Npw[cur]
                            NTs = NTsb if lvl == 0 else NTpw[cur]
                            sl = (h // 2) * 64
                            t.matmul(PS[6][p0:p0 + 64, sl:sl + 64],
                                     NTs[p0:p0 + 64, sl:sl + 64], Ns[p0:p0 + 64, sl:sl + 64],
                                     start=True, stop=True, tile_position=(p0, p0))
                            mm = t.matmul(PS[7][p0:p0 + 64, sl:sl + 64],
                                          Ns[p0:p0 + 64, sl:sl + 64], NTs[p0:p0 + 64, sl:sl + 64],
                                          start=True, stop=True, tile_position=(p0, p0))
                        st_.inc(mm)
                    sq_mm = st_.n

                    @block.scalar
                    def _(sc, lvl=lvl, nxt=nxt, sq_mm=sq_mm):
                        sc.wait_ge(st_.h, sq_mm)
                        sa.inc(sc.copy(Npw[nxt][:, :], PS[6][:, :]))
                        sa.inc(sc.copy(NTpw[nxt][:, :], PS[7][:, :]))
                    if i == 0 and nb == 0 and lvl == 0:
                        sq_evac = {0: 0}
                    sq_evac[0] = sa.n

                @block.vector
                def _(v, ap_mm=ap_mm):
                    v.wait_ge(st_.h, ap_mm)
                    v.tensor_copy(X_sb[:, 0:512], PS[4][:, :])
                    sv.inc(v.tensor_copy(X_sb[:, 512:1024], PS[5][:, :]))
                x0_evac[0] = sv.n
            x_done = x0_evac[0]

            if stage_limit < 7:
                continue
            @block.tensor
            def _(t, i=i, nb=nb, bp=bp, x_done=x_done):
                t.wait_ge(sv.h, x_done)
                t.wait_ge(sa.h, sq_evac[0])
                mm = None
                if "pt" in SUB7:
                    for h in range(16):
                        p0 = (h % 2) * 64
                        base = 0 if h < 8 else 512
                        xsl = base + ((h // 2) % 4) * 128
                        mm = t.transpose(PS[6].bitcast(BF16)[p0:p0 + 64, (h // 2) * 64:(h // 2) * 64 + 64],
                                         X_sb[p0:p0 + 64, xsl + 64:xsl + 128],
                                         c.id128_bf[p0:p0 + 64, p0:p0 + 64], tile_position=(p0, p0))
                    st_.inc(mm)
                if "gw" in SUB7:
                    for h in range(16):
                        p0 = (h % 2) * 64
                        base = 0 if h < 8 else 512
                        xsl = base + ((h // 2) % 4) * 128
                        Pg = PS[7] if h % 2 == 0 else PS[3]
                        mm = t.matmul(Pg[bp:bp + 64, (h // 2) * 64:(h // 2) * 64 + 64],
                                      GTsb[p0:p0 + 64, (h // 2) * 64:(h // 2) * 64 + 64],
                                      X_sb[p0:p0 + 64, xsl:xsl + 64],
                                      start=True, stop=True, tile_position=(p0, bp))
                    st_.inc(mm)
                if mm is None:
                    st_.inc(t.matmul(PS[7][0:64, 0:64], GTsb[0:64, 0:64], X_sb[0:64, 0:64],
                                     start=True, stop=True))
            ptgw_mm = st_.n

            @block.vector
            def _(v, i=i, nb=nb, bp=bp, ptgw_mm=ptgw_mm):
                v.wait_ge(st_.h, ptgw_mm)
                sv.inc(v.tensor_copy(pt_r[i % RD][:, nb * 512:(nb + 1) * 512],
                                     PS[6].bitcast(BF16)[:, 0:512]))
                if "gevac" in SUB7:
                    v.wait_ge(sv.h, sv.n)
                    gwr3 = gw_r[i % RD][bp:bp + 64, :].rearrange("p (s two x) -> p s two x", s=8, two=2)
                    sv.inc(v.tensor_copy(gwr3[:, :, 0, :], PS[7][bp:bp + 64, :].rearrange("p (s x) -> p s x", s=8)))
                    sv.inc(v.tensor_copy(gwr3[:, :, 1, :], PS[3][bp:bp + 64, :].rearrange("p (s x) -> p s x", s=8)))
            gw_ev[0] = sv.n

            @block.tensor
            def _(t, i=i, nb=nb, bp=bp, x_done=x_done, gw_evac=None):
                t.wait_ge(sv.h, maskprod_done[0])
                t.wait_ge(sv.h, gw_ev[0])
                if "rt" not in SUB7:
                    st_.inc(t.matmul(PS[3][0:64, 0:64], GTsb[0:64, 0:64], X_sb[0:64, 0:64],
                                     start=True, stop=True))
                else:
                    for h in range(16):
                        p0 = (h % 2) * 64
                        base = 0 if h < 8 else 512
                        xsl = base + ((h // 2) % 4) * 128
                        mm = t.matmul(PS[3][p0:p0 + 64, (h // 2) * 64:(h // 2) * 64 + 64],
                                      X_sb[p0:p0 + 64, xsl + 64:xsl + 128],
                                      Gsb[p0:p0 + 64, (h // 2) * 64:(h // 2) * 64 + 64],
                                      start=True, stop=True, tile_position=(p0, p0))
                    st_.inc(mm)
                    for h in range(16):
                        p0 = (h % 2) * 64
                        mm = t.transpose(PS[6].bitcast(BF16)[p0:p0 + 64, 512 + (h // 2) * 64:512 + (h // 2) * 64 + 64],
                                         LK[bp:bp + 64, h * 64:h * 64 + 64],
                                         c.id128_bf[bp:bp + 64, bp:bp + 64], tile_position=(bp, p0))
                    st_.inc(mm)
            rt_mm = st_.n

            @block.vector
            def _(v, i=i, nb=nb, rt_mm=rt_mm):
                v.wait_ge(st_.h, rt_mm)
                if "rtc" in SUB7:
                    sv.inc(v.tensor_copy(lkt_tmp[:, :], PS[6].bitcast(BF16)[:, 512:1024]))
                    v.wait_ge(sv.h, sv.n)
                    sv.inc(v.scalar_tensor_tensor(rt_r[i % RD][:, nb * 512:(nb + 1) * 512],
                                                  PS[3][:, :], -1.0, lkt_tmp[:, :],
                                                  ALU.mult, ALU.add))
                else:
                    sv.inc(v.memset(rt_r[i % RD][:, nb * 512:(nb + 1) * 512], 0.0))
            rtw_v = sv.n
            rt_ev[0] = rtw_v

            @block.tensor
            def _(t, i=i, nb=nb, bp=bp, x_done=x_done):
                if "wp" not in SUB7:
                    st_.inc(t.matmul(PS[4][0:64, 0:64], c.id128_bf[0:64, 0:64],
                                     X_sb[0:64, 0:64], start=True, stop=True))
                    return
                for h in range(16):
                    p0 = (h % 2) * 64
                    base = 0 if h < 8 else 512
                    xsl = base + ((h // 2) % 4) * 128
                    Pw = PS[4] if h % 2 == 0 else PS[5]
                    mm = t.matmul(Pw[bp:bp + 64, (h // 2) * 64:(h // 2) * 64 + 64],
                                  c.id128_bf[p0:p0 + 64, p0:p0 + 64],
                                  X_sb[p0:p0 + 64, xsl:xsl + 64],
                                  start=True, stop=True, tile_position=(p0, bp))
                st_.inc(mm)
            w_mm = st_.n

            @block.vector
            def _(v, i=i, nb=nb, bp=bp, w_mm=w_mm):
                v.wait_ge(st_.h, w_mm)
                if "wp" in SUB7:
                    wr3 = wn_r[i % RD][bp:bp + 64, :].rearrange("p (s two x) -> p s two x", s=8, two=2)
                    sv.inc(v.tensor_copy(wr3[:, :, 0, :], PS[4][bp:bp + 64, :].rearrange("p (s x) -> p s x", s=8)))
                    sv.inc(v.tensor_copy(wr3[:, :, 1, :], PS[5][bp:bp + 64, :].rearrange("p (s x) -> p s x", s=8)))
                else:
                    sv.inc(v.memset(wn_r[i % RD][bp:bp + 64, :], 0.0))
            wr_ev[0] = sv.n


        ring_ok[i] = sems["a"].n
        ring_sv[i] = sems["v"].n

    if n_tiles >= 1 and stage_limit >= 9:
        emit_scan(n_tiles - 1)

    @block.sync
    def _(sy):
        for s3 in so3:
            sy.wait_ge(s3.h, s3.n)


def emit_phase_barrier(nc, block, pbar, k, so_list):
    target = 5 * (k + 1)
    for engname in ("tensor", "vector", "scalar", "gpsimd", "sync"):
        @getattr(block, engname)
        def _(e):
            for so in so_list:
                e.wait_ge(so.h, so.n)
            e.sem_inc(pbar.h, 1)
            e.wait_ge(pbar.h, target)
    pbar.n = target


def build_fused(debug_outs=False):
    import contextlib
    nc = bass.Bass("TRN2", target_bir_lowering=False)
    x_in = nc.dram_tensor("x_in", [4096, 1024], F32, kind="ExternalInput").ap()
    wqkv = nc.dram_tensor("wqkv", [3072, 1024], F32, kind="ExternalInput").ap()
    wout = nc.dram_tensor("wout", [1024, 1024], F32, kind="ExternalInput").ap()
    cent = nc.dram_tensor("cent", [64, 1024], F32, kind="ExternalInput").ap()
    eb = nc.dram_tensor("eb", [1, 64], F32, kind="ExternalInput").ap()
    wsu = nc.dram_tensor("wsu", [1024, 512], F32, kind="ExternalInput")
    wsd = nc.dram_tensor("wsd", [512, 1024], F32, kind="ExternalInput")
    wu_all = nc.dram_tensor("wu_all", [64, 1024, 512], F32, kind="ExternalInput")
    wd_all = nc.dram_tensor("wd_all", [64, 512, 1024], F32, kind="ExternalInput")
    w1 = nc.dram_tensor("w1", [1024, 1024], F32, kind="ExternalInput").ap()
    gam = nc.dram_tensor("gam", [1, 1024], F32, kind="ExternalInput").ap()
    bet = nc.dram_tensor("bet", [1, 1024], F32, kind="ExternalInput").ap()
    eta = nc.dram_tensor("eta", [1, 1], F32, kind="ExternalInput").ap()
    laml = nc.dram_tensor("laml", [1, 1], F32, kind="ExternalInput").ap()
    w2 = nc.dram_tensor("w2", [1024, 1024], F32, kind="ExternalInput").ap()
    b2 = nc.dram_tensor("b2", [1, 1024], F32, kind="ExternalInput").ap()
    y_out = nc.dram_tensor("y_out", [4096, 1024], F32, kind="ExternalOutput").ap()
    if debug_outs:
        xs1 = nc.dram_tensor("xs1", [4096, 1024], F32, kind="ExternalOutput").ap()
        xs2 = nc.dram_tensor("xs2", [4096, 1024], F32, kind="ExternalOutput").ap()
        xs3 = nc.dram_tensor("xs3", [4096, 1024], F32, kind="ExternalOutput").ap()
    else:
        xs1 = nc.dram_tensor("xs1", [4096, 1024], F32).ap()
        xs2 = nc.dram_tensor("xs2", [4096, 1024], F32).ap()
        xs3 = nc.dram_tensor("xs3", [4096, 1024], F32).ap()

    c = setup_common(nc)
    ARENA = 188 * 1024
    r = nc.bump_sbuf(ARENA)
    base = r[0] if isinstance(r, tuple) else r
    arena = Arena(nc, base, base + ARENA)

    def wu_e(e):
        return bass.AP(wu_all, e * 1024 * 512, [[512, 128], [128 * 512, 8], [1, 512]])

    def wd_e(e):
        return bass.AP(wd_all, e * 512 * 1024, [[1024, 128], [128 * 1024, 4], [1, 1024]])

    with contextlib.ExitStack() as es:
        pbar = Sem(es.enter_context(nc.semaphore("pbar")))
        sA = make_sems(nc, es, "pa_")
        sB = make_sems(nc, es, "pb_")
        sC = make_sems(nc, es, "pc_")
        sD = make_sems(nc, es, "pd_")
        with nc.Block() as block:
            arena.reset()
            with nc.named_scope("phA"):
                emit_phaseA(nc, block, c, {"x_in": x_in, "wqkv": wqkv, "wout": wout,
                                           "y_out": xs1}, sA, arena=arena)
                emit_phase_barrier(nc, block, pbar, 0, [sA["s0"], sA["s1"]])
            arena.reset()
            with nc.named_scope("phB"):
                emit_phaseB(nc, block, c, {
                    "x_in": xs1, "cent": cent, "eb": eb,
                    "wsu": bass.AP(wsu, 0, [[512, 128], [128 * 512, 8], [1, 512]]),
                    "wsd": bass.AP(wsd, 0, [[1024, 128], [128 * 1024, 4], [1, 1024]]),
                    "wu_e": wu_e, "wd_e": wd_e, "y_out": xs2}, sB, arena=arena)
                emit_phase_barrier(nc, block, pbar, 1, [sB["st"]])
            arena.reset()
            with nc.named_scope("phC"):
                emit_phaseC(nc, block, c, {"x_in": xs2, "w_in": w1, "gam": gam,
                                           "bet": bet, "eta": eta, "laml": laml,
                                           "y_out": xs3}, sC, arena=arena)
                emit_phase_barrier(nc, block, pbar, 2, [sC["s0"], sC["s1"], sC["s2"]])
            arena.reset()
            with nc.named_scope("phD"):
                emit_phaseD(nc, block, c, {"x_in": xs3, "w_in": w2, "b_in": b2,
                                           "y_out": y_out}, sD, arena=arena)
    return nc


from concourse.bass_utils import run_bass_kernel_spmd

N_CORES = 8


def kernel(L0_attn_x, L0_attn_W_qkv, L0_attn_W_out, L0_ffn_centroids,
           L0_ffn_expert_biases, L0_ffn_Ws_up, L0_ffn_Ws_down, L0_ffn_W_up,
           L0_ffn_W_down, L1_attn_W, L1_attn_ln_gamma, L1_attn_ln_beta,
           L1_attn_eta, L1_attn_lam_logit, L1_ffn_weight, L1_ffn_bias):
    cast = lambda a: np.ascontiguousarray(np.asarray(a, dtype=np.float32))
    x = cast(L0_attn_x)
    nc = build_fused()
    res = run_bass_kernel_spmd(nc, [{
        "x_in": x[b],
        "wqkv": cast(L0_attn_W_qkv), "wout": cast(L0_attn_W_out),
        "cent": cast(L0_ffn_centroids),
        "eb": cast(L0_ffn_expert_biases).reshape(1, 64),
        "wsu": cast(L0_ffn_Ws_up), "wsd": cast(L0_ffn_Ws_down),
        "wu_all": cast(L0_ffn_W_up), "wd_all": cast(L0_ffn_W_down),
        "w1": cast(L1_attn_W),
        "gam": cast(L1_attn_ln_gamma).reshape(1, 1024),
        "bet": cast(L1_attn_ln_beta).reshape(1, 1024),
        "eta": cast(L1_attn_eta).reshape(1, 1),
        "laml": cast(L1_attn_lam_logit).reshape(1, 1),
        "w2": cast(L1_ffn_weight), "b2": cast(L1_ffn_bias).reshape(1, 1024)}
        for b in range(N_CORES)], core_ids=list(range(N_CORES)))
    return np.stack([np.asarray(res.results[b]["y_out"])
                     for b in range(N_CORES)], axis=0)


# version 07 (final)
# speedup vs baseline: 1.1935x; 1.1935x over previous
import numpy as np
import concourse.bass as bass
import concourse.mybir as mybir

F32 = mybir.dt.float32
BF16 = mybir.dt.bfloat16
I32 = mybir.dt.int32
AF = mybir.ActivationFunctionType
ALU = mybir.AluOpType
AX = mybir.AxisListType

NT = 32
SUB7 = {"pt", "gw", "rt", "evac", "wp", "gevac", "rtc"}
SUB9 = {"u", "o", "d", "supd", "y", "ln", "st", "skel", "ue"}
DT = 1024
TOK = NT * 128


class Sem:
    def __init__(self, h):
        self.h = h
        self.n = 0

    def inc(self, instr, k=1):
        instr.then_inc(self.h, k)
        self.n += k
        return self.n


CONST_VALUES = [1e-5, 1e-30, 1.0 / 7.5, 0.0, 1.0, -1.0, 0.5, 1e-6, 1e-8]


class Common:
    pass


def seq_dma(eng, sem, out, in_):
    eng.wait_ge(sem.h, sem.n)
    instr = eng.dma_start(out, in_)
    instr.then_inc(sem.h, 16)
    sem.n += 16
    return instr



class Arena:
    def __init__(self, nc, base, limit):
        self.nc = nc
        self.base = base
        self.off = base
        self.limit = limit

    def reset(self):
        self.off = self.base

    def __call__(self, name, shape, dtype):
        isz = {F32: 4, BF16: 2, I32: 4}[dtype]
        off = (self.off + 31) & ~31
        t = self.nc.alloc_sbuf_tensor_at(name, shape, dtype, offset=off)
        self.off = off + shape[-1] * isz
        assert self.off <= self.limit, (name, self.off, self.limit)
        return t


def setup_common(nc):
    import ml_dtypes
    c = Common()
    c.id128_bf = nc.alloc_sbuf_tensor("c_id128b", [128, 128], BF16)
    c.id128_f = nc.alloc_sbuf_tensor("c_id128f", [128, 128], F32)
    c.ones_f = nc.alloc_sbuf_tensor("c_onesf", [1, 128], F32)
    c.ones_cb = nc.alloc_sbuf_tensor("c_onescb", [128, 1], BF16)
    c.consts = nc.alloc_sbuf_tensor("c_consts", [128, len(CONST_VALUES)], F32)
    idn = np.eye(128, dtype=np.float32)
    c.d_idb = nc.inline_tensor(idn.astype(ml_dtypes.bfloat16), "d_idb")
    c.d_idf = nc.inline_tensor(idn, "d_idf")
    c.d_onesf = nc.inline_tensor(np.ones((1, 128), np.float32), "d_onesf")
    c.d_onescb = nc.inline_tensor(np.ones((128, 1), np.float32).astype(ml_dtypes.bfloat16), "d_onescb")
    c.d_consts = nc.inline_tensor(np.array([CONST_VALUES], np.float32), "d_consts")
    for i, v in enumerate(CONST_VALUES):
        nc.const_aps.register(F32, v, c.consts[:, i:i + 1])
    c.PS = [nc.alloc_psum_tensor(f"cps{j}", [128, 512], F32) for j in range(8)]
    return c


def dram_ap(handle, shape):
    r, co = shape
    return bass.AP(handle, 0, [[co, r], [1, 1], [1, co]])


def dram_row_bcast(handle, cols, parts=128):
    return bass.AP(handle, 0, [[0, parts], [1, 1], [1, cols]])


def emit_common_load(c, sy, ld):
    seq_dma(sy, ld, c.id128_bf[:, :], dram_ap(c.d_idb, (128, 128)))
    seq_dma(sy, ld, c.id128_f[:, :], dram_ap(c.d_idf, (128, 128)))
    seq_dma(sy, ld, c.ones_f[:, :], dram_ap(c.d_onesf, (1, 128)))
    seq_dma(sy, ld, c.ones_cb[:, :], dram_ap(c.d_onescb, (128, 1)))
    seq_dma(sy, ld, c.consts[:, :], dram_row_bcast(c.d_consts, len(CONST_VALUES)))


def emit_ln_tile(nc, block, c, v_tile, scr, sv, sa, pre_v,
                 gamma_b=None, beta_b=None, out_tile=None, eps=1e-5):
    out_tile = v_tile if out_tile is None else out_tile
    sqs, ssq, s1 = scr["sqs"], scr["ssq"], scr["s1"]
    mean, var, rstd, nb = scr["mean"], scr["var"], scr["rstd"], scr["nb"]

    @block.scalar
    def _(sc):
        sc.wait_ge(sv.h, pre_v)
        sa.inc(sc.activation(sqs, v_tile, AF.Square, accum_out=ssq))
    ssq_done = sa.n

    @block.vector
    def _(v):
        v.wait_ge(sv.h, pre_v)
        sv.inc(v.reduce_sum(s1, v_tile, axis=AX.X))
        v.wait_ge(sv.h, sv.n)
        sv.inc(v.tensor_scalar_mul(mean, s1, 1.0 / DT))
        v.wait_ge(sv.h, sv.n)
        sv.inc(v.tensor_tensor(var, mean, mean, ALU.mult))
        v.wait_ge(sa.h, ssq_done)
        v.wait_ge(sv.h, sv.n)
        sv.inc(v.scalar_tensor_tensor(var, ssq, 1.0 / DT, var, ALU.mult, ALU.subtract))
    var_done = sv.n

    @block.scalar
    def _(sc):
        sc.wait_ge(sv.h, var_done)
        sa.inc(sc.activation(var, var, AF.Sqrt, bias=eps))
    std_done = sa.n

    @block.vector
    def _(v):
        v.wait_ge(sa.h, std_done)
        sv.inc(v.reciprocal(rstd, var))
        v.wait_ge(sv.h, sv.n)
        sv.inc(v.tensor_tensor(nb, mean, rstd, ALU.mult))
        v.wait_ge(sv.h, sv.n)
        sv.inc(v.tensor_scalar_mul(nb, nb, -1.0))
        v.wait_ge(sv.h, sv.n)
        sv.inc(v.tensor_scalar(out_tile, v_tile, rstd, nb, ALU.mult, ALU.add))
        if gamma_b is not None:
            v.wait_ge(sv.h, sv.n)
            sv.inc(v.tensor_tensor(out_tile, out_tile, gamma_b, ALU.mult))
            v.wait_ge(sv.h, sv.n)
            sv.inc(v.tensor_tensor(out_tile, out_tile, beta_b, ALU.add))
    return sv.n


def alloc_ln_scratch(A, parts=128):
    return {
        "sqs": A("ln_sqs", [parts, DT], F32)[:, :],
        "ssq": A("ln_ssq", [parts, 1], F32)[:, :],
        "s1": A("ln_s1", [parts, 1], F32)[:, :],
        "mean": A("ln_mean", [parts, 1], F32)[:, :],
        "var": A("ln_var", [parts, 1], F32)[:, :],
        "rstd": A("ln_rstd", [parts, 1], F32)[:, :],
        "nb": A("ln_nb", [parts, 1], F32)[:, :],
    }



def emit_phaseD(nc, block, c, io, sems, pref="pd_", arena=None):
    so2 = [sems["s0"], sems["s1"]]
    """io: x_in [4096,1024] f32 dram AP, w_in [1024,1024], b_in [1,1024],
    y_out [4096,1024]. sems: dict ld, v, a, t, st (fresh counters ok)."""
    A = (lambda n, sh, dt: arena(pref + n, sh, dt)) if arena is not None \
        else (lambda n, sh, dt: nc.alloc_sbuf_tensor(pref + n, sh, dt))

    xt = [A(f"xt{j}", [128, DT], F32) for j in range(2)]
    wtf = A("wtf", [128, DT], F32)
    wd = A("wd", [128, DT], BF16)
    wdT = A("wdT", [128, 8 * DT], BF16)
    wdT3 = wdT.rearrange("p (k o) -> p k o", k=8)
    bias_b = A("biasb", [128, DT], F32)
    bias_r = A("biasr", [1, DT], F32)
    qs = []
    for j in range(2):
        qs.append({nm: A(f"{nm}{j}", [128, 32] if nm in ("am", "scb", "iscb")
                         else [128, DT], I32 if nm == "ti" else F32)
                   for nm in ("am", "scb", "iscb", "uu", "au", "ge2", "ge4",
                              "invs", "ss", "tt", "ti", "tf", "sbl")})
    xd = [A(f"xd{j}", [128, DT], BF16) for j in range(2)]
    xdf = [A(f"xdf{j}", [128, DT], BF16) for j in range(2)]
    ot = [A(f"ot{j}", [128, DT], F32) for j in range(2)]
    lns = alloc_ln_scratch(A)

    ps_tr = [c.PS[0].bitcast(BF16), c.PS[1][:, :].bitcast(BF16)]
    ps_y = [c.PS[2 + j][:, :] for j in range(4)]

    ld, sv, sa, st_, so = sems["ld"], sems["v"], sems["a"], sems["t"], sems["st"]

    @block.sync
    def _(sy):
        emit_common_load(c, sy, ld)
        seq_dma(sy, ld, bias_r[:, :], io["b_in"])
    ld_common = ld.n

    @block.tensor
    def _(t):
        t.wait_ge(ld.h, ld_common)
        t.matmul(ps_y[0][:, :], c.ones_f[0:1, 0:128], bias_r[0:1, 0:512],
                 start=True, stop=True)
        st_.inc(t.matmul(ps_y[1][:, :], c.ones_f[0:1, 0:128], bias_r[0:1, 512:1024],
                         start=True, stop=True))
    bias_mm = st_.n

    @block.vector
    def _(v):
        v.wait_ge(st_.h, bias_mm)
        sv.inc(v.tensor_copy(bias_b[:, 0:512], ps_y[0][:, :]))
        sv.inc(v.tensor_copy(bias_b[:, 512:1024], ps_y[1][:, :]))
    bias_done = sv.n

    def quant_chain(q, src, dst, pre, dst_guard=None):
        am, scb, iscb = q["am"], q["scb"], q["iscb"]
        uu, au, ge2, ge4 = q["uu"], q["au"], q["ge2"], q["ge4"]
        invs, ss, tt_, ti, tf, sbl = (q["invs"], q["ss"], q["tt"], q["ti"],
                                      q["tf"], q["sbl"])

        @block.vector
        def _(v):
            for (sh, n_) in pre:
                v.wait_ge(sh.h, n_)
            sv.inc(v.reduce_max(am[:, :], src.rearrange("p (g x) -> p g x", g=32),
                                axis=AX.X, apply_absolute_value=True))
            v.wait_ge(sv.h, sv.n)
            sv.inc(v.tensor_scalar(scb[:, :], am[:, :], 1.0 / 7.5, 1e-30,
                                   ALU.mult, ALU.max))
            v.wait_ge(sv.h, sv.n)
            sv.inc(v.reciprocal(iscb[:, :], scb[:, :]))
            v.wait_ge(sv.h, sv.n)
            sv.inc(v.tensor_tensor(
                uu.rearrange("p (g x) -> p g x", g=32), src.rearrange("p (g x) -> p g x", g=32),
                iscb.rearrange("p (g o) -> p g o", o=1).to_broadcast([128, 32, 32]),
                ALU.mult))
        src_free = sv.n
        uu_done = sv.n

        @block.scalar
        def _(sc):
            sc.wait_ge(sv.h, uu_done)
            sa.inc(sc.activation(au[:, :], uu[:, :], AF.Abs))
        au_done = sa.n

        @block.vector
        def _(v):
            v.wait_ge(sa.h, au_done)
            sv.inc(v.tensor_single_scalar(ge2[:, :], au[:, :], 2.0, ALU.is_ge))
            sv.inc(v.tensor_single_scalar(ge4[:, :], au[:, :], 4.0, ALU.is_ge))
            v.wait_ge(sv.h, sv.n)
            sv.inc(v.tensor_scalar(invs[:, :], ge2[:, :], -4.0, 8.0, ALU.mult, ALU.add))
            sv.inc(v.tensor_scalar(ss[:, :], ge2[:, :], 0.125, 0.125, ALU.mult, ALU.add))
            v.wait_ge(sv.h, sv.n)
            sv.inc(v.scalar_tensor_tensor(invs[:, :], ge4[:, :], -2.0, invs[:, :],
                                          ALU.mult, ALU.add))
            sv.inc(v.scalar_tensor_tensor(ss[:, :], ge4[:, :], 0.25, ss[:, :],
                                          ALU.mult, ALU.add))
            v.wait_ge(sv.h, sv.n)
            sv.inc(v.tensor_tensor(tt_[:, :], uu[:, :], invs[:, :], ALU.mult))
        t_done = sv.n

        @block.scalar
        def _(sc):
            sc.wait_ge(sv.h, t_done)
            sa.inc(sc.copy(ti[:, :], tt_[:, :]))
            sc.wait_ge(sa.h, sa.n)
            sa.inc(sc.copy(tf[:, :], ti[:, :]))
        tf_done = sa.n

        @block.vector
        def _(v):
            sv.inc(v.tensor_tensor(
                sbl.rearrange("p (g x) -> p g x", g=32), ss.rearrange("p (g x) -> p g x", g=32),
                scb.rearrange("p (g o) -> p g o", o=1).to_broadcast([128, 32, 32]),
                ALU.mult))
            v.wait_ge(sa.h, tf_done)
            v.wait_ge(sv.h, sv.n)
            sv.inc(v.tensor_tensor(au[:, :], tt_[:, :], tf[:, :], ALU.subtract))
            v.wait_ge(sv.h, sv.n)
            sv.inc(v.scalar_tensor_tensor(tf[:, :], au[:, :], 0.5, tf[:, :],
                                          ALU.is_gt, ALU.add))
            sv.inc(v.tensor_single_scalar(ge2[:, :], au[:, :], -0.5, ALU.is_lt))
            v.wait_ge(sv.h, sv.n)
            sv.inc(v.tensor_tensor(tf[:, :], tf[:, :], ge2[:, :], ALU.subtract))
            v.wait_ge(sv.h, sv.n)
            if dst_guard is not None:
                v.wait_ge(dst_guard[0].h, dst_guard[1])
            sv.inc(v.tensor_tensor(dst, tf[:, :], sbl[:, :], ALU.mult))
        return src_free, sv.n

    w_free = {}
    wdT_evac = {}
    last_tr = [st_.n]
    for r in range(8):
        @block.sync
        def _(sy, r=r):
            if r >= 1:
                sy.wait_ge(sv.h, w_free[r - 1])
            seq_dma(sy, ld, wtf[:, :], io["w_in"][r * 128:(r + 1) * 128, :])
        w_free[r], wq_end = quant_chain(qs[r % 2], wtf[:, :], wd[:, :], [(ld, ld.n)],
                                        dst_guard=(st_, last_tr[0]))

        @block.tensor
        def _(t, r=r, wq_end=wq_end):
            t.wait_ge(sv.h, wq_end)
            if r >= 2:
                t.wait_ge(sv.h, wdT_evac[r - 2])
            for kk in range(8):
                mm = t.transpose(ps_tr[r % 2][:, kk * 128:(kk + 1) * 128],
                                 wd[:, kk * 128:(kk + 1) * 128], c.id128_bf[:, :])
            st_.inc(mm)
        tr_n = st_.n
        last_tr[0] = tr_n

        @block.vector
        def _(v, r=r, tr_n=tr_n):
            v.wait_ge(st_.h, tr_n)
            sv.inc(v.tensor_copy(wdT3[:, :, r * 128:(r + 1) * 128],
                                 ps_tr[r % 2].rearrange("p (k o) -> p k o", k=8)))
        wdT_evac[r] = sv.n
    wdT_done = sv.n

    x_free = {}
    xdf_evac = {}
    y_free = {}
    store_done = {}
    xq_done = {}
    tr_done = {}

    def emit_s1(i):
        @block.sync
        def _(sy, i=i):
            if i >= 2:
                sy.wait_ge(sv.h, x_free[i - 2])
                sy.wait_ge(so2[i % 2].h, store_done[i - 2])
            seq_dma(sy, ld, xt[i % 2][:, :], io["x_in"][i * 128:(i + 1) * 128, :])
        x_ld = ld.n
        dg = (st_, tr_done[i - 2]) if i >= 2 else (st_, last_tr[0])
        _, xq = quant_chain(qs[i % 2], xt[i % 2][:, :], xd[i % 2][:, :],
                            [(ld, x_ld)], dst_guard=dg)
        xq_done[i] = xq

    def emit_s2(i):
        @block.tensor
        def _(t, i=i):
            t.wait_ge(sv.h, xq_done[i])
            if i < 2:
                t.wait_ge(sv.h, wdT_done)
            else:
                t.wait_ge(sv.h, xdf_evac[i - 2])
            for kk in range(8):
                mm = t.transpose(ps_tr[i % 2][:, kk * 128:(kk + 1) * 128],
                                 xd[i % 2][:, kk * 128:(kk + 1) * 128], c.id128_bf[:, :])
            st_.inc(mm)
        tr_done[i] = st_.n

        @block.vector
        def _(v, i=i):
            v.wait_ge(st_.h, tr_done[i])
            sv.inc(v.tensor_copy(xdf[i % 2][:, :], ps_tr[i % 2][:, :]))
        xdf_evac[i] = sv.n

        @block.tensor
        def _(t, i=i):
            t.wait_ge(sv.h, xdf_evac[i])
            if i == 0:
                t.wait_ge(sv.h, bias_done)
            if i >= 2:
                t.wait_ge(sv.h, y_free[i - 2])
            for half in range(2):
                for kk in range(8):
                    mm = t.matmul(ps_y[2 * (i % 2) + half][:, :],
                                  xdf[i % 2][:, kk * 128:(kk + 1) * 128],
                                  wdT3[:, kk, half * 512:(half + 1) * 512],
                                  start=(kk == 0), stop=(kk == 7))
            st_.inc(mm)
        mm_done = st_.n

        @block.vector
        def _(v, i=i, mm_done=mm_done):
            v.wait_ge(st_.h, mm_done)
            if i >= 2:
                v.wait_ge(so2[i % 2].h, store_done[i - 2])
            sv.inc(v.tensor_tensor(ot[i % 2][:, 0:512], ps_y[2 * (i % 2)][:, :],
                                   bias_b[:, 0:512], ALU.add))
            sv.inc(v.tensor_tensor(ot[i % 2][:, 512:1024], ps_y[2 * (i % 2) + 1][:, :],
                                   bias_b[:, 512:1024], ALU.add))
            v.wait_ge(sv.h, sv.n)
            sv.inc(v.tensor_tensor(ot[i % 2][:, :], ot[i % 2][:, :], xt[i % 2][:, :],
                                   ALU.add))
        y_free[i] = sv.n
        x_free[i] = sv.n
        ot_done = sv.n

        ln_done = emit_ln_tile(nc, block, c, ot[i % 2][:, :], lns, sv, sa, ot_done)

        @block.sync
        def _(sy, i=i, ln_done=ln_done):
            sy.wait_ge(sv.h, ln_done)
            seq_dma(sy, so2[i % 2], io["y_out"][i * 128:(i + 1) * 128, :], ot[i % 2][:, :])
        store_done[i] = so2[i % 2].n

    emit_s1(0)
    for i in range(1, NT):
        emit_s1(i)
        emit_s2(i - 1)
    emit_s2(NT - 1)

    @block.sync
    def _(sy):
        sy.wait_ge(so2[0].h, so2[0].n)
        sy.wait_ge(so2[1].h, so2[1].n)
    return


def make_sems(nc, es, pref, names=("ld", "v", "a", "t", "st", "wld", "wc", "s0", "s1", "s2")):
    return {nm: Sem(es.enter_context(nc.semaphore(pref + nm))) for nm in names}



def emit_phaseB(nc, block, c, io, sems, pref="pb_", n_experts=64, nbuf=2, arena=None):
    A = (lambda n, sh, dt: arena(pref + n, sh, dt)) if arena is not None \
        else (lambda n, sh, dt: nc.alloc_sbuf_tensor(pref + n, sh, dt))

    HT = 16
    acc = [A(f"acc{j}", [128, DT], F32) for j in range(HT)]
    x_fm = A("xfm", [128, 8 * 2048], BF16)
    x_fm3 = x_fm.rearrange("p (k t) -> p k t", k=8)
    xb = A("xb", [128, DT], BF16)
    cw = [A(f"cw{j}", [128, 64], F32) for j in range(HT)]
    centT = A("centT", [128, 8 * 64], F32)
    centT3 = centT.rearrange("p (k e) -> p k e", k=8)
    xf32 = A("xf32", [128, DT], F32)
    cent_s = xf32
    ebb = A("ebb", [128, 64], F32)
    eb_r = A("eb_r", [1, 64], F32)
    aff = A("aff", [128, 64], F32)
    biased = A("biased", [128, 64], F32)
    max8 = A("max8", [128, 8], F32)
    rep = A("rep", [128, 64], F32)
    mask = A("mask", [128, 64], F32)
    sig = A("sig", [128, 64], F32)
    ssel = A("ssel", [128, 64], F32)
    nrm = A("nrm", [128, 1], F32)
    wub = [A(f"wub{j}", [128, 8 * 512], BF16) for j in range(nbuf)]
    wdb = [A(f"wdb{j}", [128, 4 * DT], BF16) for j in range(nbuf)]
    wstg_u = A("wstgu", [128, 8 * 512], F32)
    wstg_d = A("wstgd", [128, 4 * DT], F32)
    h_s = A("hs", [128, 4 * 512], BF16)
    lns = alloc_ln_scratch(A)

    ps_h = [c.PS[j][:, :] for j in range(4)]
    ps_y = [c.PS[4 + j][:, :] for j in range(4)]
    ps_trv = ps_h[0].bitcast(BF16)
    ps_aff = ps_h[1]

    ld, sv, sa, st_, so = sems["ld"], sems["v"], sems["a"], sems["t"], sems["st"]
    wld = sems["wld"]
    wcast = sems["wc"]

    NE = n_experts

    @block.sync
    def _(sy):
        emit_common_load(c, sy, ld)
        seq_dma(sy, ld, eb_r[:, :], io["eb"])
        seq_dma(sy, ld, cent_s[0:64, :], io["cent"])
    setup_ld = ld.n

    @block.tensor
    def _(t):
        t.wait_ge(ld.h, setup_ld)
        for kk in range(8):
            mm = t.transpose(ps_h[2][:, kk * 64:(kk + 1) * 64],
                             cent_s[0:64, kk * 128:(kk + 1) * 128], c.id128_f[0:64, 0:64])
        st_.inc(mm)
        st_.inc(t.matmul(ps_aff[:, 0:64], c.ones_f[0:1, 0:128], eb_r[0:1, 0:64],
                         start=True, stop=True))
    setup_mm = st_.n

    @block.vector
    def _(v):
        v.wait_ge(st_.h, setup_mm)
        sv.inc(v.tensor_copy(centT[:, :], ps_h[2][:, :]))
        v.wait_ge(sv.h, sv.n)
        sv.inc(v.tensor_copy(ebb[:, :], ps_aff[:, 0:64]))
    setup_done = sv.n

    last_up = [st_.n]
    store_done = {}
    dn_tile = {}
    te_done_c = {}
    xf32_evac = {0: 0}

    for half in range(2):
        base_tile = half * HT
        tile_ld = {}
        xfm_done = {}
        cw_done = {}
        for i in range(HT):
            gi = base_tile + i
            @block.sync
            def _(sy, i=i, gi=gi, half=half, so_all=None):
                if half == 1:
                    sy.wait_ge(so.h, half0_stores)
                seq_dma(sy, ld, acc[i][:, :], io["x_in"][gi * 128:(gi + 1) * 128, :])
            tile_ld[i] = ld.n

            @block.scalar
            def _(sc, i=i):
                sc.wait_ge(ld.h, tile_ld[i])
                if i == 0 and half == 0:
                    sc.wait_ge(sv.h, setup_done)
                sa.inc(sc.copy(xb[:, :], acc[i][:, :]))
            xb_done = sa.n

            @block.tensor
            def _(t, i=i, xb_done=xb_done, half=half):
                sc_wait = xb_done
                t.wait_ge(sa.h, sc_wait)
                if i >= 1:
                    t.wait_ge(sv.h, xfm_done[i - 1])
                if half == 1 and i == 0:
                    t.wait_ge(sv.h, half0_acc_done)
                for kk in range(8):
                    mm = t.transpose(ps_trv[:, kk * 128:(kk + 1) * 128],
                                     xb[:, kk * 128:(kk + 1) * 128], c.id128_bf[:, :])
                st_.inc(mm)
            tr_done = st_.n

            @block.vector
            def _(v, i=i, tr_done=tr_done):
                v.wait_ge(st_.h, tr_done)
                sv.inc(v.tensor_copy(x_fm3[:, :, i * 128:(i + 1) * 128],
                                     ps_trv.rearrange("p (k t) -> p k t", k=8)))
            xfm_done[i] = sv.n

            @block.tensor
            def _(t, i=i):
                t.wait_ge(sv.h, xfm_done[i])
                if i >= 1 or half == 1:
                    t.wait_ge(sv.h, xf32_evac[0])
                if i == 0 and half == 0:
                    t.wait_ge(sv.h, setup_done)
                for kk in range(8):
                    pb = ps_h[2] if kk < 4 else ps_h[3]
                    mm = t.transpose(pb[:, (kk % 4) * 128:(kk % 4 + 1) * 128],
                                     acc[i][:, kk * 128:(kk + 1) * 128], c.id128_f[:, :])
                st_.inc(mm)
            xtr_mm = st_.n

            @block.vector
            def _(v, i=i, xtr_mm=xtr_mm):
                v.wait_ge(st_.h, xtr_mm)
                sv.inc(v.tensor_copy(xf32[:, 0:512], ps_h[2][:, :]))
                sv.inc(v.tensor_copy(xf32[:, 512:1024], ps_h[3][:, :]))
                xf32_evac[0] = sv.n
            xf32_done = sv.n

            @block.tensor
            def _(t, i=i, xf32_done=xf32_done):
                t.wait_ge(sv.h, xf32_done)
                if i >= 1:
                    t.wait_ge(sv.h, aff_evac[i - 1])
                for kk in range(8):
                    mm = t.matmul(ps_aff[:, 0:64], xf32[:, kk * 128:(kk + 1) * 128],
                                  centT3[:, kk, :], start=(kk == 0), stop=(kk == 7))
                st_.inc(mm)
            aff_mm = st_.n

            @block.vector
            def _(v, i=i, aff_mm=aff_mm):
                v.wait_ge(st_.h, aff_mm)
                sv.inc(v.tensor_copy(aff[:, :], ps_aff[:, 0:64]))
            if i == 0 and half == 0:
                aff_evac = {}
            aff_evac[i] = sv.n

            @block.scalar
            def _(sc, i=i):
                sc.wait_ge(sv.h, aff_evac[i])
                sa.inc(sc.activation(sig[:, :], aff[:, :], AF.Sigmoid))
            sig_done = sa.n

            @block.vector
            def _(v, i=i, sig_done=sig_done):
                v.wait_ge(sv.h, aff_evac[i])
                sv.inc(v.tensor_tensor(biased[:, :], aff[:, :], ebb[:, :], ALU.add))
                v.wait_ge(sv.h, sv.n)
                sv.inc(v.max(out=max8[:, :], in_=biased[:, :]))
                v.wait_ge(sv.h, sv.n)
                sv.inc(v.match_replace(out=rep[:, :], in_to_replace=max8[:, :],
                                       in_values=biased[:, :], imm_value=-1e30))
                v.wait_ge(sv.h, sv.n)
                sv.inc(v.tensor_tensor(mask[:, :], biased[:, :], rep[:, :], ALU.not_equal))
                v.wait_ge(sa.h, sig_done)
                v.wait_ge(sv.h, sv.n)
                sv.inc(v.tensor_tensor(ssel[:, :], sig[:, :], mask[:, :], ALU.mult))
                v.wait_ge(sv.h, sv.n)
                sv.inc(v.reduce_sum(nrm[:, :], ssel[:, :], axis=AX.X))
                v.wait_ge(sv.h, sv.n)
                sv.inc(v.tensor_scalar_add(nrm[:, :], nrm[:, :], 1e-8))
                v.wait_ge(sv.h, sv.n)
                sv.inc(v.reciprocal(nrm[:, :], nrm[:, :]))
                v.wait_ge(sv.h, sv.n)
                sv.inc(v.tensor_scalar(cw[i][:, :], ssel[:, :], nrm[:, :], None,
                                       ALU.mult))
            cw_done[i] = sv.n
        routing_done = sv.n

        eld_done = {}
        cast_done = {}
        te_done = {}
        acc_done = {}
        silu_done = {}

        def emit_wload(ei):
            e = ei - 1
            @block.sync
            def _(sy, e=e, ei=ei):
                if ei >= 1:
                    sy.wait_ge(wcast.h, cast_done[ei - 1])
                if e < 0:
                    src_u = io["wsu"]
                    src_d = io["wsd"]
                else:
                    src_u = io["wu_e"](e)
                    src_d = io["wd_e"](e)
                seq_dma(sy, wld, wstg_u.rearrange("p (db f) -> p db f", db=8), src_u)
                seq_dma(sy, wld, wstg_d.rearrange("p (fb d) -> p fb d", fb=4), src_d)
            eld_done[ei] = wld.n

        emit_wload(0)

        @block.scalar
        def _(sc):
            sc.wait_ge(wld.h, eld_done[0])
            sc.copy(wub[0][:, :], wstg_u[:, :])
            wcast.inc(sc.copy(wdb[0][:, :], wstg_d[:, :]))
        cast_done[0] = wcast.n

        for ei in range(NE + 1):
            buf = ei % nbuf
            if ei + 1 <= NE:
                emit_wload(ei + 1)
            nbuf_next = (ei + 1) % nbuf

            for cc in range(4):
                up_fb = {}

                @block.tensor
                def _(t, ei=ei, cc=cc, buf=buf, half=half, up_fb=up_fb):
                    if cc == 0:
                        t.wait_ge(wcast.h, cast_done[ei])
                        if ei == 0:
                            t.wait_ge(sv.h, routing_done)
                    key = (ei, cc - 1) if cc > 0 else (ei - 1, 3)
                    if key in silu_done:
                        t.wait_ge(sa.h, silu_done[key])
                    for fb in range(4):
                        for db in range(8):
                            mm = t.matmul(ps_h[fb][:, :],
                                          wub[buf].rearrange("p (db f) -> p db f", db=8)[:, db, fb * 128:(fb + 1) * 128],
                                          x_fm3[:, db, cc * 512:(cc + 1) * 512],
                                          start=(db == 0), stop=(db == 7))
                        st_.inc(mm)
                        up_fb[fb] = st_.n
                    last_up[0] = st_.n
                up_done = st_.n

                @block.scalar
                def _(sc, ei=ei, cc=cc, up_fb=up_fb):
                    key = (ei, cc - 1) if cc > 0 else (ei - 1, 3)
                    if key in te_done_c:
                        sc.wait_ge(st_.h, te_done_c[key])
                    for fb in range(4):
                        sc.wait_ge(st_.h, up_fb[fb])
                        sa.inc(sc.activation(h_s.rearrange("p (fb f) -> p fb f", fb=4)[:, fb, :],
                                             ps_h[fb][:, :], AF.Silu))
                silu_done[(ei, cc)] = sa.n

                if ei + 1 <= NE:
                    @block.scalar
                    def _(sc, ei=ei, cc=cc, nb=nbuf_next):
                        if cc == 0:
                            sc.wait_ge(wld.h, eld_done[ei + 1])
                            if (ei + 1 - nbuf) in te_done:
                                sc.wait_ge(st_.h, te_done[ei + 1 - nbuf])
                        if cc < 2:
                            ci = sc.copy(wub[nb][:, cc * 2048:(cc + 1) * 2048],
                                         wstg_u[:, cc * 2048:(cc + 1) * 2048])
                        else:
                            cd = cc - 2
                            ci = sc.copy(wdb[nb][:, cd * 2048:(cd + 1) * 2048],
                                         wstg_d[:, cd * 2048:(cd + 1) * 2048])
                        if cc == 3:
                            wcast.inc(ci)
                    if cc == 3:
                        cast_done[ei + 1] = wcast.n

                for ti in range(4):
                    gt = cc * 4 + ti
                    par = gt % 2

                    @block.tensor
                    def _(t, ei=ei, cc=cc, buf=buf, ti=ti, gt=gt, par=par):
                        if ti == 0:
                            t.wait_ge(sa.h, silu_done[(ei, cc)])
                        if (ei, gt - 2) in acc_done:
                            t.wait_ge(sv.h, acc_done[(ei, gt - 2)])
                        elif ei > 0 and (ei - 1, gt + 14) in acc_done:
                            t.wait_ge(sv.h, acc_done[(ei - 1, gt + 14)])
                        for haf in range(2):
                            for fb in range(4):
                                mm = t.matmul(ps_y[2 * par + haf][:, :],
                                              h_s.rearrange("p (fb f) -> p fb f", fb=4)[:, fb, ti * 128:(ti + 1) * 128],
                                              wdb[buf].rearrange("p (fb d) -> p fb d", fb=4)[:, fb, haf * 512:(haf + 1) * 512],
                                              start=(fb == 0), stop=(fb == 3))
                        st_.inc(mm)
                        dn_tile[(ei, gt)] = st_.n

                    @block.vector
                    def _(v, ei=ei, gt=gt, par=par):
                        v.wait_ge(st_.h, dn_tile[(ei, gt)])
                        cwcol = c.consts[:, 4:5] if ei == 0 else cw[gt][:, ei - 1:ei]
                        sv.inc(v.scalar_tensor_tensor(acc[gt][:, 0:512], ps_y[2 * par][:, :],
                                               cwcol, acc[gt][:, 0:512], ALU.mult, ALU.add))
                        sv.inc(v.scalar_tensor_tensor(acc[gt][:, 512:1024], ps_y[2 * par + 1][:, :],
                                                      cwcol, acc[gt][:, 512:1024], ALU.mult, ALU.add))
                        acc_done[(ei, gt)] = sv.n
                te_done_c[(ei, cc)] = st_.n
                if cc == 3:
                    te_done[ei] = st_.n
        experts_done = sv.n
        if half == 0:
            half0_acc_done = experts_done

        for i in range(HT):
            gi = base_tile + i
            lnd = emit_ln_tile(nc, block, c, acc[i][:, :], lns, sv, sa, experts_done)
            @block.sync
            def _(sy, i=i, gi=gi, lnd=lnd):
                sy.wait_ge(sv.h, lnd)
                seq_dma(sy, so, io["y_out"][gi * 128:(gi + 1) * 128, :], acc[i][:, :])
            store_done[i] = so.n
        if half == 0:
            half0_stores = so.n

    @block.sync
    def _(sy):
        sy.wait_ge(so.h, so.n)



def emit_phaseA(nc, block, c, io, sems, pref="pa_", n_tiles=NT, arena=None):
    so2 = [sems["s0"], sems["s1"]]
    A = (lambda n, sh, dt: arena(pref + n, sh, dt)) if arena is not None \
        else (lambda n, sh, dt: nc.alloc_sbuf_tensor(pref + n, sh, dt))

    H = 16
    xt = [A(f"xt{j}", [128, DT], F32) for j in range(2)]
    xb = A("xb", [128, DT], BF16)
    xfm = A("xfm", [128, 8 * 128], BF16)
    xfm3 = xfm.rearrange("p (k t) -> p k t", k=8)
    wstg = A("wstg", [128, DT], F32)
    wstgb = A("wstgb", [128, DT], BF16)
    wqT = A("wqT", [128, 8 * 3072], BF16)
    wqT3 = wqT.rearrange("p (k o) -> p k o", k=8)
    woT = A("woT", [128, 8 * DT], BF16)
    woT3 = woT.rearrange("p (k o) -> p k o", k=8)
    q_tm = A("q_tm", [128, DT], F32)
    qn = A("qn", [128, DT], BF16)
    q_fm = A("q_fm", [64, 16 * 128], BF16)
    k_tm = A("k_tm", [128, DT], BF16)
    v_tm = A("v_tm", [128, DT], BF16)
    qmax = A("qmax", [128, 16], F32)
    qsum = A("qsum", [128, 16], F32)
    qscl = A("qscl", [128, 16], F32)
    ctx_f = A("ctx_f", [64, 1040], F32)
    ctx_b = A("ctx_b", [64, 1040], BF16)
    dinv = A("dinv", [128, 16], F32)
    attn_tm = A("attn_tm", [128, DT], BF16)
    attn_fm = A("attn_fm", [128, 8 * 128], BF16)
    attn_fm3 = attn_fm.rearrange("p (k t) -> p k t", k=8)
    ot = [A(f"ot{j}", [128, DT], F32) for j in range(2)]
    lns = alloc_ln_scratch(A)

    G = [c.PS[j][:, :] for j in range(6)]
    C0 = c.PS[6][:, :]
    C1 = c.PS[7][:, :]
    Gb = [g.bitcast(BF16) for g in G]

    ld, sv, sa, st_, so = sems["ld"], sems["v"], sems["a"], sems["t"], sems["st"]
    wld = sems["wld"]
    wcast = sems["wc"]

    @block.sync
    def _(sy):
        emit_common_load(c, sy, ld)
    ld_common = ld.n
    wq_rows = 24
    wo_rows = 8
    ev_done = {}
    for r in range(wq_rows + wo_rows):
        is_wo = r >= wq_rows
        rr = r - wq_rows if is_wo else r
        @block.sync
        def _(sy, r=r, rr=rr, is_wo=is_wo):
            if r >= 1:
                sy.wait_ge(wcast.h, r)
            src = (io["wout"] if is_wo else io["wqkv"])[rr * 128:(rr + 1) * 128, :]
            seq_dma(sy, wld, wstg[:, :], src)
        @block.gpsimd
        def _(g, r=r):
            g.wait_ge(wld.h, wld.n)
            if r >= 1:
                g.wait_ge(st_.h, tr_done[r - 1])
            wcast.inc(g.tensor_copy(wstgb[:, :], wstg[:, :]))
        @block.tensor
        def _(t, r=r, rr=rr, is_wo=is_wo):
            t.wait_ge(wcast.h, wcast.n)
            if r == 0:
                t.wait_ge(ld.h, ld_common)
            if r >= 2:
                t.wait_ge(sv.h, ev_done[r - 2])
            for kk in range(8):
                mm = t.transpose(Gb[r % 2][:, kk * 128:(kk + 1) * 128],
                                 wstgb[:, kk * 128:(kk + 1) * 128], c.id128_bf[:, :])
            st_.inc(mm)
        if r == 0:
            tr_done = {}
        tr_done[r] = st_.n
        @block.vector
        def _(v, r=r, rr=rr, is_wo=is_wo):
            v.wait_ge(st_.h, tr_done[r])
            dst3 = woT3 if is_wo else wqT3
            sv.inc(v.tensor_copy(dst3[:, :, rr * 128:(rr + 1) * 128],
                                 Gb[r % 2].rearrange("p (k t) -> p k t", k=8)))
        ev_done[r] = sv.n
    wT_done = sv.n

    @block.vector
    def _(v):
        sv.inc(v.memset(ctx_f[:, :], 0.0))
        v.wait_ge(sv.h, sv.n)
        sv.inc(v.tensor_copy(ctx_b[:, :], ctx_f[:, :]))
    ctx_init = sv.n

    x_free = {}
    store_done = {}
    evq = {}
    ctx_evac = {0: 0}
    for i in range(n_tiles):
        @block.sync
        def _(sy, i=i):
            if i >= 2:
                sy.wait_ge(sv.h, x_free[i - 2])
                sy.wait_ge(so2[i % 2].h, store_done[i - 2])
            seq_dma(sy, ld, xt[i % 2][:, :], io["x_in"][i * 128:(i + 1) * 128, :])
        x_ld = ld.n

        @block.scalar
        def _(sc, i=i, x_ld=x_ld):
            sc.wait_ge(ld.h, x_ld)
            if i >= 1:
                sc.wait_ge(st_.h, xT_mm[0])
            sa.inc(sc.copy(xb[:, :], xt[i % 2][:, :]))
        xb_done = sa.n

        @block.tensor
        def _(t, i=i, xb_done=xb_done):
            t.wait_ge(sa.h, xb_done)
            if i == 0:
                t.wait_ge(sv.h, wT_done)
            else:
                t.wait_ge(sv.h, xfm_evac[0])
            for kk in range(8):
                mm = t.transpose(Gb[0][:, kk * 128:(kk + 1) * 128],
                                 xb[:, kk * 128:(kk + 1) * 128], c.id128_bf[:, :])
            st_.inc(mm)
        if i == 0:
            xT_mm = {}
            xfm_evac = {0: 0}
        xT_mm[0] = st_.n
        xT_done = st_.n

        @block.vector
        def _(v, i=i, xT_done=xT_done):
            v.wait_ge(st_.h, xT_done)
            sv.inc(v.tensor_copy(xfm3[:, :, :], Gb[0].rearrange("p (k t) -> p k t", k=8)))
        xfm_evac[0] = sv.n
        xfm_done = sv.n

        pass_mm = {}
        for p_ in range(2):
            @block.tensor
            def _(t, i=i, p_=p_):
                t.wait_ge(sv.h, xfm_done)
                if p_ == 0:
                    t.wait_ge(sv.h, attn_evac[0] if i >= 1 else 0)
                else:
                    t.wait_ge(sv.h, evac_v[0] if i >= 1 else 0)
                for ob in range(3):
                    for kk in range(8):
                        mm = t.matmul(G[3 * p_ + ob][:, :], xfm3[:, kk, :],
                                      wqT3[:, kk, (3 * p_ + ob) * 512:(3 * p_ + ob + 1) * 512],
                                      start=(kk == 0), stop=(kk == 7))
                st_.inc(mm)
            pass_mm[p_] = st_.n

        if i == 0:
            evac_v = {0: 0}
            attn_evac = {0: 0}

        @block.vector
        def _(v, i=i, p0=pass_mm[0]):
            v.wait_ge(st_.h, p0)
            sv.inc(v.reduce_max(qmax[:, 0:8], G[0].rearrange("p (g x) -> p g x", g=8), axis=AX.X))
            sv.inc(v.reduce_max(qmax[:, 8:16], G[1].rearrange("p (g x) -> p g x", g=8), axis=AX.X))
            v.wait_ge(sv.h, sv.n)
            sv.inc(v.tensor_tensor(q_tm[:, 0:512].rearrange("p (g x) -> p g x", g=8),
                            G[0].rearrange("p (g x) -> p g x", g=8),
                            qmax[:, 0:8].rearrange("p (g o) -> p g o", o=1).to_broadcast([128, 8, 64]),
                            ALU.subtract))
            sv.inc(v.tensor_tensor(q_tm[:, 512:1024].rearrange("p (g x) -> p g x", g=8),
                                   G[1].rearrange("p (g x) -> p g x", g=8),
                                   qmax[:, 8:16].rearrange("p (g o) -> p g o", o=1).to_broadcast([128, 8, 64]),
                                   ALU.subtract))
        qsub_done = sv.n

        @block.scalar
        def _(sc, i=i, qsub_done=qsub_done):
            sc.wait_ge(sv.h, qsub_done)
            sa.inc(sc.activation(q_tm[:, :], q_tm[:, :], AF.Exp))
            sa.inc(sc.activation(k_tm[:, 0:512], G[2][:, :], AF.Exp))
        qexp_done = sa.n

        @block.vector
        def _(v, i=i, qexp_done=qexp_done):
            v.wait_ge(sa.h, qexp_done)
            sv.inc(v.reduce_sum(qsum[:, :], q_tm.rearrange("p (g x) -> p g x", g=16), axis=AX.X))
            v.wait_ge(sv.h, sv.n)
            sv.inc(v.reciprocal(qscl[:, :], qsum[:, :]))
            v.wait_ge(sv.h, sv.n)
            sv.inc(v.tensor_scalar_mul(qscl[:, :], qscl[:, :], 0.125))
            v.wait_ge(sv.h, sv.n)
            sv.inc(v.tensor_tensor(qn.rearrange("p (g x) -> p g x", g=16),
                                   q_tm.rearrange("p (g x) -> p g x", g=16),
                                   qscl.rearrange("p (g o) -> p g o", o=1).to_broadcast([128, 16, 64]),
                                   ALU.mult))
        qn_done = sv.n

        @block.scalar
        def _(sc, i=i, p1=pass_mm[1]):
            sc.wait_ge(st_.h, p1)
            sa.inc(sc.activation(k_tm[:, 512:1024], G[3][:, :], AF.Exp))
            sa.inc(sc.activation(v_tm[:, 0:512], G[4][:, :], AF.Copy))
            sa.inc(sc.activation(v_tm[:, 512:1024], G[5][:, :], AF.Copy))
        kv_done = sa.n

        @block.tensor
        def _(t, i=i, qn_done=qn_done):
            t.wait_ge(sv.h, qn_done)
            for h in range(16):
                mm = t.transpose(Gb[0 if h < 8 else 1][0:64, (h % 8) * 128:(h % 8 + 1) * 128],
                                 qn[:, h * 64:(h + 1) * 64], c.id128_bf[:, :])
            st_.inc(mm)
        qT_mm = st_.n

        @block.vector
        def _(v, i=i, qT_mm=qT_mm):
            v.wait_ge(st_.h, qT_mm)
            sv.inc(v.tensor_copy(q_fm[0:64, 0:1024], Gb[0][0:64, 0:1024]))
            sv.inc(v.tensor_copy(q_fm[0:64, 1024:2048], Gb[1][0:64, 0:1024]))
        qfm_done = sv.n

        for nb in range(2):
            gb = 2 * i + nb
            bp = 64 * nb
            tp_attn = (0, bp)
            tp_ctx = (bp, 0)
            @block.tensor
            def _(t, i=i, nb=nb, gb=gb, bp=bp, qfm_done=qfm_done, kv_done=kv_done):
                t.wait_ge(sv.h, qfm_done)
                t.wait_ge(sa.h, kv_done)
                t.wait_ge(sa.h, ctx_evac[0])
                if gb >= 1:
                    t.wait_ge(sv.h, attn_evac[0])
                for h in range(16):
                    qs = q_fm[0:64, h * 128 + bp: h * 128 + bp + 64]
                    tp = None if bp == 0 else (0, 64)
                    mm = t.matmul(G[1 if h < 8 else 2][bp:bp + 64, (h % 8) * 64:(h % 8) * 64 + 64],
                                  qs, ctx_b[0:64, h * 64:h * 64 + 64],
                                  start=True, stop=True, tile_position=tp)
                    mm = t.matmul(G[0][bp:bp + 64, h:h + 1],
                                  qs, ctx_b[0:64, 1024 + h:1025 + h],
                                  start=True, stop=True, tile_position=tp)
                st_.inc(mm)
                for h in range(16):
                    Cps = C0 if h < 8 else C1
                    tp = None if bp == 0 else (64, 0)
                    kslice = k_tm[bp:bp + 64, h * 64:(h + 1) * 64]
                    mm = t.matmul(Cps[0:64, (h % 8) * 64:(h % 8) * 64 + 64],
                                  kslice, v_tm[bp:bp + 64, h * 64:(h + 1) * 64],
                                  start=True, stop=True, tile_position=tp)
                    mm = t.matmul(G[0][0:64, 16 + h:17 + h],
                                  kslice, c.ones_cb[bp:bp + 64, 0:1],
                                  start=True, stop=True, tile_position=(bp, 0))
                st_.inc(mm)
            attn_mm = st_.n

            @block.vector
            def _(v, i=i, nb=nb, gb=gb, bp=bp, attn_mm=attn_mm):
                v.wait_ge(st_.h, attn_mm)
                sv.inc(v.tensor_scalar_max(dinv[bp:bp + 64, :], G[0][bp:bp + 64, 0:16], 1e-6))
                v.wait_ge(sv.h, sv.n)
                sv.inc(v.reciprocal(dinv[bp:bp + 64, :], dinv[bp:bp + 64, :]))
                v.wait_ge(sv.h, sv.n)
                sv.inc(v.tensor_tensor(
                    attn_tm[bp:bp + 64, 0:512].rearrange("p (g x) -> p g x", g=8),
                    G[1][bp:bp + 64, :].rearrange("p (g x) -> p g x", g=8),
                    dinv[bp:bp + 64, 0:8].rearrange("p (g o) -> p g o", o=1).to_broadcast([64, 8, 64]),
                    ALU.mult))
                sv.inc(v.tensor_tensor(
                    attn_tm[bp:bp + 64, 512:1024].rearrange("p (g x) -> p g x", g=8),
                    G[2][bp:bp + 64, :].rearrange("p (g x) -> p g x", g=8),
                    dinv[bp:bp + 64, 8:16].rearrange("p (g o) -> p g o", o=1).to_broadcast([64, 8, 64]),
                    ALU.mult))
                v.wait_ge(sv.h, sv.n)
                sv.inc(v.tensor_tensor(ctx_f[0:64, 0:512], ctx_f[0:64, 0:512], C0[0:64, :], ALU.add))
                sv.inc(v.tensor_tensor(ctx_f[0:64, 512:1024], ctx_f[0:64, 512:1024], C1[0:64, :], ALU.add))
                sv.inc(v.tensor_tensor(ctx_f[0:64, 1024:1040], ctx_f[0:64, 1024:1040],
                                       G[0][0:64, 16:32], ALU.add))
            attn_evac[0] = sv.n
            ctx_add = sv.n

            @block.scalar
            def _(sc, gb=gb, ctx_add=ctx_add):
                sc.wait_ge(sv.h, ctx_add)
                sa.inc(sc.copy(ctx_b[:, :], ctx_f[:, :]))
            ctx_evac[0] = sa.n

        @block.tensor
        def _(t, i=i):
            t.wait_ge(sv.h, attn_evac[0])
            for kk in range(8):
                mm = t.transpose(Gb[3][:, kk * 128:(kk + 1) * 128],
                                 attn_tm[:, kk * 128:(kk + 1) * 128], c.id128_bf[:, :])
            st_.inc(mm)
        aT_mm = st_.n

        @block.vector
        def _(v, i=i, aT_mm=aT_mm):
            v.wait_ge(st_.h, aT_mm)
            sv.inc(v.tensor_copy(attn_fm3[:, :, :], Gb[3].rearrange("p (k t) -> p k t", k=8)))
        afm_done = sv.n

        @block.tensor
        def _(t, i=i, afm_done=afm_done):
            t.wait_ge(sv.h, afm_done)
            for haf in range(2):
                for kk in range(8):
                    mm = t.matmul(G[4 + haf][:, :], attn_fm3[:, kk, :],
                                  woT3[:, kk, haf * 512:(haf + 1) * 512],
                                  start=(kk == 0), stop=(kk == 7))
            st_.inc(mm)
        wo_mm = st_.n

        @block.vector
        def _(v, i=i, wo_mm=wo_mm):
            v.wait_ge(st_.h, wo_mm)
            if i >= 2:
                v.wait_ge(so2[i % 2].h, store_done[i - 2])
            sv.inc(v.tensor_tensor(ot[i % 2][:, 0:512], G[4][:, :], xt[i % 2][:, 0:512], ALU.add))
            sv.inc(v.tensor_tensor(ot[i % 2][:, 512:1024], G[5][:, :], xt[i % 2][:, 512:1024], ALU.add))
        evac_v[0] = sv.n
        x_free[i] = sv.n
        ot_done = sv.n

        ln_done = emit_ln_tile(nc, block, c, ot[i % 2][:, :], lns, sv, sa, ot_done)

        @block.sync
        def _(sy, i=i, ln_done=ln_done):
            sy.wait_ge(sv.h, ln_done)
            seq_dma(sy, so2[i % 2], io["y_out"][i * 128:(i + 1) * 128, :], ot[i % 2][:, :])
        store_done[i] = so2[i % 2].n

    @block.sync
    def _(sy):
        sy.wait_ge(so2[0].h, so2[0].n)
        sy.wait_ge(so2[1].h, so2[1].n)



def emit_phaseC_setup(nc, block, c, io, sems, sb, pref="pc_", arena=None):
    A = (lambda n, sh, dt: arena(pref + n, sh, dt)) if arena is not None \
        else (lambda n, sh, dt: nc.alloc_sbuf_tensor(pref + n, sh, dt))
    ld, sv, sa, st_ = sems["ld"], sems["v"], sems["a"], sems["t"]
    wld, wcast = sems["wld"], sems["wc"]

    eta_s = A("eta", [1, 1], F32)
    laml_s = A("laml", [1, 1], F32)
    lam_s = A("lam", [1, 4], F32)
    prow = A("prow", [1, 65], F32)
    lamrow = A("lamrow", [1, 64], F32)
    onesr = A("onesr", [1, 64], F32)
    bc = A("bc", [128, 8], F32)
    lampow = A("lampow", [128, 1], F32)
    lamrev = A("lamrev", [128, 1], F32)
    nbetapow = A("nbetapow", [128, 1], F32)
    E128 = A("E128", [128, 64], F32)
    masks = A("masks", [128, 4 * 64], F32)
    idbeta = A("idbeta", [128, 128], BF16)
    gam_b = A("gamb", [128, DT], F32)
    bet_b = A("betb", [128, DT], F32)
    gam_r = A("gamr", [1, DT], F32)
    bet_r = A("betr", [1, DT], F32)
    wT = A("wT", [128, 8 * DT], BF16)
    wT3 = wT.rearrange("p (k o) -> p k o", k=8)
    wstg = A("wstg", [128, DT], F32)
    wstgb = A("wstgb", [128, DT], BF16)
    sb.update(locals())

    import ml_dtypes
    rev = np.zeros((64, 64), np.float32)
    for i in range(64):
        rev[i, 63 - i] = 1.0
    d_rev = nc.inline_tensor(rev.astype(ml_dtypes.bfloat16), pref + "d_rev")
    revm = A("revm", [64, 64], BF16)
    sb["revm"] = revm

    PS = [c.PS[j][:, :] for j in range(8)]
    sb["PS"] = PS

    @block.sync
    def _(sy):
        emit_common_load(c, sy, ld)
        seq_dma(sy, ld, eta_s[:, :], io["eta"])
        seq_dma(sy, ld, laml_s[:, :], io["laml"])
        seq_dma(sy, ld, gam_r[:, :], io["gam"])
        seq_dma(sy, ld, bet_r[:, :], io["bet"])
        seq_dma(sy, ld, revm[:, :], dram_ap(d_rev, (64, 64)))
    setup_ld = ld.n

    @block.scalar
    def _(sc):
        sc.wait_ge(ld.h, setup_ld)
        sa.inc(sc.activation(lam_s[0:1, 0:1], laml_s[0:1, 0:1], AF.Sigmoid))
        sc.wait_ge(sa.h, sa.n)
        sa.inc(sc.activation(lam_s[0:1, 3:4], lam_s[0:1, 0:1], AF.Ln))
    lam_done = sa.n

    @block.vector
    def _(v):
        v.wait_ge(sa.h, lam_done)
        sv.inc(v.tensor_scalar(lam_s[0:1, 1:2], lam_s[0:1, 0:1], -1.0, 1.0, ALU.mult, ALU.add))
        v.wait_ge(sv.h, sv.n)
        sv.inc(v.tensor_tensor(lam_s[0:1, 1:2], lam_s[0:1, 1:2], eta_s[0:1, 0:1], ALU.mult))
        v.wait_ge(sv.h, sv.n)
        sv.inc(v.tensor_scalar_mul(lam_s[0:1, 1:2], lam_s[0:1, 1:2], 1.0 / 64))
        sv.inc(v.memset(onesr[0:1, :], 1.0))
        v.wait_ge(sv.h, sv.n)
        sv.inc(v.tensor_scalar(lamrow[0:1, :], onesr[0:1, :], lam_s[0:1, 0:1], None, ALU.mult))
        v.wait_ge(sv.h, sv.n)
        sv.inc(v.tensor_tensor_scan(prow[0:1, 1:65], lamrow[0:1, :], onesr[0:1, :], 1.0,
                                    ALU.mult, ALU.mult))
        sv.inc(v.memset(prow[0:1, 0:1], 1.0))
        v.wait_ge(sv.h, sv.n)
        sv.inc(v.tensor_copy(lam_s[0:1, 2:3], prow[0:1, 64:65]))
    scal_done = sv.n

    @block.tensor
    def _(t):
        t.wait_ge(sv.h, scal_done)
        t.matmul(PS[0][0:64, 0:1], prow[0:1, 1:65], c.ones_f[0:1, 0:1], start=True, stop=True)
        t.matmul(PS[0][64:128, 0:1], prow[0:1, 1:65], c.ones_f[0:1, 0:1], start=True, stop=True,
                 tile_position=(0, 64))
        t.matmul(PS[0][:, 8:9], c.ones_f[0:1, 0:128], lam_s[0:1, 1:2], start=True, stop=True)
        t.matmul(PS[0][:, 9:10], c.ones_f[0:1, 0:128], lam_s[0:1, 2:3], start=True, stop=True)
        t.matmul(PS[0][:, 10:11], c.ones_f[0:1, 0:128], lam_s[0:1, 3:4], start=True, stop=True)
        st_.inc(t.matmul(PS[0][:, 11:12], c.ones_f[0:1, 0:128], lam_s[0:1, 0:1], start=True, stop=True))
    bc_mm = st_.n

    @block.vector
    def _(v):
        v.wait_ge(st_.h, bc_mm)
        sv.inc(v.tensor_copy(lampow[:, :], PS[0][:, 0:1]))
        sv.inc(v.tensor_copy(bc[:, 0:4], PS[0][:, 8:12]))
    bc_done = sv.n

    lampow_b = A("lampowb", [128, 1], BF16)
    sb["lampow_b"] = lampow_b

    @block.scalar
    def _(sc):
        sc.wait_ge(sv.h, bc_done)
        sa.inc(sc.copy(lampow_b[:, :], lampow[:, :]))
    lpb_done = sa.n

    @block.tensor
    def _(t):
        t.wait_ge(sa.h, lpb_done)
        t.matmul(PS[1][0:64, 0:1], revm[0:64, 0:64], lampow_b[0:64, 0:1], start=True, stop=True)
        st_.inc(t.matmul(PS[1][64:128, 0:1], revm[0:64, 0:64], lampow_b[0:64, 0:1],
                         start=True, stop=True, tile_position=(0, 64)))
    rev_mm = st_.n

    @block.vector
    def _(v):
        v.wait_ge(st_.h, rev_mm)
        sv.inc(v.reciprocal(bc[:, 4:5], bc[:, 3:4]))
        v.wait_ge(sv.h, sv.n)
        sv.inc(v.tensor_tensor(lamrev[:, :], PS[1][:, 0:1], bc[:, 4:5], ALU.mult))
    lamrev_done = sv.n

    @block.gpsimd
    def _(g):
        g.wait_ge(sv.h, lamrev_done)
        g.iota(E128[0:64, :], [[-1, 64]], base=0, channel_multiplier=1,
               allow_small_or_imprecise_dtypes=True)
        g.iota(E128[64:128, :], [[-1, 64]], base=0, channel_multiplier=1,
               allow_small_or_imprecise_dtypes=True)
        wcast.inc(g.memset(masks[:, :], 0.0))
    iota_done = wcast.n

    @block.vector
    def _(v):
        v.wait_ge(wcast.h, iota_done)
        sv.inc(v.tensor_scalar_max(masks[:, 128:192], E128[:, :], 0.0))
    emax_done = sv.n

    @block.scalar
    def _(sc):
        sc.wait_ge(sv.h, emax_done)
        sa.inc(sc.activation(masks[:, 128:192], masks[:, 128:192], AF.Exp, scale=bc[:, 2:3]))
    lexp_done = sa.n

    @block.vector
    def _(v):
        v.wait_ge(sa.h, lexp_done)
        sv.inc(v.tensor_single_scalar(masks[:, 0:64], E128[:, :], 0.0, ALU.is_gt))
        v.wait_ge(sv.h, sv.n)
        sv.inc(v.tensor_tensor(masks[:, 0:64], masks[:, 0:64], masks[:, 128:192], ALU.mult))
        v.wait_ge(sv.h, sv.n)
        sv.inc(v.tensor_scalar(masks[:, 0:64], masks[:, 0:64], bc[:, 0:1], -1.0,
                               ALU.mult, ALU.mult))
        sv.inc(v.tensor_single_scalar(masks[:, 192:256], E128[:, :], 0.0, ALU.is_ge))
        v.wait_ge(sv.h, sv.n)
        sv.inc(v.tensor_tensor(masks[:, 128:192], masks[:, 128:192], masks[:, 192:256], ALU.mult))
    l_done = sv.n

    @block.vector
    def _(v):
        v.wait_ge(sv.h, l_done)
        sv.inc(v.tensor_scalar_mul(E128[:, :], E128[:, :], -1.0))
        v.wait_ge(sv.h, sv.n)
        sv.inc(v.tensor_scalar_max(masks[:, 64:128], E128[:, :], 0.0))
    emax2_done = sv.n

    @block.scalar
    def _(sc):
        sc.wait_ge(sv.h, emax2_done)
        sa.inc(sc.activation(masks[:, 64:128], masks[:, 64:128], AF.Exp, scale=bc[:, 2:3]))
    lexp2_done = sa.n

    @block.vector
    def _(v):
        v.wait_ge(sa.h, lexp2_done)
        sv.inc(v.tensor_single_scalar(masks[:, 192:256], E128[:, :], 0.0, ALU.is_gt))
        v.wait_ge(sv.h, sv.n)
        sv.inc(v.tensor_tensor(masks[:, 192:256], masks[:, 192:256], masks[:, 64:128], ALU.mult))
        v.wait_ge(sv.h, sv.n)
        sv.inc(v.tensor_scalar(masks[:, 192:256], masks[:, 192:256], bc[:, 0:1], -1.0,
                               ALU.mult, ALU.mult))
        sv.inc(v.tensor_single_scalar(E128[:, :], E128[:, :], 0.0, ALU.is_ge))
        v.wait_ge(sv.h, sv.n)
        sv.inc(v.tensor_tensor(masks[:, 64:128], masks[:, 64:128], E128[:, :], ALU.mult))
    masks_done = sv.n

    @block.vector
    def _(v):
        v.wait_ge(sv.h, masks_done)
        sv.inc(v.tensor_scalar(idbeta[:, :], c.id128_f[:, :], bc[:, 0:1], None, ALU.mult))
    idb_done = sv.n

    @block.tensor
    def _(t):
        t.wait_ge(ld.h, setup_ld)
        t.wait_ge(sv.h, idb_done)
        t.matmul(PS[2][:, :], c.ones_f[0:1, 0:128], gam_r[0:1, 0:512], start=True, stop=True)
        t.matmul(PS[3][:, :], c.ones_f[0:1, 0:128], gam_r[0:1, 512:1024], start=True, stop=True)
        t.matmul(PS[4][:, :], c.ones_f[0:1, 0:128], bet_r[0:1, 0:512], start=True, stop=True)
        st_.inc(t.matmul(PS[5][:, :], c.ones_f[0:1, 0:128], bet_r[0:1, 512:1024], start=True, stop=True))
    gb_mm = st_.n

    @block.vector
    def _(v):
        v.wait_ge(st_.h, gb_mm)
        sv.inc(v.tensor_copy(gam_b[:, 0:512], PS[2][:, :]))
        sv.inc(v.tensor_copy(gam_b[:, 512:1024], PS[3][:, :]))
        sv.inc(v.tensor_copy(bet_b[:, 0:512], PS[4][:, :]))
        sv.inc(v.tensor_copy(bet_b[:, 512:1024], PS[5][:, :]))
    gb_done = sv.n

    tr_done = {}
    ev_done = {}
    for r in range(8):
        @block.sync
        def _(sy, r=r):
            if r >= 1:
                sy.wait_ge(wcast.h, iota_done + r)
            seq_dma(sy, wld, wstg[:, :], io["w_in"][r * 128:(r + 1) * 128, :])
        @block.gpsimd
        def _(g, r=r):
            g.wait_ge(wld.h, wld.n)
            if r >= 1:
                g.wait_ge(st_.h, tr_done[r - 1])
            wcast.inc(g.tensor_copy(wstgb[:, :], wstg[:, :]))
        @block.tensor
        def _(t, r=r):
            t.wait_ge(wcast.h, wcast.n)
            if r == 0:
                t.wait_ge(sv.h, gb_done)
            if r >= 2:
                t.wait_ge(sv.h, ev_done[r - 2])
            for kk in range(8):
                mm = t.transpose(PS[r % 2].bitcast(BF16)[:, kk * 128:(kk + 1) * 128],
                                 wstgb[:, kk * 128:(kk + 1) * 128], c.id128_bf[:, :])
            st_.inc(mm)
        tr_done[r] = st_.n
        @block.vector
        def _(v, r=r):
            v.wait_ge(st_.h, tr_done[r])
            sv.inc(v.tensor_copy(wT3[:, :, r * 128:(r + 1) * 128],
                                 PS[r % 2].bitcast(BF16).rearrange("p (k t) -> p k t", k=8)))
        ev_done[r] = sv.n
    sb["setup_done"] = sv.n
    return sb


def emit_phaseC(nc, block, c, io, sems, pref="pc_", n_tiles=NT, stage_limit=99, arena=None):
    so3 = [sems["s0"], sems["s1"], sems["s2"]]
    A = (lambda n, sh, dt: arena(pref + n, sh, dt)) if arena is not None \
        else (lambda n, sh, dt: nc.alloc_sbuf_tensor(pref + n, sh, dt))
    sb = {}
    emit_phaseC_setup(nc, block, c, io, sems, sb, pref, arena=arena)
    ld, sv, sa, st_, so = sems["ld"], sems["v"], sems["a"], sems["t"], sems["st"]
    PS = sb["PS"]
    masks, idbeta = sb["masks"], sb["idbeta"]
    lampow, lamrev, bc = sb["lampow"], sb["lamrev"], sb["bc"]
    gam_b, bet_b, wT3 = sb["gam_b"], sb["bet_b"], sb["wT3"]

    RD = 3
    xt_r = [A(f"xtr{j}", [128, DT], F32) for j in range(RD)]
    st_r = [A(f"str{j}", [128, DT], BF16) for j in range(RD)]
    kb_r = [A(f"kbr{j}", [128, DT], BF16) for j in range(RD)]
    pt_r = [A(f"ptr{j}", [128, DT], BF16) for j in range(RD)]
    gw_r = [A(f"gwr{j}", [128, DT], BF16) for j in range(RD)]
    rt_r = [A(f"rtr{j}", [128, DT], BF16) for j in range(RD)]
    wn_r = [A(f"wnr{j}", [128, DT], BF16) for j in range(RD)]

    xa_tm = A("xa_tm", [128, DT], BF16)
    xa_fm = A("xa_fm", [128, 8 * 128], BF16)
    xa_fm3 = xa_fm.rearrange("p (k t) -> p k t", k=8)
    LK = A("LK", [128, DT], BF16)
    NLK = A("NLK", [128, DT], BF16)
    Nsb = A("Nsb", [128, 512], BF16)
    NTsb = A("NTsb", [128, 512], BF16)
    Gsb = A("Gsb", [128, 512], BF16)
    GTsb = A("GTsb", [128, 512], BF16)
    Npw = [A(f"Npw{j}", [128, 512], BF16) for j in range(2)]
    NTpw = [A(f"NTpw{j}", [128, 512], BF16) for j in range(2)]
    X_sb = A("X_sb", [128, 8 * 128], BF16)
    S_f = A("S_f", [128, 1024], F32)
    S_sb = A("S_sb", [128, 1024], BF16)
    U_sb = A("U_sb", [128, 1024], BF16)
    lkt_tmp = A("lkt_tmp", [128, 512], BF16)
    yt = A("yt", [128, DT], F32)
    ot = [A(f"ot{j}", [128, DT], F32) for j in range(2)]
    lns = alloc_ln_scratch(A)

    mb3 = lambda col: masks[:, col * 64:(col + 1) * 64].rearrange(
        "p (o x) -> p o x", o=1).to_broadcast([128, 8, 64])
    scol = lambda h: (0 if h < 8 else 512) + ((h // 2) % 4) * 64

    @block.vector
    def _(v):
        v.wait_ge(sv.h, sb["setup_done"])
        for j in range(8):
            v.memset(PS[j], 0.0)
        sv.inc(v.memset(S_f[:, :], 0.0))
        v.wait_ge(sv.h, sv.n)
        sv.inc(v.tensor_copy(S_sb[:, :], S_f[:, :]))
    init_done = sv.n

    ring_ok = {}
    ring_sv = {}
    store_done = {}
    x_free = {}
    scan_done = {-1: 0}

    def emit_scan(it):
        r = it % RD

        for nb in range(2):
            cp = 64 * nb
            csl = slice(nb * 512, (nb + 1) * 512)
            @block.tensor
            def _(t, it=it, nb=nb, cp=cp):
                t.wait_ge(sa.h, ring_ok[it])
                t.wait_ge(sv.h, ring_sv[it])
                t.wait_ge(sv.h, init_done)
                t.wait_ge(sa.h, s_refresh[0])
                if it >= 1 or nb == 1:
                    t.wait_ge(sv.h, ydone[0])
                    t.wait_ge(sv.h, u_evac[0])
                    t.wait_ge(sv.h, sdelta_done[0])
                if "u" not in SUB9:
                    st_.inc(t.matmul(PS[6][0:64, 0:64], c.id128_bf[0:64, 0:64],
                                     S_sb[0:64, 0:64], start=True, stop=True))
                else:
                    for h in range(16):
                        p0 = (h % 2) * 64
                        Pu = PS[6] if h % 2 == 0 else PS[7]
                        mm = t.matmul(Pu[cp:cp + 64, (h // 2) * 64:(h // 2) * 64 + 64],
                                      pt_r[it % RD][p0:p0 + 64, nb * 512 + (h // 2) * 64: nb * 512 + (h // 2) * 64 + 64],
                                      S_sb[p0:p0 + 64, scol(h):scol(h) + 64],
                                      start=True, stop=True, tile_position=(p0, cp))
                    st_.inc(mm)
            u_mm = st_.n

            @block.vector
            def _(v, it=it, nb=nb, cp=cp, u_mm=u_mm):
                v.wait_ge(st_.h, u_mm)
                if "ue" in SUB9:
                    u3 = U_sb[cp:cp + 64, :].rearrange("p (s two x) -> p s two x", s=8, two=2)
                    wr3 = wn_r[it % RD][cp:cp + 64, :].rearrange("p (s two x) -> p s two x", s=8, two=2)
                    sv.inc(v.scalar_tensor_tensor(u3[:, :, 0, :],
                                           PS[6][cp:cp + 64, :].rearrange("p (s x) -> p s x", s=8),
                                           -1.0, wr3[:, :, 0, :], ALU.mult, ALU.add))
                    sv.inc(v.scalar_tensor_tensor(u3[:, :, 1, :],
                                                  PS[7][cp:cp + 64, :].rearrange("p (s x) -> p s x", s=8),
                                                  -1.0, wr3[:, :, 1, :], ALU.mult, ALU.add))
                else:
                    sv.inc(v.memset(U_sb[cp:cp + 64, :], 0.0))
            u_evac[0] = sv.n

            @block.tensor
            def _(t, it=it, nb=nb, cp=cp):
                if "o" not in SUB9:
                    st_.inc(t.matmul(PS[2][0:64, 0:64], c.id128_bf[0:64, 0:64],
                                     S_sb[0:64, 0:64], start=True, stop=True))
                    return
                for h in range(16):
                    p0 = (h % 2) * 64
                    Po = PS[2] if h % 2 == 0 else PS[5]
                    mm = t.matmul(Po[cp:cp + 64, (h // 2) * 64:(h // 2) * 64 + 64],
                                  rt_r[it % RD][p0:p0 + 64, nb * 512 + (h // 2) * 64: nb * 512 + (h // 2) * 64 + 64],
                                  S_sb[p0:p0 + 64, scol(h):scol(h) + 64],
                                  start=True, stop=True, tile_position=(p0, cp))
                st_.inc(mm)
            o_mm = st_.n

            @block.tensor
            def _(t, it=it, nb=nb, cp=cp):
                t.wait_ge(sv.h, u_evac[0])
                if "d" not in SUB9:
                    st_.inc(t.matmul(PS[3][0:64, 0:64], c.id128_bf[0:64, 0:64],
                                     S_sb[0:64, 0:64], start=True, stop=True))
                    return
                for h in range(16):
                    p0 = (h % 2) * 64
                    Pd = PS[3] if h < 8 else PS[4]
                    mm = t.matmul(Pd[p0:p0 + 64, ((h // 2) % 4) * 64:((h // 2) % 4) * 64 + 64],
                                  kb_r[it % RD][cp:cp + 64, h * 64:h * 64 + 64],
                                  U_sb[cp:cp + 64, h * 64:h * 64 + 64],
                                  start=True, stop=True, tile_position=(cp, p0))
                st_.inc(mm)
            d_mm = st_.n

            @block.vector
            def _(v, d_mm=d_mm):
                v.wait_ge(st_.h, d_mm)
                if "supd" in SUB9:
                    sv.inc(v.scalar_tensor_tensor(S_f[:, 0:512], S_f[:, 0:512], bc[:, 1:2], PS[3][:, :],
                                           ALU.mult, ALU.add))
                    sv.inc(v.scalar_tensor_tensor(S_f[:, 512:1024], S_f[:, 512:1024], bc[:, 1:2], PS[4][:, :],
                                                  ALU.mult, ALU.add))
                else:
                    sv.inc(v.memset(S_f[0:1, 0:1], 0.0))
            sdelta_done[0] = sv.n

            @block.scalar
            def _(sc):
                sc.wait_ge(sv.h, sdelta_done[0])
                sa.inc(sc.copy(S_sb[:, :], S_f[:, :]))
            s_refresh[0] = sa.n

            @block.vector
            def _(v, it=it, nb=nb, cp=cp, o_mm=o_mm):
                v.wait_ge(st_.h, o_mm)
                if "y" in SUB9:
                    y3 = yt[cp:cp + 64, :].rearrange("p (s two x) -> p s two x", s=8, two=2)
                    s3 = st_r[it % RD][cp:cp + 64, :].rearrange("p (s two x) -> p s two x", s=8, two=2)
                    sv.inc(v.tensor_tensor(y3[:, :, 0, :], PS[2][cp:cp + 64, :].rearrange("p (s x) -> p s x", s=8),
                                    s3[:, :, 0, :], ALU.add))
                    sv.inc(v.tensor_tensor(y3[:, :, 1, :], PS[5][cp:cp + 64, :].rearrange("p (s x) -> p s x", s=8),
                                           s3[:, :, 1, :], ALU.add))
                    v.wait_ge(sv.h, sv.n)
                    sv.inc(v.tensor_tensor(yt[cp:cp + 64, :], yt[cp:cp + 64, :],
                                           gw_r[it % RD][cp:cp + 64, :], ALU.add))
                else:
                    sv.inc(v.memset(yt[cp:cp + 64, :], 0.5))
            ydone[0] = sv.n

        yt_done = sv.n
        if "ln" in SUB9:
            ln1 = emit_ln_tile(nc, block, c, yt[:, :], lns, sv, sa, yt_done,
                               gamma_b=gam_b[:, :], beta_b=bet_b[:, :])

            @block.vector
            def _(v, it=it, ln1=ln1):
                v.wait_ge(sv.h, ln1)
                if it >= 2:
                    v.wait_ge(so3[(it - 2) % 3].h, store_done[it - 2])
                sv.inc(v.tensor_tensor(ot[it % 2][:, :], yt[:, :], xt_r[it % RD][:, :], ALU.add))
            x_free[it] = sv.n
            ot_done = sv.n

            ln2 = emit_ln_tile(nc, block, c, ot[it % 2][:, :], lns, sv, sa, ot_done)
        else:
            @block.vector
            def _(v, it=it, yt_done=yt_done):
                v.wait_ge(sv.h, yt_done)
                if it >= 2:
                    v.wait_ge(so3[(it - 2) % 3].h, store_done[it - 2])
                sv.inc(v.tensor_copy(ot[it % 2][:, :], yt[:, :]))
            x_free[it] = sv.n
            ln2 = sv.n

        @block.sync
        def _(sy, it=it, ln2=ln2):
            sy.wait_ge(sv.h, ln2)
            seq_dma(sy, so3[it % 3], io["y_out"][it * 128:(it + 1) * 128, :], ot[it % 2][:, :])
        store_done[it] = so3[it % 3].n
        scan_done[it] = sv.n

    u_evac = {0: 0}
    sdelta_done = {0: 0}
    ydone = {0: 0}
    maskprod_done = {0: 0}
    x0_evac = {0: 0}
    solve_done = {0: 0}
    gw_ev = {0: 0}
    wr_ev = {0: 0}
    rt_ev = {0: 0}
    s_refresh = {0: init_done and 0}
    s_refresh[0] = sems["a"].n

    for i in range(n_tiles):
        if i >= 1 and stage_limit >= 9:
            emit_scan(i - 1)

        r = i % RD
        @block.sync
        def _(sy, i=i):
            if i >= RD:
                sy.wait_ge(sv.h, x_free.get(i - RD, 0))
                sy.wait_ge(so3[(i - RD) % 3].h, store_done.get(i - RD, 0))
            seq_dma(sy, ld, xt_r[i % RD][:, :], io["x_in"][i * 128:(i + 1) * 128, :])
        x_ld = ld.n

        @block.scalar
        def _(sc, i=i, x_ld=x_ld):
            sc.wait_ge(ld.h, x_ld)
            if i >= 1:
                sc.wait_ge(sv.h, lk_done[0])
            sa.inc(sc.activation(xa_tm[:, :], xt_r[i % RD][:, :], AF.Tanh))
        tanh_done = sa.n

        @block.tensor
        def _(t, i=i, tanh_done=tanh_done):
            t.wait_ge(sa.h, tanh_done)
            if i == 0:
                t.wait_ge(sv.h, init_done)
            else:
                t.wait_ge(sv.h, xafm_evac[0])
            for kk in range(8):
                mm = t.transpose(PS[0].bitcast(BF16)[:, kk * 128:(kk + 1) * 128],
                                 xa_tm[:, kk * 128:(kk + 1) * 128], c.id128_bf[:, :])
            st_.inc(mm)
        xaT_mm = st_.n

        @block.vector
        def _(v, i=i, xaT_mm=xaT_mm):
            v.wait_ge(st_.h, xaT_mm)
            sv.inc(v.tensor_copy(xa_fm3[:, :, :],
                                 PS[0].bitcast(BF16).rearrange("p (k t) -> p k t", k=8)))
        if i == 0:
            xafm_evac = {0: 0}
            lk_done = {0: 0}
            st_evac = {0: 0}
        xafm_evac[0] = sv.n
        xafm_done = sv.n

        if stage_limit < 2:
            continue
        @block.tensor
        def _(t, i=i, xafm_done=xafm_done):
            t.wait_ge(sv.h, xafm_done)
            if i >= 1:
                t.wait_ge(sv.h, ydone[0])
                t.wait_ge(sv.h, st_evac[0])
            for haf in range(2):
                for kk in range(8):
                    mm = t.matmul(PS[1 + haf][:, :], xa_fm3[:, kk, :],
                                  wT3[:, kk, haf * 512:(haf + 1) * 512],
                                  start=(kk == 0), stop=(kk == 7))
            st_.inc(mm)
        stat_mm = st_.n

        @block.vector
        def _(v, i=i, stat_mm=stat_mm):
            v.wait_ge(st_.h, stat_mm)
            sv.inc(v.tensor_copy(st_r[i % RD][:, 0:512], PS[1][:, :]))
            sv.inc(v.tensor_copy(st_r[i % RD][:, 512:1024], PS[2][:, :]))
        st_evac[0] = sv.n
        static_done = sv.n

        if stage_limit < 3:
            continue
        @block.vector
        def _(v, i=i, static_done=static_done, tanh_done=tanh_done):
            v.wait_ge(sa.h, tanh_done)
            sv.inc(v.tensor_scalar(LK[:, :], xa_tm[:, :], lampow[:, 0:1], None, ALU.mult))
            v.wait_ge(sv.h, sv.n)
            sv.inc(v.tensor_scalar(NLK[:, :], LK[:, :], -1.0, None, ALU.mult))
            sv.inc(v.tensor_scalar(kb_r[i % RD][:, :], xa_tm[:, :], lamrev[:, 0:1], None, ALU.mult))
        lk_done[0] = sv.n

        for nb in range(2):
            if stage_limit < 4 or (nb == 1 and stage_limit < 8):
                continue
            bp = 64 * nb
            csl_base = nb * 512
            @block.tensor
            def _(t, i=i, nb=nb, bp=bp, sa_bar=None):
                t.wait_ge(sv.h, lk_done[0])
                t.wait_ge(sv.h, sdelta_done[0])
                t.wait_ge(sv.h, rt_ev[0])
                t.wait_ge(sa.h, sa.n)
                if i >= 1 or nb == 1:
                    t.wait_ge(sv.h, maskprod_done[0])
                for h in range(16):
                    p0 = (h % 2) * 64
                    mm = t.matmul(PS[3][p0:p0 + 64, (h // 2) * 64:(h // 2) * 64 + 64],
                                  xa_fm3[(h % 2) * 64:(h % 2) * 64 + 64, h // 2, bp:bp + 64],
                                  xa_fm3[(h % 2) * 64:(h % 2) * 64 + 64, h // 2, bp:bp + 64],
                                  start=True, stop=True, tile_position=(p0, p0))
                st_.inc(mm)
            a_mm = st_.n

            @block.vector
            def _(v, i=i, nb=nb, a_mm=a_mm):
                v.wait_ge(st_.h, a_mm)
                PA3 = PS[3].rearrange("p (s x) -> p s x", s=8)
                sv.inc(v.tensor_tensor(Nsb.rearrange("p (s x) -> p s x", s=8), PA3, mb3(0), ALU.mult))
                sv.inc(v.tensor_tensor(NTsb.rearrange("p (s x) -> p s x", s=8), PA3, mb3(3), ALU.mult))
                sv.inc(v.tensor_tensor(Gsb.rearrange("p (s x) -> p s x", s=8), PA3, mb3(2), ALU.mult))
                sv.inc(v.tensor_tensor(GTsb.rearrange("p (s x) -> p s x", s=8), PA3, mb3(1), ALU.mult))
            maskprod_done[0] = sv.n

            if stage_limit < 5:
                continue
            @block.tensor
            def _(t, i=i, nb=nb, bp=bp, static_done=static_done):
                t.wait_ge(sv.h, static_done)
                t.wait_ge(sv.h, lk_done[0])
                if i >= 1 or nb == 1:
                    t.wait_ge(sv.h, x0_evac[0])
                    t.wait_ge(sv.h, wr_ev[0])
                    t.wait_ge(sv.h, ydone[0])
                for h in range(16):
                    p0 = (h % 2) * 64
                    PX = PS[4] if h < 8 else PS[5]
                    sl = (h % 8) // 2 * 128 + (h % 2) * 0
                    sl = ((h // 2) % 4) * 128
                    t.matmul(PX[p0:p0 + 64, sl:sl + 64],
                             idbeta[bp:bp + 64, bp:bp + 64],
                             st_r[i % RD][bp:bp + 64, h * 64:h * 64 + 64],
                             start=True, stop=True, tile_position=(bp, p0))
                    mm = t.matmul(PX[p0:p0 + 64, sl + 64:sl + 128],
                                  idbeta[bp:bp + 64, bp:bp + 64],
                                  LK[bp:bp + 64, h * 64:h * 64 + 64],
                                  start=True, stop=True, tile_position=(bp, p0))
                st_.inc(mm)
            x0_mm = st_.n

            @block.vector
            def _(v, x0_mm=x0_mm):
                v.wait_ge(st_.h, x0_mm)
                sv.inc(v.tensor_copy(X_sb[:, 0:512], PS[4][:, :]))
                sv.inc(v.tensor_copy(X_sb[:, 512:1024], PS[5][:, :]))
            x0_evac[0] = sv.n

            if stage_limit < 6:
                continue
            for lvl in range(6):
                cur = lvl % 2
                nxt = 1 - cur
                @block.tensor
                def _(t, i=i, nb=nb, lvl=lvl, cur=cur):
                    t.wait_ge(sv.h, x0_evac[0])
                    if lvl == 0:
                        pass
                    else:
                        t.wait_ge(sa.h, sq_evac[0])
                    for h in range(16):
                        p0 = (h % 2) * 64
                        PX = PS[4] if h < 8 else PS[5]
                        sl = ((h // 2) % 4) * 128
                        NTs = NTsb if lvl == 0 else NTpw[cur]
                        t.matmul(PX[p0:p0 + 64, sl:sl + 128],
                                 c.id128_bf[p0:p0 + 64, p0:p0 + 64],
                                 X_sb[p0:p0 + 64, sl:sl + 128],
                                 start=True, stop=False, tile_position=(p0, p0))
                        mm = t.matmul(PX[p0:p0 + 64, sl:sl + 128],
                                      NTs[p0:p0 + 64, (h // 2) * 64:(h // 2) * 64 + 64],
                                      X_sb[p0:p0 + 64, sl:sl + 128],
                                      start=False, stop=True, tile_position=(p0, p0))
                    st_.inc(mm)
                ap_mm = st_.n

                if lvl < 5:
                    @block.tensor
                    def _(t, i=i, nb=nb, lvl=lvl, cur=cur, nxt=nxt, sa_now=None):
                        if lvl >= 1:
                            t.wait_ge(sa.h, sq_evac[0])
                        else:
                            t.wait_ge(sv.h, maskprod_done[0])
                            t.wait_ge(sa.h, sa.n)
                        for h in range(16):
                            p0 = (h % 2) * 64
                            Ns = Nsb if lvl == 0 else Npw[cur]
                            NTs = NTsb if lvl == 0 else NTpw[cur]
                            sl = (h // 2) * 64
                            t.matmul(PS[6][p0:p0 + 64, sl:sl + 64],
                                     NTs[p0:p0 + 64, sl:sl + 64], Ns[p0:p0 + 64, sl:sl + 64],
                                     start=True, stop=True, tile_position=(p0, p0))
                            mm = t.matmul(PS[7][p0:p0 + 64, sl:sl + 64],
                                          Ns[p0:p0 + 64, sl:sl + 64], NTs[p0:p0 + 64, sl:sl + 64],
                                          start=True, stop=True, tile_position=(p0, p0))
                        st_.inc(mm)
                    sq_mm = st_.n

                    @block.scalar
                    def _(sc, lvl=lvl, nxt=nxt, sq_mm=sq_mm):
                        sc.wait_ge(st_.h, sq_mm)
                        sa.inc(sc.copy(Npw[nxt][:, :], PS[6][:, :]))
                        sa.inc(sc.copy(NTpw[nxt][:, :], PS[7][:, :]))
                    if i == 0 and nb == 0 and lvl == 0:
                        sq_evac = {0: 0}
                    sq_evac[0] = sa.n

                @block.vector
                def _(v, ap_mm=ap_mm):
                    v.wait_ge(st_.h, ap_mm)
                    v.tensor_copy(X_sb[:, 0:512], PS[4][:, :])
                    sv.inc(v.tensor_copy(X_sb[:, 512:1024], PS[5][:, :]))
                x0_evac[0] = sv.n
            x_done = x0_evac[0]

            if stage_limit < 7:
                continue
            @block.tensor
            def _(t, i=i, nb=nb, bp=bp, x_done=x_done):
                t.wait_ge(sv.h, x_done)
                t.wait_ge(sa.h, sq_evac[0])
                mm = None
                if "pt" in SUB7:
                    for h in range(16):
                        p0 = (h % 2) * 64
                        base = 0 if h < 8 else 512
                        xsl = base + ((h // 2) % 4) * 128
                        mm = t.transpose(PS[6].bitcast(BF16)[p0:p0 + 64, (h // 2) * 64:(h // 2) * 64 + 64],
                                         X_sb[p0:p0 + 64, xsl + 64:xsl + 128],
                                         c.id128_bf[p0:p0 + 64, p0:p0 + 64], tile_position=(p0, p0))
                    st_.inc(mm)
                if "gw" in SUB7:
                    for h in range(16):
                        p0 = (h % 2) * 64
                        base = 0 if h < 8 else 512
                        xsl = base + ((h // 2) % 4) * 128
                        Pg = PS[7] if h % 2 == 0 else PS[3]
                        mm = t.matmul(Pg[bp:bp + 64, (h // 2) * 64:(h // 2) * 64 + 64],
                                      GTsb[p0:p0 + 64, (h // 2) * 64:(h // 2) * 64 + 64],
                                      X_sb[p0:p0 + 64, xsl:xsl + 64],
                                      start=True, stop=True, tile_position=(p0, bp))
                    st_.inc(mm)
                if mm is None:
                    st_.inc(t.matmul(PS[7][0:64, 0:64], GTsb[0:64, 0:64], X_sb[0:64, 0:64],
                                     start=True, stop=True))
            ptgw_mm = st_.n

            @block.vector
            def _(v, i=i, nb=nb, bp=bp, ptgw_mm=ptgw_mm):
                v.wait_ge(st_.h, ptgw_mm)
                sv.inc(v.tensor_copy(pt_r[i % RD][:, nb * 512:(nb + 1) * 512],
                                     PS[6].bitcast(BF16)[:, 0:512]))
                if "gevac" in SUB7:
                    v.wait_ge(sv.h, sv.n)
                    gwr3 = gw_r[i % RD][bp:bp + 64, :].rearrange("p (s two x) -> p s two x", s=8, two=2)
                    sv.inc(v.tensor_copy(gwr3[:, :, 0, :], PS[7][bp:bp + 64, :].rearrange("p (s x) -> p s x", s=8)))
                    sv.inc(v.tensor_copy(gwr3[:, :, 1, :], PS[3][bp:bp + 64, :].rearrange("p (s x) -> p s x", s=8)))
            gw_ev[0] = sv.n

            @block.tensor
            def _(t, i=i, nb=nb, bp=bp, x_done=x_done, gw_evac=None):
                t.wait_ge(sv.h, maskprod_done[0])
                t.wait_ge(sv.h, gw_ev[0])
                if "rt" not in SUB7:
                    st_.inc(t.matmul(PS[3][0:64, 0:64], GTsb[0:64, 0:64], X_sb[0:64, 0:64],
                                     start=True, stop=True))
                else:
                    for h in range(16):
                        p0 = (h % 2) * 64
                        base = 0 if h < 8 else 512
                        xsl = base + ((h // 2) % 4) * 128
                        mm = t.matmul(PS[3][p0:p0 + 64, (h // 2) * 64:(h // 2) * 64 + 64],
                                      X_sb[p0:p0 + 64, xsl + 64:xsl + 128],
                                      Gsb[p0:p0 + 64, (h // 2) * 64:(h // 2) * 64 + 64],
                                      start=True, stop=True, tile_position=(p0, p0))
                    st_.inc(mm)
                    for h in range(16):
                        p0 = (h % 2) * 64
                        mm = t.transpose(PS[6].bitcast(BF16)[p0:p0 + 64, 512 + (h // 2) * 64:512 + (h // 2) * 64 + 64],
                                         LK[bp:bp + 64, h * 64:h * 64 + 64],
                                         c.id128_bf[bp:bp + 64, bp:bp + 64], tile_position=(bp, p0))
                    st_.inc(mm)
            rt_mm = st_.n

            @block.vector
            def _(v, i=i, nb=nb, rt_mm=rt_mm):
                v.wait_ge(st_.h, rt_mm)
                if "rtc" in SUB7:
                    sv.inc(v.tensor_copy(lkt_tmp[:, :], PS[6].bitcast(BF16)[:, 512:1024]))
                    v.wait_ge(sv.h, sv.n)
                    sv.inc(v.scalar_tensor_tensor(rt_r[i % RD][:, nb * 512:(nb + 1) * 512],
                                                  PS[3][:, :], -1.0, lkt_tmp[:, :],
                                                  ALU.mult, ALU.add))
                else:
                    sv.inc(v.memset(rt_r[i % RD][:, nb * 512:(nb + 1) * 512], 0.0))
            rtw_v = sv.n
            rt_ev[0] = rtw_v

            @block.tensor
            def _(t, i=i, nb=nb, bp=bp, x_done=x_done):
                if "wp" not in SUB7:
                    st_.inc(t.matmul(PS[4][0:64, 0:64], c.id128_bf[0:64, 0:64],
                                     X_sb[0:64, 0:64], start=True, stop=True))
                    return
                for h in range(16):
                    p0 = (h % 2) * 64
                    base = 0 if h < 8 else 512
                    xsl = base + ((h // 2) % 4) * 128
                    Pw = PS[4] if h % 2 == 0 else PS[5]
                    mm = t.matmul(Pw[bp:bp + 64, (h // 2) * 64:(h // 2) * 64 + 64],
                                  c.id128_bf[p0:p0 + 64, p0:p0 + 64],
                                  X_sb[p0:p0 + 64, xsl:xsl + 64],
                                  start=True, stop=True, tile_position=(p0, bp))
                st_.inc(mm)
            w_mm = st_.n

            @block.vector
            def _(v, i=i, nb=nb, bp=bp, w_mm=w_mm):
                v.wait_ge(st_.h, w_mm)
                if "wp" in SUB7:
                    wr3 = wn_r[i % RD][bp:bp + 64, :].rearrange("p (s two x) -> p s two x", s=8, two=2)
                    sv.inc(v.tensor_copy(wr3[:, :, 0, :], PS[4][bp:bp + 64, :].rearrange("p (s x) -> p s x", s=8)))
                    sv.inc(v.tensor_copy(wr3[:, :, 1, :], PS[5][bp:bp + 64, :].rearrange("p (s x) -> p s x", s=8)))
                else:
                    sv.inc(v.memset(wn_r[i % RD][bp:bp + 64, :], 0.0))
            wr_ev[0] = sv.n


        ring_ok[i] = sems["a"].n
        ring_sv[i] = sems["v"].n

    if n_tiles >= 1 and stage_limit >= 9:
        emit_scan(n_tiles - 1)

    @block.sync
    def _(sy):
        for s3 in so3:
            sy.wait_ge(s3.h, s3.n)


def emit_phase_barrier(nc, block, pbar, k, so_list):
    target = 5 * (k + 1)
    for engname in ("tensor", "vector", "scalar", "gpsimd", "sync"):
        @getattr(block, engname)
        def _(e):
            for so in so_list:
                e.wait_ge(so.h, so.n)
            e.sem_inc(pbar.h, 1)
            e.wait_ge(pbar.h, target)
    pbar.n = target


def build_fused(debug_outs=False):
    import contextlib
    nc = bass.Bass("TRN2", target_bir_lowering=False)
    x_in = nc.dram_tensor("x_in", [4096, 1024], F32, kind="ExternalInput").ap()
    wqkv = nc.dram_tensor("wqkv", [3072, 1024], F32, kind="ExternalInput").ap()
    wout = nc.dram_tensor("wout", [1024, 1024], F32, kind="ExternalInput").ap()
    cent = nc.dram_tensor("cent", [64, 1024], F32, kind="ExternalInput").ap()
    eb = nc.dram_tensor("eb", [1, 64], F32, kind="ExternalInput").ap()
    wsu = nc.dram_tensor("wsu", [1024, 512], F32, kind="ExternalInput")
    wsd = nc.dram_tensor("wsd", [512, 1024], F32, kind="ExternalInput")
    wu_all = nc.dram_tensor("wu_all", [64, 1024, 512], F32, kind="ExternalInput")
    wd_all = nc.dram_tensor("wd_all", [64, 512, 1024], F32, kind="ExternalInput")
    w1 = nc.dram_tensor("w1", [1024, 1024], F32, kind="ExternalInput").ap()
    gam = nc.dram_tensor("gam", [1, 1024], F32, kind="ExternalInput").ap()
    bet = nc.dram_tensor("bet", [1, 1024], F32, kind="ExternalInput").ap()
    eta = nc.dram_tensor("eta", [1, 1], F32, kind="ExternalInput").ap()
    laml = nc.dram_tensor("laml", [1, 1], F32, kind="ExternalInput").ap()
    w2 = nc.dram_tensor("w2", [1024, 1024], F32, kind="ExternalInput").ap()
    b2 = nc.dram_tensor("b2", [1, 1024], F32, kind="ExternalInput").ap()
    y_out = nc.dram_tensor("y_out", [4096, 1024], F32, kind="ExternalOutput").ap()
    if debug_outs:
        xs1 = nc.dram_tensor("xs1", [4096, 1024], F32, kind="ExternalOutput").ap()
        xs2 = nc.dram_tensor("xs2", [4096, 1024], F32, kind="ExternalOutput").ap()
        xs3 = nc.dram_tensor("xs3", [4096, 1024], F32, kind="ExternalOutput").ap()
    else:
        xs1 = nc.dram_tensor("xs1", [4096, 1024], F32).ap()
        xs2 = nc.dram_tensor("xs2", [4096, 1024], F32).ap()
        xs3 = nc.dram_tensor("xs3", [4096, 1024], F32).ap()

    c = setup_common(nc)
    ARENA = 188 * 1024
    r = nc.bump_sbuf(ARENA)
    base = r[0] if isinstance(r, tuple) else r
    arena = Arena(nc, base, base + ARENA)

    def wu_e(e):
        return bass.AP(wu_all, e * 1024 * 512, [[512, 128], [128 * 512, 8], [1, 512]])

    def wd_e(e):
        return bass.AP(wd_all, e * 512 * 1024, [[1024, 128], [128 * 1024, 4], [1, 1024]])

    with contextlib.ExitStack() as es:
        pbar = Sem(es.enter_context(nc.semaphore("pbar")))
        sA = make_sems(nc, es, "pa_")
        sB = make_sems(nc, es, "pb_")
        sC = make_sems(nc, es, "pc_")
        sD = make_sems(nc, es, "pd_")
        with nc.Block() as block:
            arena.reset()
            with nc.named_scope("phA"):
                emit_phaseA(nc, block, c, {"x_in": x_in, "wqkv": wqkv, "wout": wout,
                                           "y_out": xs1}, sA, arena=arena)
                emit_phase_barrier(nc, block, pbar, 0, [sA["s0"], sA["s1"]])
            arena.reset()
            with nc.named_scope("phB"):
                emit_phaseB(nc, block, c, {
                    "x_in": xs1, "cent": cent, "eb": eb,
                    "wsu": bass.AP(wsu, 0, [[512, 128], [128 * 512, 8], [1, 512]]),
                    "wsd": bass.AP(wsd, 0, [[1024, 128], [128 * 1024, 4], [1, 1024]]),
                    "wu_e": wu_e, "wd_e": wd_e, "y_out": xs2}, sB, arena=arena)
                emit_phase_barrier(nc, block, pbar, 1, [sB["st"]])
            arena.reset()
            with nc.named_scope("phC"):
                emit_phaseC(nc, block, c, {"x_in": xs2, "w_in": w1, "gam": gam,
                                           "bet": bet, "eta": eta, "laml": laml,
                                           "y_out": xs3}, sC, arena=arena)
                emit_phase_barrier(nc, block, pbar, 2, [sC["s0"], sC["s1"], sC["s2"]])
            arena.reset()
            with nc.named_scope("phD"):
                emit_phaseD(nc, block, c, {"x_in": xs3, "w_in": w2, "b_in": b2,
                                           "y_out": y_out}, sD, arena=arena)
    return nc


from concourse.bass_utils import run_bass_kernel_spmd

N_CORES = 8


def kernel(L0_attn_x, L0_attn_W_qkv, L0_attn_W_out, L0_ffn_centroids,
           L0_ffn_expert_biases, L0_ffn_Ws_up, L0_ffn_Ws_down, L0_ffn_W_up,
           L0_ffn_W_down, L1_attn_W, L1_attn_ln_gamma, L1_attn_ln_beta,
           L1_attn_eta, L1_attn_lam_logit, L1_ffn_weight, L1_ffn_bias):
    cast = lambda a: np.ascontiguousarray(np.asarray(a, dtype=np.float32))
    x = cast(L0_attn_x)
    nc = build_fused()
    res = run_bass_kernel_spmd(nc, [{
        "x_in": x[b],
        "wqkv": cast(L0_attn_W_qkv), "wout": cast(L0_attn_W_out),
        "cent": cast(L0_ffn_centroids),
        "eb": cast(L0_ffn_expert_biases).reshape(1, 64),
        "wsu": cast(L0_ffn_Ws_up), "wsd": cast(L0_ffn_Ws_down),
        "wu_all": cast(L0_ffn_W_up), "wd_all": cast(L0_ffn_W_down),
        "w1": cast(L1_attn_W),
        "gam": cast(L1_attn_ln_gamma).reshape(1, 1024),
        "bet": cast(L1_attn_ln_beta).reshape(1, 1024),
        "eta": cast(L1_attn_eta).reshape(1, 1),
        "laml": cast(L1_attn_lam_logit).reshape(1, 1),
        "w2": cast(L1_ffn_weight), "b2": cast(L1_ffn_bias).reshape(1, 1024)}
        for b in range(N_CORES)], core_ids=list(range(N_CORES)))
    return np.stack([np.asarray(res.results[b]["y_out"])
                     for b in range(N_CORES)], axis=0)


# version 08
# speedup vs baseline: 1.1945x; 1.0008x over previous
import numpy as np
import concourse.bass as bass
import concourse.mybir as mybir

F32 = mybir.dt.float32
BF16 = mybir.dt.bfloat16
I32 = mybir.dt.int32
AF = mybir.ActivationFunctionType
ALU = mybir.AluOpType
AX = mybir.AxisListType

NT = 32
SUB7 = {"pt", "gw", "rt", "evac", "wp", "gevac", "rtc"}
SUB9 = {"u", "o", "d", "supd", "y", "ln", "st", "skel", "ue"}
DT = 1024
TOK = NT * 128


class Sem:
    def __init__(self, h):
        self.h = h
        self.n = 0

    def inc(self, instr, k=1):
        instr.then_inc(self.h, k)
        self.n += k
        return self.n


CONST_VALUES = [1e-5, 1e-30, 1.0 / 7.5, 0.0, 1.0, -1.0, 0.5, 1e-6, 1e-8]


class Common:
    pass


def seq_dma(eng, sem, out, in_):
    eng.wait_ge(sem.h, sem.n)
    instr = eng.dma_start(out, in_)
    instr.then_inc(sem.h, 16)
    sem.n += 16
    return instr



class Arena:
    def __init__(self, nc, base, limit):
        self.nc = nc
        self.base = base
        self.off = base
        self.limit = limit

    def reset(self):
        self.off = self.base

    def __call__(self, name, shape, dtype):
        isz = {F32: 4, BF16: 2, I32: 4}[dtype]
        off = (self.off + 31) & ~31
        t = self.nc.alloc_sbuf_tensor_at(name, shape, dtype, offset=off)
        self.off = off + shape[-1] * isz
        assert self.off <= self.limit, (name, self.off, self.limit)
        return t


def setup_common(nc):
    import ml_dtypes
    c = Common()
    c.id128_bf = nc.alloc_sbuf_tensor("c_id128b", [128, 128], BF16)
    c.id128_f = nc.alloc_sbuf_tensor("c_id128f", [128, 128], F32)
    c.ones_f = nc.alloc_sbuf_tensor("c_onesf", [1, 128], F32)
    c.ones_cb = nc.alloc_sbuf_tensor("c_onescb", [128, 1], BF16)
    c.consts = nc.alloc_sbuf_tensor("c_consts", [128, len(CONST_VALUES)], F32)
    idn = np.eye(128, dtype=np.float32)
    c.d_idb = nc.inline_tensor(idn.astype(ml_dtypes.bfloat16), "d_idb")
    c.d_idf = nc.inline_tensor(idn, "d_idf")
    c.d_onesf = nc.inline_tensor(np.ones((1, 128), np.float32), "d_onesf")
    c.d_onescb = nc.inline_tensor(np.ones((128, 1), np.float32).astype(ml_dtypes.bfloat16), "d_onescb")
    c.d_consts = nc.inline_tensor(np.array([CONST_VALUES], np.float32), "d_consts")
    for i, v in enumerate(CONST_VALUES):
        nc.const_aps.register(F32, v, c.consts[:, i:i + 1])
    c.PS = [nc.alloc_psum_tensor(f"cps{j}", [128, 512], F32) for j in range(8)]
    return c


def dram_ap(handle, shape):
    r, co = shape
    return bass.AP(handle, 0, [[co, r], [1, 1], [1, co]])


def dram_row_bcast(handle, cols, parts=128):
    return bass.AP(handle, 0, [[0, parts], [1, 1], [1, cols]])


def emit_common_load(c, sy, ld):
    seq_dma(sy, ld, c.id128_bf[:, :], dram_ap(c.d_idb, (128, 128)))
    seq_dma(sy, ld, c.id128_f[:, :], dram_ap(c.d_idf, (128, 128)))
    seq_dma(sy, ld, c.ones_f[:, :], dram_ap(c.d_onesf, (1, 128)))
    seq_dma(sy, ld, c.ones_cb[:, :], dram_ap(c.d_onescb, (128, 1)))
    seq_dma(sy, ld, c.consts[:, :], dram_row_bcast(c.d_consts, len(CONST_VALUES)))


def emit_ln_tile(nc, block, c, v_tile, scr, sv, sa, pre_v,
                 gamma_b=None, beta_b=None, out_tile=None, eps=1e-5):
    out_tile = v_tile if out_tile is None else out_tile
    sqs, ssq, s1 = scr["sqs"], scr["ssq"], scr["s1"]
    mean, var, rstd, nb = scr["mean"], scr["var"], scr["rstd"], scr["nb"]

    @block.scalar
    def _(sc):
        sc.wait_ge(sv.h, pre_v)
        sa.inc(sc.activation(sqs, v_tile, AF.Square, accum_out=ssq))
    ssq_done = sa.n

    @block.vector
    def _(v):
        v.wait_ge(sv.h, pre_v)
        sv.inc(v.reduce_sum(s1, v_tile, axis=AX.X))
        v.wait_ge(sv.h, sv.n)
        sv.inc(v.tensor_scalar_mul(mean, s1, 1.0 / DT))
        v.wait_ge(sv.h, sv.n)
        sv.inc(v.tensor_tensor(var, mean, mean, ALU.mult))
        v.wait_ge(sa.h, ssq_done)
        v.wait_ge(sv.h, sv.n)
        sv.inc(v.scalar_tensor_tensor(var, ssq, 1.0 / DT, var, ALU.mult, ALU.subtract))
    var_done = sv.n

    @block.scalar
    def _(sc):
        sc.wait_ge(sv.h, var_done)
        sa.inc(sc.activation(var, var, AF.Sqrt, bias=eps))
    std_done = sa.n

    @block.vector
    def _(v):
        v.wait_ge(sa.h, std_done)
        sv.inc(v.reciprocal(rstd, var))
        v.wait_ge(sv.h, sv.n)
        sv.inc(v.tensor_tensor(nb, mean, rstd, ALU.mult))
        v.wait_ge(sv.h, sv.n)
        sv.inc(v.tensor_scalar_mul(nb, nb, -1.0))
        v.wait_ge(sv.h, sv.n)
        sv.inc(v.tensor_scalar(out_tile, v_tile, rstd, nb, ALU.mult, ALU.add))
        if gamma_b is not None:
            v.wait_ge(sv.h, sv.n)
            sv.inc(v.tensor_tensor(out_tile, out_tile, gamma_b, ALU.mult))
            v.wait_ge(sv.h, sv.n)
            sv.inc(v.tensor_tensor(out_tile, out_tile, beta_b, ALU.add))
    return sv.n


def alloc_ln_scratch(A, parts=128):
    return {
        "sqs": A("ln_sqs", [parts, DT], F32)[:, :],
        "ssq": A("ln_ssq", [parts, 1], F32)[:, :],
        "s1": A("ln_s1", [parts, 1], F32)[:, :],
        "mean": A("ln_mean", [parts, 1], F32)[:, :],
        "var": A("ln_var", [parts, 1], F32)[:, :],
        "rstd": A("ln_rstd", [parts, 1], F32)[:, :],
        "nb": A("ln_nb", [parts, 1], F32)[:, :],
    }



def emit_phaseD(nc, block, c, io, sems, pref="pd_", arena=None):
    so2 = [sems["s0"], sems["s1"]]
    """io: x_in [4096,1024] f32 dram AP, w_in [1024,1024], b_in [1,1024],
    y_out [4096,1024]. sems: dict ld, v, a, t, st (fresh counters ok)."""
    A = (lambda n, sh, dt: arena(pref + n, sh, dt)) if arena is not None \
        else (lambda n, sh, dt: nc.alloc_sbuf_tensor(pref + n, sh, dt))

    xt = [A(f"xt{j}", [128, DT], F32) for j in range(2)]
    wtf = A("wtf", [128, DT], F32)
    wd = A("wd", [128, DT], BF16)
    wdT = A("wdT", [128, 8 * DT], BF16)
    wdT3 = wdT.rearrange("p (k o) -> p k o", k=8)
    bias_b = A("biasb", [128, DT], F32)
    bias_r = A("biasr", [1, DT], F32)
    qs = []
    for j in range(2):
        qs.append({nm: A(f"{nm}{j}", [128, 32] if nm in ("am", "scb", "iscb")
                         else [128, DT], I32 if nm == "ti" else F32)
                   for nm in ("am", "scb", "iscb", "uu", "au", "ge2", "ge4",
                              "invs", "ss", "tt", "ti", "tf", "sbl")})
    xd = [A(f"xd{j}", [128, DT], BF16) for j in range(2)]
    xdf = [A(f"xdf{j}", [128, DT], BF16) for j in range(2)]
    ot = [A(f"ot{j}", [128, DT], F32) for j in range(2)]
    lns = alloc_ln_scratch(A)

    ps_tr = [c.PS[0].bitcast(BF16), c.PS[1][:, :].bitcast(BF16)]
    ps_y = [c.PS[2 + j][:, :] for j in range(4)]

    ld, sv, sa, st_, so = sems["ld"], sems["v"], sems["a"], sems["t"], sems["st"]

    @block.sync
    def _(sy):
        emit_common_load(c, sy, ld)
        seq_dma(sy, ld, bias_r[:, :], io["b_in"])
    ld_common = ld.n

    @block.tensor
    def _(t):
        t.wait_ge(ld.h, ld_common)
        t.matmul(ps_y[0][:, :], c.ones_f[0:1, 0:128], bias_r[0:1, 0:512],
                 start=True, stop=True)
        st_.inc(t.matmul(ps_y[1][:, :], c.ones_f[0:1, 0:128], bias_r[0:1, 512:1024],
                         start=True, stop=True))
    bias_mm = st_.n

    @block.vector
    def _(v):
        v.wait_ge(st_.h, bias_mm)
        sv.inc(v.tensor_copy(bias_b[:, 0:512], ps_y[0][:, :]))
        sv.inc(v.tensor_copy(bias_b[:, 512:1024], ps_y[1][:, :]))
    bias_done = sv.n

    def quant_chain(q, src, dst, pre, dst_guard=None):
        am, scb, iscb = q["am"], q["scb"], q["iscb"]
        uu, au, ge2, ge4 = q["uu"], q["au"], q["ge2"], q["ge4"]
        invs, ss, tt_, ti, tf, sbl = (q["invs"], q["ss"], q["tt"], q["ti"],
                                      q["tf"], q["sbl"])

        @block.vector
        def _(v):
            for (sh, n_) in pre:
                v.wait_ge(sh.h, n_)
            sv.inc(v.reduce_max(am[:, :], src.rearrange("p (g x) -> p g x", g=32),
                                axis=AX.X, apply_absolute_value=True))
            v.wait_ge(sv.h, sv.n)
            sv.inc(v.tensor_scalar(scb[:, :], am[:, :], 1.0 / 7.5, 1e-30,
                                   ALU.mult, ALU.max))
            v.wait_ge(sv.h, sv.n)
            sv.inc(v.reciprocal(iscb[:, :], scb[:, :]))
            v.wait_ge(sv.h, sv.n)
            sv.inc(v.tensor_tensor(
                uu.rearrange("p (g x) -> p g x", g=32), src.rearrange("p (g x) -> p g x", g=32),
                iscb.rearrange("p (g o) -> p g o", o=1).to_broadcast([128, 32, 32]),
                ALU.mult))
        src_free = sv.n
        uu_done = sv.n

        @block.vector
        def _(v):
            v.wait_ge(sv.h, uu_done)
            sv.inc(v.scalar_tensor_tensor(au[:, :], uu[:, :], -1.0, uu[:, :],
                                          ALU.mult, ALU.max))
            v.wait_ge(sv.h, sv.n)
            sv.inc(v.tensor_single_scalar(ge2[:, :], au[:, :], 2.0, ALU.is_ge))
            sv.inc(v.tensor_single_scalar(ge4[:, :], au[:, :], 4.0, ALU.is_ge))
            v.wait_ge(sv.h, sv.n)
            sv.inc(v.tensor_scalar(invs[:, :], ge2[:, :], -4.0, 8.0, ALU.mult, ALU.add))
            sv.inc(v.tensor_scalar(ss[:, :], ge2[:, :], 0.125, 0.125, ALU.mult, ALU.add))
            v.wait_ge(sv.h, sv.n)
            sv.inc(v.scalar_tensor_tensor(invs[:, :], ge4[:, :], -2.0, invs[:, :],
                                          ALU.mult, ALU.add))
            sv.inc(v.scalar_tensor_tensor(ss[:, :], ge4[:, :], 0.25, ss[:, :],
                                          ALU.mult, ALU.add))
            v.wait_ge(sv.h, sv.n)
            sv.inc(v.tensor_tensor(tt_[:, :], uu[:, :], invs[:, :], ALU.mult))
        t_done = sv.n

        @block.scalar
        def _(sc):
            sc.wait_ge(sv.h, t_done)
            sa.inc(sc.copy(ti[:, :], tt_[:, :]))
            sc.wait_ge(sa.h, sa.n)
            sa.inc(sc.copy(tf[:, :], ti[:, :]))
        tf_done = sa.n

        @block.vector
        def _(v):
            sv.inc(v.tensor_tensor(
                sbl.rearrange("p (g x) -> p g x", g=32), ss.rearrange("p (g x) -> p g x", g=32),
                scb.rearrange("p (g o) -> p g o", o=1).to_broadcast([128, 32, 32]),
                ALU.mult))
            v.wait_ge(sa.h, tf_done)
            v.wait_ge(sv.h, sv.n)
            sv.inc(v.tensor_tensor(au[:, :], tt_[:, :], tf[:, :], ALU.subtract))
            v.wait_ge(sv.h, sv.n)
            sv.inc(v.scalar_tensor_tensor(tf[:, :], au[:, :], 0.5, tf[:, :],
                                          ALU.is_gt, ALU.add))
            sv.inc(v.tensor_single_scalar(ge2[:, :], au[:, :], -0.5, ALU.is_lt))
            v.wait_ge(sv.h, sv.n)
            sv.inc(v.tensor_tensor(tf[:, :], tf[:, :], ge2[:, :], ALU.subtract))
            v.wait_ge(sv.h, sv.n)
            if dst_guard is not None:
                v.wait_ge(dst_guard[0].h, dst_guard[1])
            sv.inc(v.tensor_tensor(dst, tf[:, :], sbl[:, :], ALU.mult))
        return src_free, sv.n

    w_free = {}
    wdT_evac = {}
    last_tr = [st_.n]
    for r in range(8):
        @block.sync
        def _(sy, r=r):
            if r >= 1:
                sy.wait_ge(sv.h, w_free[r - 1])
            seq_dma(sy, ld, wtf[:, :], io["w_in"][r * 128:(r + 1) * 128, :])
        w_free[r], wq_end = quant_chain(qs[r % 2], wtf[:, :], wd[:, :], [(ld, ld.n)],
                                        dst_guard=(st_, last_tr[0]))

        @block.tensor
        def _(t, r=r, wq_end=wq_end):
            t.wait_ge(sv.h, wq_end)
            if r >= 2:
                t.wait_ge(sv.h, wdT_evac[r - 2])
            for kk in range(8):
                mm = t.transpose(ps_tr[r % 2][:, kk * 128:(kk + 1) * 128],
                                 wd[:, kk * 128:(kk + 1) * 128], c.id128_bf[:, :])
            st_.inc(mm)
        tr_n = st_.n
        last_tr[0] = tr_n

        @block.vector
        def _(v, r=r, tr_n=tr_n):
            v.wait_ge(st_.h, tr_n)
            sv.inc(v.tensor_copy(wdT3[:, :, r * 128:(r + 1) * 128],
                                 ps_tr[r % 2].rearrange("p (k o) -> p k o", k=8)))
        wdT_evac[r] = sv.n
    wdT_done = sv.n

    x_free = {}
    xdf_evac = {}
    y_free = {}
    store_done = {}
    xq_done = {}
    tr_done = {}

    def emit_s1(i):
        @block.sync
        def _(sy, i=i):
            if i >= 2:
                sy.wait_ge(sv.h, x_free[i - 2])
                sy.wait_ge(so2[i % 2].h, store_done[i - 2])
            seq_dma(sy, ld, xt[i % 2][:, :], io["x_in"][i * 128:(i + 1) * 128, :])
        x_ld = ld.n
        dg = (st_, tr_done[i - 2]) if i >= 2 else (st_, last_tr[0])
        _, xq = quant_chain(qs[i % 2], xt[i % 2][:, :], xd[i % 2][:, :],
                            [(ld, x_ld)], dst_guard=dg)
        xq_done[i] = xq

    def emit_s2(i):
        @block.tensor
        def _(t, i=i):
            t.wait_ge(sv.h, xq_done[i])
            if i < 2:
                t.wait_ge(sv.h, wdT_done)
            else:
                t.wait_ge(sv.h, xdf_evac[i - 2])
            for kk in range(8):
                mm = t.transpose(ps_tr[i % 2][:, kk * 128:(kk + 1) * 128],
                                 xd[i % 2][:, kk * 128:(kk + 1) * 128], c.id128_bf[:, :])
            st_.inc(mm)
        tr_done[i] = st_.n

        @block.vector
        def _(v, i=i):
            v.wait_ge(st_.h, tr_done[i])
            sv.inc(v.tensor_copy(xdf[i % 2][:, :], ps_tr[i % 2][:, :]))
        xdf_evac[i] = sv.n

        @block.tensor
        def _(t, i=i):
            t.wait_ge(sv.h, xdf_evac[i])
            if i == 0:
                t.wait_ge(sv.h, bias_done)
            if i >= 2:
                t.wait_ge(sv.h, y_free[i - 2])
            for half in range(2):
                for kk in range(8):
                    mm = t.matmul(ps_y[2 * (i % 2) + half][:, :],
                                  xdf[i % 2][:, kk * 128:(kk + 1) * 128],
                                  wdT3[:, kk, half * 512:(half + 1) * 512],
                                  start=(kk == 0), stop=(kk == 7))
            st_.inc(mm)
        mm_done = st_.n

        @block.vector
        def _(v, i=i, mm_done=mm_done):
            v.wait_ge(st_.h, mm_done)
            if i >= 2:
                v.wait_ge(so2[i % 2].h, store_done[i - 2])
            sv.inc(v.tensor_tensor(ot[i % 2][:, 0:512], ps_y[2 * (i % 2)][:, :],
                                   bias_b[:, 0:512], ALU.add))
            sv.inc(v.tensor_tensor(ot[i % 2][:, 512:1024], ps_y[2 * (i % 2) + 1][:, :],
                                   bias_b[:, 512:1024], ALU.add))
            v.wait_ge(sv.h, sv.n)
            sv.inc(v.tensor_tensor(ot[i % 2][:, :], ot[i % 2][:, :], xt[i % 2][:, :],
                                   ALU.add))
        y_free[i] = sv.n
        x_free[i] = sv.n
        ot_done = sv.n

        ln_done = emit_ln_tile(nc, block, c, ot[i % 2][:, :], lns, sv, sa, ot_done)

        @block.sync
        def _(sy, i=i, ln_done=ln_done):
            sy.wait_ge(sv.h, ln_done)
            seq_dma(sy, so2[i % 2], io["y_out"][i * 128:(i + 1) * 128, :], ot[i % 2][:, :])
        store_done[i] = so2[i % 2].n

    emit_s1(0)
    for i in range(1, NT):
        emit_s1(i)
        emit_s2(i - 1)
    emit_s2(NT - 1)

    @block.sync
    def _(sy):
        sy.wait_ge(so2[0].h, so2[0].n)
        sy.wait_ge(so2[1].h, so2[1].n)
    return


def make_sems(nc, es, pref, names=("ld", "v", "a", "t", "st", "wld", "wc", "s0", "s1", "s2")):
    return {nm: Sem(es.enter_context(nc.semaphore(pref + nm))) for nm in names}



def emit_phaseB(nc, block, c, io, sems, pref="pb_", n_experts=64, nbuf=2, arena=None):
    A = (lambda n, sh, dt: arena(pref + n, sh, dt)) if arena is not None \
        else (lambda n, sh, dt: nc.alloc_sbuf_tensor(pref + n, sh, dt))

    HT = 16
    acc = [A(f"acc{j}", [128, DT], F32) for j in range(HT)]
    x_fm = A("xfm", [128, 8 * 2048], BF16)
    x_fm3 = x_fm.rearrange("p (k t) -> p k t", k=8)
    xb = A("xb", [128, DT], BF16)
    cw = [A(f"cw{j}", [128, 64], F32) for j in range(HT)]
    centT = A("centT", [128, 8 * 64], F32)
    centT3 = centT.rearrange("p (k e) -> p k e", k=8)
    xf32 = A("xf32", [128, DT], F32)
    cent_s = xf32
    ebb = A("ebb", [128, 64], F32)
    eb_r = A("eb_r", [1, 64], F32)
    aff = A("aff", [128, 64], F32)
    biased = A("biased", [128, 64], F32)
    max8 = A("max8", [128, 8], F32)
    rep = A("rep", [128, 64], F32)
    mask = A("mask", [128, 64], F32)
    sig = A("sig", [128, 64], F32)
    ssel = A("ssel", [128, 64], F32)
    nrm = A("nrm", [128, 1], F32)
    wub = [A(f"wub{j}", [128, 8 * 512], BF16) for j in range(nbuf)]
    wdb = [A(f"wdb{j}", [128, 4 * DT], BF16) for j in range(nbuf)]
    wstg_u = A("wstgu", [128, 8 * 512], F32)
    wstg_d = A("wstgd", [128, 4 * DT], F32)
    h_s = A("hs", [128, 4 * 512], BF16)
    lns = alloc_ln_scratch(A)

    ps_h = [c.PS[j][:, :] for j in range(4)]
    ps_y = [c.PS[4 + j][:, :] for j in range(4)]
    ps_trv = ps_h[0].bitcast(BF16)
    ps_aff = ps_h[1]

    ld, sv, sa, st_, so = sems["ld"], sems["v"], sems["a"], sems["t"], sems["st"]
    wld = sems["wld"]
    wcast = sems["wc"]

    NE = n_experts

    @block.sync
    def _(sy):
        emit_common_load(c, sy, ld)
        seq_dma(sy, ld, eb_r[:, :], io["eb"])
        seq_dma(sy, ld, cent_s[0:64, :], io["cent"])
    setup_ld = ld.n

    @block.tensor
    def _(t):
        t.wait_ge(ld.h, setup_ld)
        for kk in range(8):
            mm = t.transpose(ps_h[2][:, kk * 64:(kk + 1) * 64],
                             cent_s[0:64, kk * 128:(kk + 1) * 128], c.id128_f[0:64, 0:64])
        st_.inc(mm)
        st_.inc(t.matmul(ps_aff[:, 0:64], c.ones_f[0:1, 0:128], eb_r[0:1, 0:64],
                         start=True, stop=True))
    setup_mm = st_.n

    @block.vector
    def _(v):
        v.wait_ge(st_.h, setup_mm)
        sv.inc(v.tensor_copy(centT[:, :], ps_h[2][:, :]))
        v.wait_ge(sv.h, sv.n)
        sv.inc(v.tensor_copy(ebb[:, :], ps_aff[:, 0:64]))
    setup_done = sv.n

    last_up = [st_.n]
    store_done = {}
    dn_tile = {}
    te_done_c = {}
    xf32_evac = {0: 0}

    for half in range(2):
        base_tile = half * HT
        tile_ld = {}
        xfm_done = {}
        cw_done = {}
        for i in range(HT):
            gi = base_tile + i
            @block.sync
            def _(sy, i=i, gi=gi, half=half, so_all=None):
                if half == 1:
                    sy.wait_ge(so.h, half0_stores)
                seq_dma(sy, ld, acc[i][:, :], io["x_in"][gi * 128:(gi + 1) * 128, :])
            tile_ld[i] = ld.n

            @block.scalar
            def _(sc, i=i):
                sc.wait_ge(ld.h, tile_ld[i])
                if i == 0 and half == 0:
                    sc.wait_ge(sv.h, setup_done)
                sa.inc(sc.copy(xb[:, :], acc[i][:, :]))
            xb_done = sa.n

            @block.tensor
            def _(t, i=i, xb_done=xb_done, half=half):
                sc_wait = xb_done
                t.wait_ge(sa.h, sc_wait)
                if i >= 1:
                    t.wait_ge(sv.h, xfm_done[i - 1])
                if half == 1 and i == 0:
                    t.wait_ge(sv.h, half0_acc_done)
                for kk in range(8):
                    mm = t.transpose(ps_trv[:, kk * 128:(kk + 1) * 128],
                                     xb[:, kk * 128:(kk + 1) * 128], c.id128_bf[:, :])
                st_.inc(mm)
            tr_done = st_.n

            @block.vector
            def _(v, i=i, tr_done=tr_done):
                v.wait_ge(st_.h, tr_done)
                sv.inc(v.tensor_copy(x_fm3[:, :, i * 128:(i + 1) * 128],
                                     ps_trv.rearrange("p (k t) -> p k t", k=8)))
            xfm_done[i] = sv.n

            @block.tensor
            def _(t, i=i):
                t.wait_ge(sv.h, xfm_done[i])
                if i >= 1 or half == 1:
                    t.wait_ge(sv.h, xf32_evac[0])
                if i == 0 and half == 0:
                    t.wait_ge(sv.h, setup_done)
                for kk in range(8):
                    pb = ps_h[2] if kk < 4 else ps_h[3]
                    mm = t.transpose(pb[:, (kk % 4) * 128:(kk % 4 + 1) * 128],
                                     acc[i][:, kk * 128:(kk + 1) * 128], c.id128_f[:, :])
                st_.inc(mm)
            xtr_mm = st_.n

            @block.vector
            def _(v, i=i, xtr_mm=xtr_mm):
                v.wait_ge(st_.h, xtr_mm)
                sv.inc(v.tensor_copy(xf32[:, 0:512], ps_h[2][:, :]))
                sv.inc(v.tensor_copy(xf32[:, 512:1024], ps_h[3][:, :]))
                xf32_evac[0] = sv.n
            xf32_done = sv.n

            @block.tensor
            def _(t, i=i, xf32_done=xf32_done):
                t.wait_ge(sv.h, xf32_done)
                if i >= 1:
                    t.wait_ge(sv.h, aff_evac[i - 1])
                for kk in range(8):
                    mm = t.matmul(ps_aff[:, 0:64], xf32[:, kk * 128:(kk + 1) * 128],
                                  centT3[:, kk, :], start=(kk == 0), stop=(kk == 7))
                st_.inc(mm)
            aff_mm = st_.n

            @block.vector
            def _(v, i=i, aff_mm=aff_mm):
                v.wait_ge(st_.h, aff_mm)
                sv.inc(v.tensor_copy(aff[:, :], ps_aff[:, 0:64]))
            if i == 0 and half == 0:
                aff_evac = {}
            aff_evac[i] = sv.n

            @block.scalar
            def _(sc, i=i):
                sc.wait_ge(sv.h, aff_evac[i])
                sa.inc(sc.activation(sig[:, :], aff[:, :], AF.Sigmoid))
            sig_done = sa.n

            @block.vector
            def _(v, i=i, sig_done=sig_done):
                v.wait_ge(sv.h, aff_evac[i])
                sv.inc(v.tensor_tensor(biased[:, :], aff[:, :], ebb[:, :], ALU.add))
                v.wait_ge(sv.h, sv.n)
                sv.inc(v.max(out=max8[:, :], in_=biased[:, :]))
                v.wait_ge(sv.h, sv.n)
                sv.inc(v.match_replace(out=rep[:, :], in_to_replace=max8[:, :],
                                       in_values=biased[:, :], imm_value=-1e30))
                v.wait_ge(sv.h, sv.n)
                sv.inc(v.tensor_tensor(mask[:, :], biased[:, :], rep[:, :], ALU.not_equal))
                v.wait_ge(sa.h, sig_done)
                v.wait_ge(sv.h, sv.n)
                sv.inc(v.tensor_tensor(ssel[:, :], sig[:, :], mask[:, :], ALU.mult))
                v.wait_ge(sv.h, sv.n)
                sv.inc(v.reduce_sum(nrm[:, :], ssel[:, :], axis=AX.X))
                v.wait_ge(sv.h, sv.n)
                sv.inc(v.tensor_scalar_add(nrm[:, :], nrm[:, :], 1e-8))
                v.wait_ge(sv.h, sv.n)
                sv.inc(v.reciprocal(nrm[:, :], nrm[:, :]))
                v.wait_ge(sv.h, sv.n)
                sv.inc(v.tensor_scalar(cw[i][:, :], ssel[:, :], nrm[:, :], None,
                                       ALU.mult))
            cw_done[i] = sv.n
        routing_done = sv.n

        eld_done = {}
        cast_done = {}
        te_done = {}
        acc_done = {}
        silu_done = {}

        def emit_wload(ei):
            e = ei - 1
            @block.sync
            def _(sy, e=e, ei=ei):
                if ei >= 1:
                    sy.wait_ge(wcast.h, cast_done[ei - 1])
                if e < 0:
                    src_u = io["wsu"]
                    src_d = io["wsd"]
                else:
                    src_u = io["wu_e"](e)
                    src_d = io["wd_e"](e)
                seq_dma(sy, wld, wstg_u.rearrange("p (db f) -> p db f", db=8), src_u)
                seq_dma(sy, wld, wstg_d.rearrange("p (fb d) -> p fb d", fb=4), src_d)
            eld_done[ei] = wld.n

        emit_wload(0)

        @block.scalar
        def _(sc):
            sc.wait_ge(wld.h, eld_done[0])
            sc.copy(wub[0][:, :], wstg_u[:, :])
            wcast.inc(sc.copy(wdb[0][:, :], wstg_d[:, :]))
        cast_done[0] = wcast.n

        for ei in range(NE + 1):
            buf = ei % nbuf
            if ei + 1 <= NE:
                emit_wload(ei + 1)
            nbuf_next = (ei + 1) % nbuf

            for cc in range(4):
                up_fb = {}

                @block.tensor
                def _(t, ei=ei, cc=cc, buf=buf, half=half, up_fb=up_fb):
                    if cc == 0:
                        t.wait_ge(wcast.h, cast_done[ei])
                        if ei == 0:
                            t.wait_ge(sv.h, routing_done)
                    key = (ei, cc - 1) if cc > 0 else (ei - 1, 3)
                    if key in silu_done:
                        t.wait_ge(sa.h, silu_done[key])
                    for fb in range(4):
                        for db in range(8):
                            mm = t.matmul(ps_h[fb][:, :],
                                          wub[buf].rearrange("p (db f) -> p db f", db=8)[:, db, fb * 128:(fb + 1) * 128],
                                          x_fm3[:, db, cc * 512:(cc + 1) * 512],
                                          start=(db == 0), stop=(db == 7))
                        st_.inc(mm)
                        up_fb[fb] = st_.n
                    last_up[0] = st_.n
                up_done = st_.n

                @block.scalar
                def _(sc, ei=ei, cc=cc, up_fb=up_fb):
                    key = (ei, cc - 1) if cc > 0 else (ei - 1, 3)
                    if key in te_done_c:
                        sc.wait_ge(st_.h, te_done_c[key])
                    for fb in range(4):
                        sc.wait_ge(st_.h, up_fb[fb])
                        sa.inc(sc.activation(h_s.rearrange("p (fb f) -> p fb f", fb=4)[:, fb, :],
                                             ps_h[fb][:, :], AF.Silu))
                silu_done[(ei, cc)] = sa.n

                if ei + 1 <= NE:
                    @block.scalar
                    def _(sc, ei=ei, cc=cc, nb=nbuf_next):
                        if cc == 0:
                            sc.wait_ge(wld.h, eld_done[ei + 1])
                            if (ei + 1 - nbuf) in te_done:
                                sc.wait_ge(st_.h, te_done[ei + 1 - nbuf])
                        if cc < 2:
                            ci = sc.copy(wub[nb][:, cc * 2048:(cc + 1) * 2048],
                                         wstg_u[:, cc * 2048:(cc + 1) * 2048])
                        else:
                            cd = cc - 2
                            ci = sc.copy(wdb[nb][:, cd * 2048:(cd + 1) * 2048],
                                         wstg_d[:, cd * 2048:(cd + 1) * 2048])
                        if cc == 3:
                            wcast.inc(ci)
                    if cc == 3:
                        cast_done[ei + 1] = wcast.n

                for ti in range(4):
                    gt = cc * 4 + ti
                    par = gt % 2

                    @block.tensor
                    def _(t, ei=ei, cc=cc, buf=buf, ti=ti, gt=gt, par=par):
                        if ti == 0:
                            t.wait_ge(sa.h, silu_done[(ei, cc)])
                        if (ei, gt - 2) in acc_done:
                            t.wait_ge(sv.h, acc_done[(ei, gt - 2)])
                        elif ei > 0 and (ei - 1, gt + 14) in acc_done:
                            t.wait_ge(sv.h, acc_done[(ei - 1, gt + 14)])
                        for haf in range(2):
                            for fb in range(4):
                                mm = t.matmul(ps_y[2 * par + haf][:, :],
                                              h_s.rearrange("p (fb f) -> p fb f", fb=4)[:, fb, ti * 128:(ti + 1) * 128],
                                              wdb[buf].rearrange("p (fb d) -> p fb d", fb=4)[:, fb, haf * 512:(haf + 1) * 512],
                                              start=(fb == 0), stop=(fb == 3))
                        st_.inc(mm)
                        dn_tile[(ei, gt)] = st_.n

                    @block.vector
                    def _(v, ei=ei, gt=gt, par=par):
                        v.wait_ge(st_.h, dn_tile[(ei, gt)])
                        cwcol = c.consts[:, 4:5] if ei == 0 else cw[gt][:, ei - 1:ei]
                        sv.inc(v.scalar_tensor_tensor(acc[gt][:, 0:512], ps_y[2 * par][:, :],
                                               cwcol, acc[gt][:, 0:512], ALU.mult, ALU.add))
                        sv.inc(v.scalar_tensor_tensor(acc[gt][:, 512:1024], ps_y[2 * par + 1][:, :],
                                                      cwcol, acc[gt][:, 512:1024], ALU.mult, ALU.add))
                        acc_done[(ei, gt)] = sv.n
                te_done_c[(ei, cc)] = st_.n
                if cc == 3:
                    te_done[ei] = st_.n
        experts_done = sv.n
        if half == 0:
            half0_acc_done = experts_done

        for i in range(HT):
            gi = base_tile + i
            lnd = emit_ln_tile(nc, block, c, acc[i][:, :], lns, sv, sa, experts_done)
            @block.sync
            def _(sy, i=i, gi=gi, lnd=lnd):
                sy.wait_ge(sv.h, lnd)
                seq_dma(sy, so, io["y_out"][gi * 128:(gi + 1) * 128, :], acc[i][:, :])
            store_done[i] = so.n
        if half == 0:
            half0_stores = so.n

    @block.sync
    def _(sy):
        sy.wait_ge(so.h, so.n)



def emit_phaseA(nc, block, c, io, sems, pref="pa_", n_tiles=NT, arena=None):
    so2 = [sems["s0"], sems["s1"]]
    A = (lambda n, sh, dt: arena(pref + n, sh, dt)) if arena is not None \
        else (lambda n, sh, dt: nc.alloc_sbuf_tensor(pref + n, sh, dt))

    H = 16
    xt = [A(f"xt{j}", [128, DT], F32) for j in range(2)]
    xb = A("xb", [128, DT], BF16)
    xfm = A("xfm", [128, 8 * 128], BF16)
    xfm3 = xfm.rearrange("p (k t) -> p k t", k=8)
    wstg = A("wstg", [128, DT], F32)
    wstgb = A("wstgb", [128, DT], BF16)
    wqT = A("wqT", [128, 8 * 3072], BF16)
    wqT3 = wqT.rearrange("p (k o) -> p k o", k=8)
    woT = A("woT", [128, 8 * DT], BF16)
    woT3 = woT.rearrange("p (k o) -> p k o", k=8)
    q_tm = A("q_tm", [128, DT], F32)
    qn = A("qn", [128, DT], BF16)
    q_fm = A("q_fm", [64, 16 * 128], BF16)
    k_tm = A("k_tm", [128, DT], BF16)
    v_tm = A("v_tm", [128, DT], BF16)
    qmax = A("qmax", [128, 16], F32)
    qsum = A("qsum", [128, 16], F32)
    qscl = A("qscl", [128, 16], F32)
    ctx_f = A("ctx_f", [64, 1040], F32)
    ctx_b = A("ctx_b", [64, 1040], BF16)
    dinv = A("dinv", [128, 16], F32)
    attn_tm = A("attn_tm", [128, DT], BF16)
    attn_fm = A("attn_fm", [128, 8 * 128], BF16)
    attn_fm3 = attn_fm.rearrange("p (k t) -> p k t", k=8)
    ot = [A(f"ot{j}", [128, DT], F32) for j in range(2)]
    lns = alloc_ln_scratch(A)

    G = [c.PS[j][:, :] for j in range(6)]
    C0 = c.PS[6][:, :]
    C1 = c.PS[7][:, :]
    Gb = [g.bitcast(BF16) for g in G]

    ld, sv, sa, st_, so = sems["ld"], sems["v"], sems["a"], sems["t"], sems["st"]
    wld = sems["wld"]
    wcast = sems["wc"]

    @block.sync
    def _(sy):
        emit_common_load(c, sy, ld)
    ld_common = ld.n
    wq_rows = 24
    wo_rows = 8
    ev_done = {}
    for r in range(wq_rows + wo_rows):
        is_wo = r >= wq_rows
        rr = r - wq_rows if is_wo else r
        @block.sync
        def _(sy, r=r, rr=rr, is_wo=is_wo):
            if r >= 1:
                sy.wait_ge(wcast.h, r)
            src = (io["wout"] if is_wo else io["wqkv"])[rr * 128:(rr + 1) * 128, :]
            seq_dma(sy, wld, wstg[:, :], src)
        @block.gpsimd
        def _(g, r=r):
            g.wait_ge(wld.h, wld.n)
            if r >= 1:
                g.wait_ge(st_.h, tr_done[r - 1])
            wcast.inc(g.tensor_copy(wstgb[:, :], wstg[:, :]))
        @block.tensor
        def _(t, r=r, rr=rr, is_wo=is_wo):
            t.wait_ge(wcast.h, wcast.n)
            if r == 0:
                t.wait_ge(ld.h, ld_common)
            if r >= 2:
                t.wait_ge(sv.h, ev_done[r - 2])
            for kk in range(8):
                mm = t.transpose(Gb[r % 2][:, kk * 128:(kk + 1) * 128],
                                 wstgb[:, kk * 128:(kk + 1) * 128], c.id128_bf[:, :])
            st_.inc(mm)
        if r == 0:
            tr_done = {}
        tr_done[r] = st_.n
        @block.vector
        def _(v, r=r, rr=rr, is_wo=is_wo):
            v.wait_ge(st_.h, tr_done[r])
            dst3 = woT3 if is_wo else wqT3
            sv.inc(v.tensor_copy(dst3[:, :, rr * 128:(rr + 1) * 128],
                                 Gb[r % 2].rearrange("p (k t) -> p k t", k=8)))
        ev_done[r] = sv.n
    wT_done = sv.n

    @block.vector
    def _(v):
        sv.inc(v.memset(ctx_f[:, :], 0.0))
        v.wait_ge(sv.h, sv.n)
        sv.inc(v.tensor_copy(ctx_b[:, :], ctx_f[:, :]))
    ctx_init = sv.n

    x_free = {}
    store_done = {}
    evq = {}
    ctx_evac = {0: 0}
    for i in range(n_tiles):
        @block.sync
        def _(sy, i=i):
            if i >= 2:
                sy.wait_ge(sv.h, x_free[i - 2])
                sy.wait_ge(so2[i % 2].h, store_done[i - 2])
            seq_dma(sy, ld, xt[i % 2][:, :], io["x_in"][i * 128:(i + 1) * 128, :])
        x_ld = ld.n

        @block.scalar
        def _(sc, i=i, x_ld=x_ld):
            sc.wait_ge(ld.h, x_ld)
            if i >= 1:
                sc.wait_ge(st_.h, xT_mm[0])
            sa.inc(sc.copy(xb[:, :], xt[i % 2][:, :]))
        xb_done = sa.n

        @block.tensor
        def _(t, i=i, xb_done=xb_done):
            t.wait_ge(sa.h, xb_done)
            if i == 0:
                t.wait_ge(sv.h, wT_done)
            else:
                t.wait_ge(sv.h, xfm_evac[0])
            for kk in range(8):
                mm = t.transpose(Gb[0][:, kk * 128:(kk + 1) * 128],
                                 xb[:, kk * 128:(kk + 1) * 128], c.id128_bf[:, :])
            st_.inc(mm)
        if i == 0:
            xT_mm = {}
            xfm_evac = {0: 0}
        xT_mm[0] = st_.n
        xT_done = st_.n

        @block.vector
        def _(v, i=i, xT_done=xT_done):
            v.wait_ge(st_.h, xT_done)
            sv.inc(v.tensor_copy(xfm3[:, :, :], Gb[0].rearrange("p (k t) -> p k t", k=8)))
        xfm_evac[0] = sv.n
        xfm_done = sv.n

        pass_mm = {}
        for p_ in range(2):
            @block.tensor
            def _(t, i=i, p_=p_):
                t.wait_ge(sv.h, xfm_done)
                if p_ == 0:
                    t.wait_ge(sv.h, attn_evac[0] if i >= 1 else 0)
                else:
                    t.wait_ge(sv.h, evac_v[0] if i >= 1 else 0)
                for ob in range(3):
                    for kk in range(8):
                        mm = t.matmul(G[3 * p_ + ob][:, :], xfm3[:, kk, :],
                                      wqT3[:, kk, (3 * p_ + ob) * 512:(3 * p_ + ob + 1) * 512],
                                      start=(kk == 0), stop=(kk == 7))
                st_.inc(mm)
            pass_mm[p_] = st_.n

        if i == 0:
            evac_v = {0: 0}
            attn_evac = {0: 0}

        @block.vector
        def _(v, i=i, p0=pass_mm[0]):
            v.wait_ge(st_.h, p0)
            sv.inc(v.reduce_max(qmax[:, 0:8], G[0].rearrange("p (g x) -> p g x", g=8), axis=AX.X))
            sv.inc(v.reduce_max(qmax[:, 8:16], G[1].rearrange("p (g x) -> p g x", g=8), axis=AX.X))
            v.wait_ge(sv.h, sv.n)
            sv.inc(v.tensor_tensor(q_tm[:, 0:512].rearrange("p (g x) -> p g x", g=8),
                            G[0].rearrange("p (g x) -> p g x", g=8),
                            qmax[:, 0:8].rearrange("p (g o) -> p g o", o=1).to_broadcast([128, 8, 64]),
                            ALU.subtract))
            sv.inc(v.tensor_tensor(q_tm[:, 512:1024].rearrange("p (g x) -> p g x", g=8),
                                   G[1].rearrange("p (g x) -> p g x", g=8),
                                   qmax[:, 8:16].rearrange("p (g o) -> p g o", o=1).to_broadcast([128, 8, 64]),
                                   ALU.subtract))
        qsub_done = sv.n

        @block.scalar
        def _(sc, i=i, qsub_done=qsub_done):
            sc.wait_ge(sv.h, qsub_done)
            sa.inc(sc.activation(q_tm[:, :], q_tm[:, :], AF.Exp))
            sa.inc(sc.activation(k_tm[:, 0:512], G[2][:, :], AF.Exp))
        qexp_done = sa.n

        @block.vector
        def _(v, i=i, qexp_done=qexp_done):
            v.wait_ge(sa.h, qexp_done)
            sv.inc(v.reduce_sum(qsum[:, :], q_tm.rearrange("p (g x) -> p g x", g=16), axis=AX.X))
            v.wait_ge(sv.h, sv.n)
            sv.inc(v.reciprocal(qscl[:, :], qsum[:, :]))
            v.wait_ge(sv.h, sv.n)
            sv.inc(v.tensor_scalar_mul(qscl[:, :], qscl[:, :], 0.125))
            v.wait_ge(sv.h, sv.n)
            sv.inc(v.tensor_tensor(qn.rearrange("p (g x) -> p g x", g=16),
                                   q_tm.rearrange("p (g x) -> p g x", g=16),
                                   qscl.rearrange("p (g o) -> p g o", o=1).to_broadcast([128, 16, 64]),
                                   ALU.mult))
        qn_done = sv.n

        @block.scalar
        def _(sc, i=i, p1=pass_mm[1]):
            sc.wait_ge(st_.h, p1)
            sa.inc(sc.activation(k_tm[:, 512:1024], G[3][:, :], AF.Exp))
            sa.inc(sc.activation(v_tm[:, 0:512], G[4][:, :], AF.Copy))
            sa.inc(sc.activation(v_tm[:, 512:1024], G[5][:, :], AF.Copy))
        kv_done = sa.n

        @block.tensor
        def _(t, i=i, qn_done=qn_done):
            t.wait_ge(sv.h, qn_done)
            for h in range(16):
                mm = t.transpose(Gb[0 if h < 8 else 1][0:64, (h % 8) * 128:(h % 8 + 1) * 128],
                                 qn[:, h * 64:(h + 1) * 64], c.id128_bf[:, :])
            st_.inc(mm)
        qT_mm = st_.n

        @block.vector
        def _(v, i=i, qT_mm=qT_mm):
            v.wait_ge(st_.h, qT_mm)
            sv.inc(v.tensor_copy(q_fm[0:64, 0:1024], Gb[0][0:64, 0:1024]))
            sv.inc(v.tensor_copy(q_fm[0:64, 1024:2048], Gb[1][0:64, 0:1024]))
        qfm_done = sv.n

        for nb in range(2):
            gb = 2 * i + nb
            bp = 64 * nb
            tp_attn = (0, bp)
            tp_ctx = (bp, 0)
            @block.tensor
            def _(t, i=i, nb=nb, gb=gb, bp=bp, qfm_done=qfm_done, kv_done=kv_done):
                t.wait_ge(sv.h, qfm_done)
                t.wait_ge(sa.h, kv_done)
                t.wait_ge(sa.h, ctx_evac[0])
                if gb >= 1:
                    t.wait_ge(sv.h, attn_evac[0])
                for h in range(16):
                    qs = q_fm[0:64, h * 128 + bp: h * 128 + bp + 64]
                    tp = None if bp == 0 else (0, 64)
                    mm = t.matmul(G[1 if h < 8 else 2][bp:bp + 64, (h % 8) * 64:(h % 8) * 64 + 64],
                                  qs, ctx_b[0:64, h * 64:h * 64 + 64],
                                  start=True, stop=True, tile_position=tp)
                    mm = t.matmul(G[0][bp:bp + 64, h:h + 1],
                                  qs, ctx_b[0:64, 1024 + h:1025 + h],
                                  start=True, stop=True, tile_position=tp)
                st_.inc(mm)
                for h in range(16):
                    Cps = C0 if h < 8 else C1
                    tp = None if bp == 0 else (64, 0)
                    kslice = k_tm[bp:bp + 64, h * 64:(h + 1) * 64]
                    mm = t.matmul(Cps[0:64, (h % 8) * 64:(h % 8) * 64 + 64],
                                  kslice, v_tm[bp:bp + 64, h * 64:(h + 1) * 64],
                                  start=True, stop=True, tile_position=tp)
                    mm = t.matmul(G[0][0:64, 16 + h:17 + h],
                                  kslice, c.ones_cb[bp:bp + 64, 0:1],
                                  start=True, stop=True, tile_position=(bp, 0))
                st_.inc(mm)
            attn_mm = st_.n

            @block.vector
            def _(v, i=i, nb=nb, gb=gb, bp=bp, attn_mm=attn_mm):
                v.wait_ge(st_.h, attn_mm)
                sv.inc(v.tensor_scalar_max(dinv[bp:bp + 64, :], G[0][bp:bp + 64, 0:16], 1e-6))
                v.wait_ge(sv.h, sv.n)
                sv.inc(v.reciprocal(dinv[bp:bp + 64, :], dinv[bp:bp + 64, :]))
                v.wait_ge(sv.h, sv.n)
                sv.inc(v.tensor_tensor(
                    attn_tm[bp:bp + 64, 0:512].rearrange("p (g x) -> p g x", g=8),
                    G[1][bp:bp + 64, :].rearrange("p (g x) -> p g x", g=8),
                    dinv[bp:bp + 64, 0:8].rearrange("p (g o) -> p g o", o=1).to_broadcast([64, 8, 64]),
                    ALU.mult))
                sv.inc(v.tensor_tensor(
                    attn_tm[bp:bp + 64, 512:1024].rearrange("p (g x) -> p g x", g=8),
                    G[2][bp:bp + 64, :].rearrange("p (g x) -> p g x", g=8),
                    dinv[bp:bp + 64, 8:16].rearrange("p (g o) -> p g o", o=1).to_broadcast([64, 8, 64]),
                    ALU.mult))
                v.wait_ge(sv.h, sv.n)
                sv.inc(v.tensor_tensor(ctx_f[0:64, 0:512], ctx_f[0:64, 0:512], C0[0:64, :], ALU.add))
                sv.inc(v.tensor_tensor(ctx_f[0:64, 512:1024], ctx_f[0:64, 512:1024], C1[0:64, :], ALU.add))
                sv.inc(v.tensor_tensor(ctx_f[0:64, 1024:1040], ctx_f[0:64, 1024:1040],
                                       G[0][0:64, 16:32], ALU.add))
            attn_evac[0] = sv.n
            ctx_add = sv.n

            @block.scalar
            def _(sc, gb=gb, ctx_add=ctx_add):
                sc.wait_ge(sv.h, ctx_add)
                sa.inc(sc.copy(ctx_b[:, :], ctx_f[:, :]))
            ctx_evac[0] = sa.n

        @block.tensor
        def _(t, i=i):
            t.wait_ge(sv.h, attn_evac[0])
            for kk in range(8):
                mm = t.transpose(Gb[3][:, kk * 128:(kk + 1) * 128],
                                 attn_tm[:, kk * 128:(kk + 1) * 128], c.id128_bf[:, :])
            st_.inc(mm)
        aT_mm = st_.n

        @block.vector
        def _(v, i=i, aT_mm=aT_mm):
            v.wait_ge(st_.h, aT_mm)
            sv.inc(v.tensor_copy(attn_fm3[:, :, :], Gb[3].rearrange("p (k t) -> p k t", k=8)))
        afm_done = sv.n

        @block.tensor
        def _(t, i=i, afm_done=afm_done):
            t.wait_ge(sv.h, afm_done)
            for haf in range(2):
                for kk in range(8):
                    mm = t.matmul(G[4 + haf][:, :], attn_fm3[:, kk, :],
                                  woT3[:, kk, haf * 512:(haf + 1) * 512],
                                  start=(kk == 0), stop=(kk == 7))
            st_.inc(mm)
        wo_mm = st_.n

        @block.vector
        def _(v, i=i, wo_mm=wo_mm):
            v.wait_ge(st_.h, wo_mm)
            if i >= 2:
                v.wait_ge(so2[i % 2].h, store_done[i - 2])
            sv.inc(v.tensor_tensor(ot[i % 2][:, 0:512], G[4][:, :], xt[i % 2][:, 0:512], ALU.add))
            sv.inc(v.tensor_tensor(ot[i % 2][:, 512:1024], G[5][:, :], xt[i % 2][:, 512:1024], ALU.add))
        evac_v[0] = sv.n
        x_free[i] = sv.n
        ot_done = sv.n

        ln_done = emit_ln_tile(nc, block, c, ot[i % 2][:, :], lns, sv, sa, ot_done)

        @block.sync
        def _(sy, i=i, ln_done=ln_done):
            sy.wait_ge(sv.h, ln_done)
            seq_dma(sy, so2[i % 2], io["y_out"][i * 128:(i + 1) * 128, :], ot[i % 2][:, :])
        store_done[i] = so2[i % 2].n

    @block.sync
    def _(sy):
        sy.wait_ge(so2[0].h, so2[0].n)
        sy.wait_ge(so2[1].h, so2[1].n)



def emit_phaseC_setup(nc, block, c, io, sems, sb, pref="pc_", arena=None):
    A = (lambda n, sh, dt: arena(pref + n, sh, dt)) if arena is not None \
        else (lambda n, sh, dt: nc.alloc_sbuf_tensor(pref + n, sh, dt))
    ld, sv, sa, st_ = sems["ld"], sems["v"], sems["a"], sems["t"]
    wld, wcast = sems["wld"], sems["wc"]

    eta_s = A("eta", [1, 1], F32)
    laml_s = A("laml", [1, 1], F32)
    lam_s = A("lam", [1, 4], F32)
    prow = A("prow", [1, 65], F32)
    lamrow = A("lamrow", [1, 64], F32)
    onesr = A("onesr", [1, 64], F32)
    bc = A("bc", [128, 8], F32)
    lampow = A("lampow", [128, 1], F32)
    lamrev = A("lamrev", [128, 1], F32)
    nbetapow = A("nbetapow", [128, 1], F32)
    E128 = A("E128", [128, 64], F32)
    masks = A("masks", [128, 4 * 64], F32)
    idbeta = A("idbeta", [128, 128], BF16)
    gam_b = A("gamb", [128, DT], F32)
    bet_b = A("betb", [128, DT], F32)
    gam_r = A("gamr", [1, DT], F32)
    bet_r = A("betr", [1, DT], F32)
    wT = A("wT", [128, 8 * DT], BF16)
    wT3 = wT.rearrange("p (k o) -> p k o", k=8)
    wstg = A("wstg", [128, DT], F32)
    wstgb = A("wstgb", [128, DT], BF16)
    sb.update(locals())

    import ml_dtypes
    rev = np.zeros((64, 64), np.float32)
    for i in range(64):
        rev[i, 63 - i] = 1.0
    d_rev = nc.inline_tensor(rev.astype(ml_dtypes.bfloat16), pref + "d_rev")
    revm = A("revm", [64, 64], BF16)
    sb["revm"] = revm

    PS = [c.PS[j][:, :] for j in range(8)]
    sb["PS"] = PS

    @block.sync
    def _(sy):
        emit_common_load(c, sy, ld)
        seq_dma(sy, ld, eta_s[:, :], io["eta"])
        seq_dma(sy, ld, laml_s[:, :], io["laml"])
        seq_dma(sy, ld, gam_r[:, :], io["gam"])
        seq_dma(sy, ld, bet_r[:, :], io["bet"])
        seq_dma(sy, ld, revm[:, :], dram_ap(d_rev, (64, 64)))
    setup_ld = ld.n

    @block.scalar
    def _(sc):
        sc.wait_ge(ld.h, setup_ld)
        sa.inc(sc.activation(lam_s[0:1, 0:1], laml_s[0:1, 0:1], AF.Sigmoid))
        sc.wait_ge(sa.h, sa.n)
        sa.inc(sc.activation(lam_s[0:1, 3:4], lam_s[0:1, 0:1], AF.Ln))
    lam_done = sa.n

    @block.vector
    def _(v):
        v.wait_ge(sa.h, lam_done)
        sv.inc(v.tensor_scalar(lam_s[0:1, 1:2], lam_s[0:1, 0:1], -1.0, 1.0, ALU.mult, ALU.add))
        v.wait_ge(sv.h, sv.n)
        sv.inc(v.tensor_tensor(lam_s[0:1, 1:2], lam_s[0:1, 1:2], eta_s[0:1, 0:1], ALU.mult))
        v.wait_ge(sv.h, sv.n)
        sv.inc(v.tensor_scalar_mul(lam_s[0:1, 1:2], lam_s[0:1, 1:2], 1.0 / 64))
        sv.inc(v.memset(onesr[0:1, :], 1.0))
        v.wait_ge(sv.h, sv.n)
        sv.inc(v.tensor_scalar(lamrow[0:1, :], onesr[0:1, :], lam_s[0:1, 0:1], None, ALU.mult))
        v.wait_ge(sv.h, sv.n)
        sv.inc(v.tensor_tensor_scan(prow[0:1, 1:65], lamrow[0:1, :], onesr[0:1, :], 1.0,
                                    ALU.mult, ALU.mult))
        sv.inc(v.memset(prow[0:1, 0:1], 1.0))
        v.wait_ge(sv.h, sv.n)
        sv.inc(v.tensor_copy(lam_s[0:1, 2:3], prow[0:1, 64:65]))
    scal_done = sv.n

    @block.tensor
    def _(t):
        t.wait_ge(sv.h, scal_done)
        t.matmul(PS[0][0:64, 0:1], prow[0:1, 1:65], c.ones_f[0:1, 0:1], start=True, stop=True)
        t.matmul(PS[0][64:128, 0:1], prow[0:1, 1:65], c.ones_f[0:1, 0:1], start=True, stop=True,
                 tile_position=(0, 64))
        t.matmul(PS[0][:, 8:9], c.ones_f[0:1, 0:128], lam_s[0:1, 1:2], start=True, stop=True)
        t.matmul(PS[0][:, 9:10], c.ones_f[0:1, 0:128], lam_s[0:1, 2:3], start=True, stop=True)
        t.matmul(PS[0][:, 10:11], c.ones_f[0:1, 0:128], lam_s[0:1, 3:4], start=True, stop=True)
        st_.inc(t.matmul(PS[0][:, 11:12], c.ones_f[0:1, 0:128], lam_s[0:1, 0:1], start=True, stop=True))
    bc_mm = st_.n

    @block.vector
    def _(v):
        v.wait_ge(st_.h, bc_mm)
        sv.inc(v.tensor_copy(lampow[:, :], PS[0][:, 0:1]))
        sv.inc(v.tensor_copy(bc[:, 0:4], PS[0][:, 8:12]))
    bc_done = sv.n

    lampow_b = A("lampowb", [128, 1], BF16)
    sb["lampow_b"] = lampow_b

    @block.scalar
    def _(sc):
        sc.wait_ge(sv.h, bc_done)
        sa.inc(sc.copy(lampow_b[:, :], lampow[:, :]))
    lpb_done = sa.n

    @block.tensor
    def _(t):
        t.wait_ge(sa.h, lpb_done)
        t.matmul(PS[1][0:64, 0:1], revm[0:64, 0:64], lampow_b[0:64, 0:1], start=True, stop=True)
        st_.inc(t.matmul(PS[1][64:128, 0:1], revm[0:64, 0:64], lampow_b[0:64, 0:1],
                         start=True, stop=True, tile_position=(0, 64)))
    rev_mm = st_.n

    @block.vector
    def _(v):
        v.wait_ge(st_.h, rev_mm)
        sv.inc(v.reciprocal(bc[:, 4:5], bc[:, 3:4]))
        v.wait_ge(sv.h, sv.n)
        sv.inc(v.tensor_tensor(lamrev[:, :], PS[1][:, 0:1], bc[:, 4:5], ALU.mult))
    lamrev_done = sv.n

    @block.gpsimd
    def _(g):
        g.wait_ge(sv.h, lamrev_done)
        g.iota(E128[0:64, :], [[-1, 64]], base=0, channel_multiplier=1,
               allow_small_or_imprecise_dtypes=True)
        g.iota(E128[64:128, :], [[-1, 64]], base=0, channel_multiplier=1,
               allow_small_or_imprecise_dtypes=True)
        wcast.inc(g.memset(masks[:, :], 0.0))
    iota_done = wcast.n

    @block.vector
    def _(v):
        v.wait_ge(wcast.h, iota_done)
        sv.inc(v.tensor_scalar_max(masks[:, 128:192], E128[:, :], 0.0))
    emax_done = sv.n

    @block.scalar
    def _(sc):
        sc.wait_ge(sv.h, emax_done)
        sa.inc(sc.activation(masks[:, 128:192], masks[:, 128:192], AF.Exp, scale=bc[:, 2:3]))
    lexp_done = sa.n

    @block.vector
    def _(v):
        v.wait_ge(sa.h, lexp_done)
        sv.inc(v.tensor_single_scalar(masks[:, 0:64], E128[:, :], 0.0, ALU.is_gt))
        v.wait_ge(sv.h, sv.n)
        sv.inc(v.tensor_tensor(masks[:, 0:64], masks[:, 0:64], masks[:, 128:192], ALU.mult))
        v.wait_ge(sv.h, sv.n)
        sv.inc(v.tensor_scalar(masks[:, 0:64], masks[:, 0:64], bc[:, 0:1], -1.0,
                               ALU.mult, ALU.mult))
        sv.inc(v.tensor_single_scalar(masks[:, 192:256], E128[:, :], 0.0, ALU.is_ge))
        v.wait_ge(sv.h, sv.n)
        sv.inc(v.tensor_tensor(masks[:, 128:192], masks[:, 128:192], masks[:, 192:256], ALU.mult))
    l_done = sv.n

    @block.vector
    def _(v):
        v.wait_ge(sv.h, l_done)
        sv.inc(v.tensor_scalar_mul(E128[:, :], E128[:, :], -1.0))
        v.wait_ge(sv.h, sv.n)
        sv.inc(v.tensor_scalar_max(masks[:, 64:128], E128[:, :], 0.0))
    emax2_done = sv.n

    @block.scalar
    def _(sc):
        sc.wait_ge(sv.h, emax2_done)
        sa.inc(sc.activation(masks[:, 64:128], masks[:, 64:128], AF.Exp, scale=bc[:, 2:3]))
    lexp2_done = sa.n

    @block.vector
    def _(v):
        v.wait_ge(sa.h, lexp2_done)
        sv.inc(v.tensor_single_scalar(masks[:, 192:256], E128[:, :], 0.0, ALU.is_gt))
        v.wait_ge(sv.h, sv.n)
        sv.inc(v.tensor_tensor(masks[:, 192:256], masks[:, 192:256], masks[:, 64:128], ALU.mult))
        v.wait_ge(sv.h, sv.n)
        sv.inc(v.tensor_scalar(masks[:, 192:256], masks[:, 192:256], bc[:, 0:1], -1.0,
                               ALU.mult, ALU.mult))
        sv.inc(v.tensor_single_scalar(E128[:, :], E128[:, :], 0.0, ALU.is_ge))
        v.wait_ge(sv.h, sv.n)
        sv.inc(v.tensor_tensor(masks[:, 64:128], masks[:, 64:128], E128[:, :], ALU.mult))
    masks_done = sv.n

    @block.vector
    def _(v):
        v.wait_ge(sv.h, masks_done)
        sv.inc(v.tensor_scalar(idbeta[:, :], c.id128_f[:, :], bc[:, 0:1], None, ALU.mult))
    idb_done = sv.n

    @block.tensor
    def _(t):
        t.wait_ge(ld.h, setup_ld)
        t.wait_ge(sv.h, idb_done)
        t.matmul(PS[2][:, :], c.ones_f[0:1, 0:128], gam_r[0:1, 0:512], start=True, stop=True)
        t.matmul(PS[3][:, :], c.ones_f[0:1, 0:128], gam_r[0:1, 512:1024], start=True, stop=True)
        t.matmul(PS[4][:, :], c.ones_f[0:1, 0:128], bet_r[0:1, 0:512], start=True, stop=True)
        st_.inc(t.matmul(PS[5][:, :], c.ones_f[0:1, 0:128], bet_r[0:1, 512:1024], start=True, stop=True))
    gb_mm = st_.n

    @block.vector
    def _(v):
        v.wait_ge(st_.h, gb_mm)
        sv.inc(v.tensor_copy(gam_b[:, 0:512], PS[2][:, :]))
        sv.inc(v.tensor_copy(gam_b[:, 512:1024], PS[3][:, :]))
        sv.inc(v.tensor_copy(bet_b[:, 0:512], PS[4][:, :]))
        sv.inc(v.tensor_copy(bet_b[:, 512:1024], PS[5][:, :]))
    gb_done = sv.n

    tr_done = {}
    ev_done = {}
    for r in range(8):
        @block.sync
        def _(sy, r=r):
            if r >= 1:
                sy.wait_ge(wcast.h, iota_done + r)
            seq_dma(sy, wld, wstg[:, :], io["w_in"][r * 128:(r + 1) * 128, :])
        @block.gpsimd
        def _(g, r=r):
            g.wait_ge(wld.h, wld.n)
            if r >= 1:
                g.wait_ge(st_.h, tr_done[r - 1])
            wcast.inc(g.tensor_copy(wstgb[:, :], wstg[:, :]))
        @block.tensor
        def _(t, r=r):
            t.wait_ge(wcast.h, wcast.n)
            if r == 0:
                t.wait_ge(sv.h, gb_done)
            if r >= 2:
                t.wait_ge(sv.h, ev_done[r - 2])
            for kk in range(8):
                mm = t.transpose(PS[r % 2].bitcast(BF16)[:, kk * 128:(kk + 1) * 128],
                                 wstgb[:, kk * 128:(kk + 1) * 128], c.id128_bf[:, :])
            st_.inc(mm)
        tr_done[r] = st_.n
        @block.vector
        def _(v, r=r):
            v.wait_ge(st_.h, tr_done[r])
            sv.inc(v.tensor_copy(wT3[:, :, r * 128:(r + 1) * 128],
                                 PS[r % 2].bitcast(BF16).rearrange("p (k t) -> p k t", k=8)))
        ev_done[r] = sv.n
    sb["setup_done"] = sv.n
    return sb


def emit_phaseC(nc, block, c, io, sems, pref="pc_", n_tiles=NT, stage_limit=99, arena=None):
    so3 = [sems["s0"], sems["s1"], sems["s2"]]
    A = (lambda n, sh, dt: arena(pref + n, sh, dt)) if arena is not None \
        else (lambda n, sh, dt: nc.alloc_sbuf_tensor(pref + n, sh, dt))
    sb = {}
    emit_phaseC_setup(nc, block, c, io, sems, sb, pref, arena=arena)
    ld, sv, sa, st_, so = sems["ld"], sems["v"], sems["a"], sems["t"], sems["st"]
    PS = sb["PS"]
    masks, idbeta = sb["masks"], sb["idbeta"]
    lampow, lamrev, bc = sb["lampow"], sb["lamrev"], sb["bc"]
    gam_b, bet_b, wT3 = sb["gam_b"], sb["bet_b"], sb["wT3"]

    RD = 3
    xt_r = [A(f"xtr{j}", [128, DT], F32) for j in range(RD)]
    st_r = [A(f"str{j}", [128, DT], BF16) for j in range(RD)]
    kb_r = [A(f"kbr{j}", [128, DT], BF16) for j in range(RD)]
    pt_r = [A(f"ptr{j}", [128, DT], BF16) for j in range(RD)]
    gw_r = [A(f"gwr{j}", [128, DT], BF16) for j in range(RD)]
    rt_r = [A(f"rtr{j}", [128, DT], BF16) for j in range(RD)]
    wn_r = [A(f"wnr{j}", [128, DT], BF16) for j in range(RD)]

    xa_tm = A("xa_tm", [128, DT], BF16)
    xa_fm = A("xa_fm", [128, 8 * 128], BF16)
    xa_fm3 = xa_fm.rearrange("p (k t) -> p k t", k=8)
    LK = A("LK", [128, DT], BF16)
    NLK = A("NLK", [128, DT], BF16)
    Nsb = A("Nsb", [128, 512], BF16)
    NTsb = A("NTsb", [128, 512], BF16)
    Gsb = A("Gsb", [128, 512], BF16)
    GTsb = A("GTsb", [128, 512], BF16)
    Npw = [A(f"Npw{j}", [128, 512], BF16) for j in range(2)]
    NTpw = [A(f"NTpw{j}", [128, 512], BF16) for j in range(2)]
    X_sb = A("X_sb", [128, 8 * 128], BF16)
    S_f = A("S_f", [128, 1024], F32)
    S_sb = A("S_sb", [128, 1024], BF16)
    U_sb = A("U_sb", [128, 1024], BF16)
    lkt_tmp = A("lkt_tmp", [128, 512], BF16)
    yt = A("yt", [128, DT], F32)
    ot = [A(f"ot{j}", [128, DT], F32) for j in range(2)]
    lns = alloc_ln_scratch(A)

    mb3 = lambda col: masks[:, col * 64:(col + 1) * 64].rearrange(
        "p (o x) -> p o x", o=1).to_broadcast([128, 8, 64])
    scol = lambda h: (0 if h < 8 else 512) + ((h // 2) % 4) * 64

    @block.vector
    def _(v):
        v.wait_ge(sv.h, sb["setup_done"])
        for j in range(8):
            v.memset(PS[j], 0.0)
        sv.inc(v.memset(S_f[:, :], 0.0))
        v.wait_ge(sv.h, sv.n)
        sv.inc(v.tensor_copy(S_sb[:, :], S_f[:, :]))
    init_done = sv.n

    ring_ok = {}
    ring_sv = {}
    ring_wn = {}
    store_done = {}
    x_free = {}
    scan_done = {-1: 0}

    def emit_scan(it):
        r = it % RD

        for nb in range(2):
            cp = 64 * nb
            csl = slice(nb * 512, (nb + 1) * 512)
            @block.tensor
            def _(t, it=it, nb=nb, cp=cp):
                t.wait_ge(sa.h, ring_ok[it])
                t.wait_ge(sv.h, ring_sv[it])
                t.wait_ge(sv.h, init_done)
                t.wait_ge(sa.h, s_refresh[0])
                if it >= 1 or nb == 1:
                    t.wait_ge(sv.h, ydone[0])
                    t.wait_ge(sv.h, u_evac[0])
                    t.wait_ge(sv.h, sdelta_done[0])
                if "u" not in SUB9:
                    st_.inc(t.matmul(PS[6][0:64, 0:64], c.id128_bf[0:64, 0:64],
                                     S_sb[0:64, 0:64], start=True, stop=True))
                else:
                    for h in range(16):
                        p0 = (h % 2) * 64
                        Pu = PS[6] if h % 2 == 0 else PS[7]
                        mm = t.matmul(Pu[cp:cp + 64, (h // 2) * 64:(h // 2) * 64 + 64],
                                      pt_r[it % RD][p0:p0 + 64, nb * 512 + (h // 2) * 64: nb * 512 + (h // 2) * 64 + 64],
                                      S_sb[p0:p0 + 64, scol(h):scol(h) + 64],
                                      start=True, stop=True, tile_position=(p0, cp))
                    st_.inc(mm)
            u_mm = st_.n

            @block.vector
            def _(v, it=it, nb=nb, cp=cp, u_mm=u_mm):
                v.wait_ge(st_.h, u_mm)
                if "ue" in SUB9:
                    u3 = U_sb[cp:cp + 64, :].rearrange("p (s two x) -> p s two x", s=8, two=2)
                    wr3 = wn_r[it % RD][cp:cp + 64, :].rearrange("p (s two x) -> p s two x", s=8, two=2)
                    sv.inc(v.scalar_tensor_tensor(u3[:, :, 0, :],
                                           PS[6][cp:cp + 64, :].rearrange("p (s x) -> p s x", s=8),
                                           -1.0, wr3[:, :, 0, :], ALU.mult, ALU.add))
                    sv.inc(v.scalar_tensor_tensor(u3[:, :, 1, :],
                                                  PS[7][cp:cp + 64, :].rearrange("p (s x) -> p s x", s=8),
                                                  -1.0, wr3[:, :, 1, :], ALU.mult, ALU.add))
                else:
                    sv.inc(v.memset(U_sb[cp:cp + 64, :], 0.0))
            u_evac[0] = sv.n

            @block.tensor
            def _(t, it=it, nb=nb, cp=cp):
                t.wait_ge(sv.h, ring_wn[it])
                if "o" not in SUB9:
                    st_.inc(t.matmul(PS[2][0:64, 0:64], c.id128_bf[0:64, 0:64],
                                     S_sb[0:64, 0:64], start=True, stop=True))
                    return
                for h in range(16):
                    p0 = (h % 2) * 64
                    Po = PS[2] if h % 2 == 0 else PS[5]
                    mm = t.matmul(Po[cp:cp + 64, (h // 2) * 64:(h // 2) * 64 + 64],
                                  rt_r[it % RD][p0:p0 + 64, nb * 512 + (h // 2) * 64: nb * 512 + (h // 2) * 64 + 64],
                                  S_sb[p0:p0 + 64, scol(h):scol(h) + 64],
                                  start=True, stop=True, tile_position=(p0, cp))
                st_.inc(mm)
            o_mm = st_.n

            @block.tensor
            def _(t, it=it, nb=nb, cp=cp):
                t.wait_ge(sv.h, u_evac[0])
                if "d" not in SUB9:
                    st_.inc(t.matmul(PS[3][0:64, 0:64], c.id128_bf[0:64, 0:64],
                                     S_sb[0:64, 0:64], start=True, stop=True))
                    return
                for h in range(16):
                    p0 = (h % 2) * 64
                    Pd = PS[3] if h < 8 else PS[4]
                    mm = t.matmul(Pd[p0:p0 + 64, ((h // 2) % 4) * 64:((h // 2) % 4) * 64 + 64],
                                  kb_r[it % RD][cp:cp + 64, h * 64:h * 64 + 64],
                                  U_sb[cp:cp + 64, h * 64:h * 64 + 64],
                                  start=True, stop=True, tile_position=(cp, p0))
                st_.inc(mm)
            d_mm = st_.n

            @block.vector
            def _(v, d_mm=d_mm):
                v.wait_ge(st_.h, d_mm)
                if "supd" in SUB9:
                    sv.inc(v.scalar_tensor_tensor(S_f[:, 0:512], S_f[:, 0:512], bc[:, 1:2], PS[3][:, :],
                                           ALU.mult, ALU.add))
                    sv.inc(v.scalar_tensor_tensor(S_f[:, 512:1024], S_f[:, 512:1024], bc[:, 1:2], PS[4][:, :],
                                                  ALU.mult, ALU.add))
                else:
                    sv.inc(v.memset(S_f[0:1, 0:1], 0.0))
            sdelta_done[0] = sv.n

            @block.scalar
            def _(sc):
                sc.wait_ge(sv.h, sdelta_done[0])
                sa.inc(sc.copy(S_sb[:, :], S_f[:, :]))
            s_refresh[0] = sa.n

            @block.vector
            def _(v, it=it, nb=nb, cp=cp, o_mm=o_mm):
                v.wait_ge(st_.h, o_mm)
                if "y" in SUB9:
                    y3 = yt[cp:cp + 64, :].rearrange("p (s two x) -> p s two x", s=8, two=2)
                    s3 = st_r[it % RD][cp:cp + 64, :].rearrange("p (s two x) -> p s two x", s=8, two=2)
                    sv.inc(v.tensor_tensor(y3[:, :, 0, :], PS[2][cp:cp + 64, :].rearrange("p (s x) -> p s x", s=8),
                                    s3[:, :, 0, :], ALU.add))
                    sv.inc(v.tensor_tensor(y3[:, :, 1, :], PS[5][cp:cp + 64, :].rearrange("p (s x) -> p s x", s=8),
                                           s3[:, :, 1, :], ALU.add))
                    v.wait_ge(sv.h, sv.n)
                    sv.inc(v.tensor_tensor(yt[cp:cp + 64, :], yt[cp:cp + 64, :],
                                           gw_r[it % RD][cp:cp + 64, :], ALU.add))
                else:
                    sv.inc(v.memset(yt[cp:cp + 64, :], 0.5))
            ydone[0] = sv.n

        yt_done = sv.n
        if "ln" in SUB9:
            ln1 = emit_ln_tile(nc, block, c, yt[:, :], lns, sv, sa, yt_done,
                               gamma_b=gam_b[:, :], beta_b=bet_b[:, :])

            @block.vector
            def _(v, it=it, ln1=ln1):
                v.wait_ge(sv.h, ln1)
                if it >= 2:
                    v.wait_ge(so3[(it - 2) % 3].h, store_done[it - 2])
                sv.inc(v.tensor_tensor(ot[it % 2][:, :], yt[:, :], xt_r[it % RD][:, :], ALU.add))
            x_free[it] = sv.n
            ot_done = sv.n

            ln2 = emit_ln_tile(nc, block, c, ot[it % 2][:, :], lns, sv, sa, ot_done)
        else:
            @block.vector
            def _(v, it=it, yt_done=yt_done):
                v.wait_ge(sv.h, yt_done)
                if it >= 2:
                    v.wait_ge(so3[(it - 2) % 3].h, store_done[it - 2])
                sv.inc(v.tensor_copy(ot[it % 2][:, :], yt[:, :]))
            x_free[it] = sv.n
            ln2 = sv.n

        @block.sync
        def _(sy, it=it, ln2=ln2):
            sy.wait_ge(sv.h, ln2)
            seq_dma(sy, so3[it % 3], io["y_out"][it * 128:(it + 1) * 128, :], ot[it % 2][:, :])
        store_done[it] = so3[it % 3].n
        scan_done[it] = sv.n

    u_evac = {0: 0}
    sdelta_done = {0: 0}
    ydone = {0: 0}
    maskprod_done = {0: 0}
    x0_evac = {0: 0}
    solve_done = {0: 0}
    gw_ev = {0: 0}
    wr_ev = {0: 0}
    rt_ev = {0: 0}
    s_refresh = {0: init_done and 0}
    s_refresh[0] = sems["a"].n

    for i in range(n_tiles):
        if i >= 1 and stage_limit >= 9:
            emit_scan(i - 1)

        r = i % RD
        @block.sync
        def _(sy, i=i):
            if i >= RD:
                sy.wait_ge(sv.h, x_free.get(i - RD, 0))
                sy.wait_ge(so3[(i - RD) % 3].h, store_done.get(i - RD, 0))
            seq_dma(sy, ld, xt_r[i % RD][:, :], io["x_in"][i * 128:(i + 1) * 128, :])
        x_ld = ld.n

        @block.scalar
        def _(sc, i=i, x_ld=x_ld):
            sc.wait_ge(ld.h, x_ld)
            if i >= 1:
                sc.wait_ge(sv.h, lk_done[0])
            sa.inc(sc.activation(xa_tm[:, :], xt_r[i % RD][:, :], AF.Tanh))
        tanh_done = sa.n

        @block.tensor
        def _(t, i=i, tanh_done=tanh_done):
            t.wait_ge(sa.h, tanh_done)
            if i == 0:
                t.wait_ge(sv.h, init_done)
            else:
                t.wait_ge(sv.h, xafm_evac[0])
            for kk in range(8):
                mm = t.transpose(PS[0].bitcast(BF16)[:, kk * 128:(kk + 1) * 128],
                                 xa_tm[:, kk * 128:(kk + 1) * 128], c.id128_bf[:, :])
            st_.inc(mm)
        xaT_mm = st_.n

        @block.vector
        def _(v, i=i, xaT_mm=xaT_mm):
            v.wait_ge(st_.h, xaT_mm)
            sv.inc(v.tensor_copy(xa_fm3[:, :, :],
                                 PS[0].bitcast(BF16).rearrange("p (k t) -> p k t", k=8)))
        if i == 0:
            xafm_evac = {0: 0}
            lk_done = {0: 0}
            st_evac = {0: 0}
        xafm_evac[0] = sv.n
        xafm_done = sv.n

        if stage_limit < 2:
            continue
        @block.tensor
        def _(t, i=i, xafm_done=xafm_done):
            t.wait_ge(sv.h, xafm_done)
            if i >= 1:
                t.wait_ge(sv.h, ydone[0])
                t.wait_ge(sv.h, st_evac[0])
            for haf in range(2):
                for kk in range(8):
                    mm = t.matmul(PS[1 + haf][:, :], xa_fm3[:, kk, :],
                                  wT3[:, kk, haf * 512:(haf + 1) * 512],
                                  start=(kk == 0), stop=(kk == 7))
            st_.inc(mm)
        stat_mm = st_.n

        @block.vector
        def _(v, i=i, stat_mm=stat_mm):
            v.wait_ge(st_.h, stat_mm)
            sv.inc(v.tensor_copy(st_r[i % RD][:, 0:512], PS[1][:, :]))
            sv.inc(v.tensor_copy(st_r[i % RD][:, 512:1024], PS[2][:, :]))
        st_evac[0] = sv.n
        static_done = sv.n

        if stage_limit < 3:
            continue
        @block.vector
        def _(v, i=i, static_done=static_done, tanh_done=tanh_done):
            v.wait_ge(sa.h, tanh_done)
            sv.inc(v.tensor_scalar(LK[:, :], xa_tm[:, :], lampow[:, 0:1], None, ALU.mult))
            v.wait_ge(sv.h, sv.n)
            sv.inc(v.tensor_scalar(NLK[:, :], LK[:, :], -1.0, None, ALU.mult))
            sv.inc(v.tensor_scalar(kb_r[i % RD][:, :], xa_tm[:, :], lamrev[:, 0:1], None, ALU.mult))
        lk_done[0] = sv.n

        for nb in range(2):
            if stage_limit < 4 or (nb == 1 and stage_limit < 8):
                continue
            bp = 64 * nb
            csl_base = nb * 512
            @block.tensor
            def _(t, i=i, nb=nb, bp=bp, sa_bar=None):
                t.wait_ge(sv.h, lk_done[0])
                t.wait_ge(sv.h, sdelta_done[0])
                t.wait_ge(sv.h, rt_ev[0])
                t.wait_ge(sa.h, sa.n)
                if i >= 1 or nb == 1:
                    t.wait_ge(sv.h, maskprod_done[0])
                for h in range(16):
                    p0 = (h % 2) * 64
                    mm = t.matmul(PS[3][p0:p0 + 64, (h // 2) * 64:(h // 2) * 64 + 64],
                                  xa_fm3[(h % 2) * 64:(h % 2) * 64 + 64, h // 2, bp:bp + 64],
                                  xa_fm3[(h % 2) * 64:(h % 2) * 64 + 64, h // 2, bp:bp + 64],
                                  start=True, stop=True, tile_position=(p0, p0))
                st_.inc(mm)
            a_mm = st_.n

            @block.vector
            def _(v, i=i, nb=nb, a_mm=a_mm):
                v.wait_ge(st_.h, a_mm)
                PA3 = PS[3].rearrange("p (s x) -> p s x", s=8)
                sv.inc(v.tensor_tensor(Nsb.rearrange("p (s x) -> p s x", s=8), PA3, mb3(0), ALU.mult))
                sv.inc(v.tensor_tensor(NTsb.rearrange("p (s x) -> p s x", s=8), PA3, mb3(3), ALU.mult))
                sv.inc(v.tensor_tensor(Gsb.rearrange("p (s x) -> p s x", s=8), PA3, mb3(2), ALU.mult))
                sv.inc(v.tensor_tensor(GTsb.rearrange("p (s x) -> p s x", s=8), PA3, mb3(1), ALU.mult))
            maskprod_done[0] = sv.n

            if stage_limit < 5:
                continue
            @block.tensor
            def _(t, i=i, nb=nb, bp=bp, static_done=static_done):
                t.wait_ge(sv.h, static_done)
                t.wait_ge(sv.h, lk_done[0])
                if i >= 1 or nb == 1:
                    t.wait_ge(sv.h, x0_evac[0])
                    t.wait_ge(sv.h, wr_ev[0])
                    t.wait_ge(sv.h, ydone[0])
                for h in range(16):
                    p0 = (h % 2) * 64
                    PX = PS[4] if h < 8 else PS[5]
                    sl = (h % 8) // 2 * 128 + (h % 2) * 0
                    sl = ((h // 2) % 4) * 128
                    t.matmul(PX[p0:p0 + 64, sl:sl + 64],
                             idbeta[bp:bp + 64, bp:bp + 64],
                             st_r[i % RD][bp:bp + 64, h * 64:h * 64 + 64],
                             start=True, stop=True, tile_position=(bp, p0))
                    mm = t.matmul(PX[p0:p0 + 64, sl + 64:sl + 128],
                                  idbeta[bp:bp + 64, bp:bp + 64],
                                  LK[bp:bp + 64, h * 64:h * 64 + 64],
                                  start=True, stop=True, tile_position=(bp, p0))
                st_.inc(mm)
            x0_mm = st_.n

            @block.vector
            def _(v, x0_mm=x0_mm):
                v.wait_ge(st_.h, x0_mm)
                sv.inc(v.tensor_copy(X_sb[:, 0:512], PS[4][:, :]))
                sv.inc(v.tensor_copy(X_sb[:, 512:1024], PS[5][:, :]))
            x0_evac[0] = sv.n

            if stage_limit < 6:
                continue
            for lvl in range(6):
                cur = lvl % 2
                nxt = 1 - cur
                @block.tensor
                def _(t, i=i, nb=nb, lvl=lvl, cur=cur):
                    t.wait_ge(sv.h, x0_evac[0])
                    if lvl == 0:
                        pass
                    else:
                        t.wait_ge(sa.h, sq_evac[0])
                    for h in range(16):
                        p0 = (h % 2) * 64
                        PX = PS[4] if h < 8 else PS[5]
                        sl = ((h // 2) % 4) * 128
                        NTs = NTsb if lvl == 0 else NTpw[cur]
                        t.matmul(PX[p0:p0 + 64, sl:sl + 128],
                                 c.id128_bf[p0:p0 + 64, p0:p0 + 64],
                                 X_sb[p0:p0 + 64, sl:sl + 128],
                                 start=True, stop=False, tile_position=(p0, p0))
                        mm = t.matmul(PX[p0:p0 + 64, sl:sl + 128],
                                      NTs[p0:p0 + 64, (h // 2) * 64:(h // 2) * 64 + 64],
                                      X_sb[p0:p0 + 64, sl:sl + 128],
                                      start=False, stop=True, tile_position=(p0, p0))
                    st_.inc(mm)
                ap_mm = st_.n

                if lvl < 5:
                    @block.tensor
                    def _(t, i=i, nb=nb, lvl=lvl, cur=cur, nxt=nxt, sa_now=None):
                        if lvl >= 1:
                            t.wait_ge(sa.h, sq_evac[0])
                        else:
                            t.wait_ge(sv.h, maskprod_done[0])
                            t.wait_ge(sa.h, sa.n)
                        for h in range(16):
                            p0 = (h % 2) * 64
                            Ns = Nsb if lvl == 0 else Npw[cur]
                            NTs = NTsb if lvl == 0 else NTpw[cur]
                            sl = (h // 2) * 64
                            t.matmul(PS[6][p0:p0 + 64, sl:sl + 64],
                                     NTs[p0:p0 + 64, sl:sl + 64], Ns[p0:p0 + 64, sl:sl + 64],
                                     start=True, stop=True, tile_position=(p0, p0))
                            mm = t.matmul(PS[7][p0:p0 + 64, sl:sl + 64],
                                          Ns[p0:p0 + 64, sl:sl + 64], NTs[p0:p0 + 64, sl:sl + 64],
                                          start=True, stop=True, tile_position=(p0, p0))
                        st_.inc(mm)
                    sq_mm = st_.n

                    @block.scalar
                    def _(sc, lvl=lvl, nxt=nxt, sq_mm=sq_mm):
                        sc.wait_ge(st_.h, sq_mm)
                        sa.inc(sc.copy(Npw[nxt][:, :], PS[6][:, :]))
                        sa.inc(sc.copy(NTpw[nxt][:, :], PS[7][:, :]))
                    if i == 0 and nb == 0 and lvl == 0:
                        sq_evac = {0: 0}
                    sq_evac[0] = sa.n

                @block.vector
                def _(v, ap_mm=ap_mm):
                    v.wait_ge(st_.h, ap_mm)
                    v.tensor_copy(X_sb[:, 0:512], PS[4][:, :])
                    sv.inc(v.tensor_copy(X_sb[:, 512:1024], PS[5][:, :]))
                x0_evac[0] = sv.n
            x_done = x0_evac[0]

            if stage_limit < 7:
                continue
            @block.tensor
            def _(t, i=i, nb=nb, bp=bp, x_done=x_done):
                t.wait_ge(sv.h, x_done)
                t.wait_ge(sa.h, sq_evac[0])
                mm = None
                if "pt" in SUB7:
                    for h in range(16):
                        p0 = (h % 2) * 64
                        base = 0 if h < 8 else 512
                        xsl = base + ((h // 2) % 4) * 128
                        mm = t.transpose(PS[6].bitcast(BF16)[p0:p0 + 64, (h // 2) * 64:(h // 2) * 64 + 64],
                                         X_sb[p0:p0 + 64, xsl + 64:xsl + 128],
                                         c.id128_bf[p0:p0 + 64, p0:p0 + 64], tile_position=(p0, p0))
                    st_.inc(mm)
                if "gw" in SUB7:
                    for h in range(16):
                        p0 = (h % 2) * 64
                        base = 0 if h < 8 else 512
                        xsl = base + ((h // 2) % 4) * 128
                        Pg = PS[7] if h % 2 == 0 else PS[3]
                        mm = t.matmul(Pg[bp:bp + 64, (h // 2) * 64:(h // 2) * 64 + 64],
                                      GTsb[p0:p0 + 64, (h // 2) * 64:(h // 2) * 64 + 64],
                                      X_sb[p0:p0 + 64, xsl:xsl + 64],
                                      start=True, stop=True, tile_position=(p0, bp))
                    st_.inc(mm)
                if mm is None:
                    st_.inc(t.matmul(PS[7][0:64, 0:64], GTsb[0:64, 0:64], X_sb[0:64, 0:64],
                                     start=True, stop=True))
            ptgw_mm = st_.n

            @block.vector
            def _(v, i=i, nb=nb, bp=bp, ptgw_mm=ptgw_mm):
                v.wait_ge(st_.h, ptgw_mm)
                sv.inc(v.tensor_copy(pt_r[i % RD][:, nb * 512:(nb + 1) * 512],
                                     PS[6].bitcast(BF16)[:, 0:512]))
                if "gevac" in SUB7:
                    v.wait_ge(sv.h, sv.n)
                    gwr3 = gw_r[i % RD][bp:bp + 64, :].rearrange("p (s two x) -> p s two x", s=8, two=2)
                    sv.inc(v.tensor_copy(gwr3[:, :, 0, :], PS[7][bp:bp + 64, :].rearrange("p (s x) -> p s x", s=8)))
                    sv.inc(v.tensor_copy(gwr3[:, :, 1, :], PS[3][bp:bp + 64, :].rearrange("p (s x) -> p s x", s=8)))
            gw_ev[0] = sv.n

            @block.tensor
            def _(t, i=i, nb=nb, bp=bp, x_done=x_done, gw_evac=None):
                t.wait_ge(sv.h, maskprod_done[0])
                t.wait_ge(sv.h, gw_ev[0])
                if "rt" not in SUB7:
                    st_.inc(t.matmul(PS[3][0:64, 0:64], GTsb[0:64, 0:64], X_sb[0:64, 0:64],
                                     start=True, stop=True))
                else:
                    for h in range(16):
                        p0 = (h % 2) * 64
                        base = 0 if h < 8 else 512
                        xsl = base + ((h // 2) % 4) * 128
                        mm = t.matmul(PS[3][p0:p0 + 64, (h // 2) * 64:(h // 2) * 64 + 64],
                                      X_sb[p0:p0 + 64, xsl + 64:xsl + 128],
                                      Gsb[p0:p0 + 64, (h // 2) * 64:(h // 2) * 64 + 64],
                                      start=True, stop=True, tile_position=(p0, p0))
                    st_.inc(mm)
                    for h in range(16):
                        p0 = (h % 2) * 64
                        mm = t.transpose(PS[6].bitcast(BF16)[p0:p0 + 64, 512 + (h // 2) * 64:512 + (h // 2) * 64 + 64],
                                         LK[bp:bp + 64, h * 64:h * 64 + 64],
                                         c.id128_bf[bp:bp + 64, bp:bp + 64], tile_position=(bp, p0))
                    st_.inc(mm)
            rt_mm = st_.n

            @block.vector
            def _(v, i=i, nb=nb, rt_mm=rt_mm):
                v.wait_ge(st_.h, rt_mm)
                if "rtc" in SUB7:
                    sv.inc(v.tensor_copy(lkt_tmp[:, :], PS[6].bitcast(BF16)[:, 512:1024]))
                    v.wait_ge(sv.h, sv.n)
                    sv.inc(v.scalar_tensor_tensor(rt_r[i % RD][:, nb * 512:(nb + 1) * 512],
                                                  PS[3][:, :], -1.0, lkt_tmp[:, :],
                                                  ALU.mult, ALU.add))
                else:
                    sv.inc(v.memset(rt_r[i % RD][:, nb * 512:(nb + 1) * 512], 0.0))
            rtw_v = sv.n
            rt_ev[0] = rtw_v

            @block.tensor
            def _(t, i=i, nb=nb, bp=bp, x_done=x_done):
                if "wp" not in SUB7:
                    st_.inc(t.matmul(PS[4][0:64, 0:64], c.id128_bf[0:64, 0:64],
                                     X_sb[0:64, 0:64], start=True, stop=True))
                    return
                for h in range(16):
                    p0 = (h % 2) * 64
                    base = 0 if h < 8 else 512
                    xsl = base + ((h // 2) % 4) * 128
                    Pw = PS[4] if h % 2 == 0 else PS[5]
                    mm = t.matmul(Pw[bp:bp + 64, (h // 2) * 64:(h // 2) * 64 + 64],
                                  c.id128_bf[p0:p0 + 64, p0:p0 + 64],
                                  X_sb[p0:p0 + 64, xsl:xsl + 64],
                                  start=True, stop=True, tile_position=(p0, bp))
                st_.inc(mm)
            w_mm = st_.n

            @block.vector
            def _(v, i=i, nb=nb, bp=bp, w_mm=w_mm):
                v.wait_ge(st_.h, w_mm)
                if "wp" in SUB7:
                    wr3 = wn_r[i % RD][bp:bp + 64, :].rearrange("p (s two x) -> p s two x", s=8, two=2)
                    sv.inc(v.tensor_copy(wr3[:, :, 0, :], PS[4][bp:bp + 64, :].rearrange("p (s x) -> p s x", s=8)))
                    sv.inc(v.tensor_copy(wr3[:, :, 1, :], PS[5][bp:bp + 64, :].rearrange("p (s x) -> p s x", s=8)))
                else:
                    sv.inc(v.memset(wn_r[i % RD][bp:bp + 64, :], 0.0))
            wr_ev[0] = sv.n


        ring_ok[i] = sems["a"].n
        ring_sv[i] = rt_ev[0]
        ring_wn[i] = wr_ev[0]

    if n_tiles >= 1 and stage_limit >= 9:
        emit_scan(n_tiles - 1)

    @block.sync
    def _(sy):
        for s3 in so3:
            sy.wait_ge(s3.h, s3.n)


def emit_phase_barrier(nc, block, pbar, k, so_list):
    target = 5 * (k + 1)
    for engname in ("tensor", "vector", "scalar", "gpsimd", "sync"):
        @getattr(block, engname)
        def _(e):
            for so in so_list:
                e.wait_ge(so.h, so.n)
            e.sem_inc(pbar.h, 1)
            e.wait_ge(pbar.h, target)
    pbar.n = target


def build_fused(debug_outs=False):
    import contextlib
    nc = bass.Bass("TRN2", target_bir_lowering=False)
    x_in = nc.dram_tensor("x_in", [4096, 1024], F32, kind="ExternalInput").ap()
    wqkv = nc.dram_tensor("wqkv", [3072, 1024], F32, kind="ExternalInput").ap()
    wout = nc.dram_tensor("wout", [1024, 1024], F32, kind="ExternalInput").ap()
    cent = nc.dram_tensor("cent", [64, 1024], F32, kind="ExternalInput").ap()
    eb = nc.dram_tensor("eb", [1, 64], F32, kind="ExternalInput").ap()
    wsu = nc.dram_tensor("wsu", [1024, 512], F32, kind="ExternalInput")
    wsd = nc.dram_tensor("wsd", [512, 1024], F32, kind="ExternalInput")
    wu_all = nc.dram_tensor("wu_all", [64, 1024, 512], F32, kind="ExternalInput")
    wd_all = nc.dram_tensor("wd_all", [64, 512, 1024], F32, kind="ExternalInput")
    w1 = nc.dram_tensor("w1", [1024, 1024], F32, kind="ExternalInput").ap()
    gam = nc.dram_tensor("gam", [1, 1024], F32, kind="ExternalInput").ap()
    bet = nc.dram_tensor("bet", [1, 1024], F32, kind="ExternalInput").ap()
    eta = nc.dram_tensor("eta", [1, 1], F32, kind="ExternalInput").ap()
    laml = nc.dram_tensor("laml", [1, 1], F32, kind="ExternalInput").ap()
    w2 = nc.dram_tensor("w2", [1024, 1024], F32, kind="ExternalInput").ap()
    b2 = nc.dram_tensor("b2", [1, 1024], F32, kind="ExternalInput").ap()
    y_out = nc.dram_tensor("y_out", [4096, 1024], F32, kind="ExternalOutput").ap()
    if debug_outs:
        xs1 = nc.dram_tensor("xs1", [4096, 1024], F32, kind="ExternalOutput").ap()
        xs2 = nc.dram_tensor("xs2", [4096, 1024], F32, kind="ExternalOutput").ap()
        xs3 = nc.dram_tensor("xs3", [4096, 1024], F32, kind="ExternalOutput").ap()
    else:
        xs1 = nc.dram_tensor("xs1", [4096, 1024], F32).ap()
        xs2 = nc.dram_tensor("xs2", [4096, 1024], F32).ap()
        xs3 = nc.dram_tensor("xs3", [4096, 1024], F32).ap()

    c = setup_common(nc)
    ARENA = 188 * 1024
    r = nc.bump_sbuf(ARENA)
    base = r[0] if isinstance(r, tuple) else r
    arena = Arena(nc, base, base + ARENA)

    def wu_e(e):
        return bass.AP(wu_all, e * 1024 * 512, [[512, 128], [128 * 512, 8], [1, 512]])

    def wd_e(e):
        return bass.AP(wd_all, e * 512 * 1024, [[1024, 128], [128 * 1024, 4], [1, 1024]])

    with contextlib.ExitStack() as es:
        pbar = Sem(es.enter_context(nc.semaphore("pbar")))
        sA = make_sems(nc, es, "pa_")
        sB = make_sems(nc, es, "pb_")
        sC = make_sems(nc, es, "pc_")
        sD = make_sems(nc, es, "pd_")
        with nc.Block() as block:
            arena.reset()
            with nc.named_scope("phA"):
                emit_phaseA(nc, block, c, {"x_in": x_in, "wqkv": wqkv, "wout": wout,
                                           "y_out": xs1}, sA, arena=arena)
                emit_phase_barrier(nc, block, pbar, 0, [sA["s0"], sA["s1"]])
            arena.reset()
            with nc.named_scope("phB"):
                emit_phaseB(nc, block, c, {
                    "x_in": xs1, "cent": cent, "eb": eb,
                    "wsu": bass.AP(wsu, 0, [[512, 128], [128 * 512, 8], [1, 512]]),
                    "wsd": bass.AP(wsd, 0, [[1024, 128], [128 * 1024, 4], [1, 1024]]),
                    "wu_e": wu_e, "wd_e": wd_e, "y_out": xs2}, sB, arena=arena)
                emit_phase_barrier(nc, block, pbar, 1, [sB["st"]])
            arena.reset()
            with nc.named_scope("phC"):
                emit_phaseC(nc, block, c, {"x_in": xs2, "w_in": w1, "gam": gam,
                                           "bet": bet, "eta": eta, "laml": laml,
                                           "y_out": xs3}, sC, arena=arena)
                emit_phase_barrier(nc, block, pbar, 2, [sC["s0"], sC["s1"], sC["s2"]])
            arena.reset()
            with nc.named_scope("phD"):
                emit_phaseD(nc, block, c, {"x_in": xs3, "w_in": w2, "b_in": b2,
                                           "y_out": y_out}, sD, arena=arena)
    return nc


from concourse.bass_utils import run_bass_kernel_spmd

N_CORES = 8


def kernel(L0_attn_x, L0_attn_W_qkv, L0_attn_W_out, L0_ffn_centroids,
           L0_ffn_expert_biases, L0_ffn_Ws_up, L0_ffn_Ws_down, L0_ffn_W_up,
           L0_ffn_W_down, L1_attn_W, L1_attn_ln_gamma, L1_attn_ln_beta,
           L1_attn_eta, L1_attn_lam_logit, L1_ffn_weight, L1_ffn_bias):
    cast = lambda a: np.ascontiguousarray(np.asarray(a, dtype=np.float32))
    x = cast(L0_attn_x)
    nc = build_fused()
    res = run_bass_kernel_spmd(nc, [{
        "x_in": x[b],
        "wqkv": cast(L0_attn_W_qkv), "wout": cast(L0_attn_W_out),
        "cent": cast(L0_ffn_centroids),
        "eb": cast(L0_ffn_expert_biases).reshape(1, 64),
        "wsu": cast(L0_ffn_Ws_up), "wsd": cast(L0_ffn_Ws_down),
        "wu_all": cast(L0_ffn_W_up), "wd_all": cast(L0_ffn_W_down),
        "w1": cast(L1_attn_W),
        "gam": cast(L1_attn_ln_gamma).reshape(1, 1024),
        "bet": cast(L1_attn_ln_beta).reshape(1, 1024),
        "eta": cast(L1_attn_eta).reshape(1, 1),
        "laml": cast(L1_attn_lam_logit).reshape(1, 1),
        "w2": cast(L1_ffn_weight), "b2": cast(L1_ffn_bias).reshape(1, 1024)}
        for b in range(N_CORES)], core_ids=list(range(N_CORES)))
    return np.stack([np.asarray(res.results[b]["y_out"])
                     for b in range(N_CORES)], axis=0)
